# version 23 (final)
# baseline (speedup 1.0000x reference)
import contextlib
import numpy as np
import ml_dtypes
import concourse.bass as bass
import concourse.mybir as mybir
from concourse.bass_utils import run_bass_kernel_spmd
from concourse.ap import AP as _AP


def _ap(x):
    return x if isinstance(x, _AP) else x[:]

F32 = mybir.dt.float32
BF16 = mybir.dt.bfloat16
AF = mybir.ActivationFunctionType
ALU = mybir.AluOpType
AX = mybir.AxisListType

D = 1024
NH = 16
KVH = 4
HD = 64
LAT = 256
NEG = -1.0e6
BF = ml_dtypes.bfloat16
QORDER = [0, 4, 1, 5, 2, 6, 3, 7, 8, 12, 9, 13, 10, 14, 11, 15]
QSLOT = {h: s for s, h in enumerate(QORDER)}
_DBG = None
_DBG_TILE = 2
_DBG_FULL = False


def _rope_tables(S, base):
    exps = np.arange(0, HD, 2).astype(np.float32) / np.float32(HD)
    invf = (np.float32(1.0) / np.power(np.float32(base), exps)).astype(np.float32)
    t = np.arange(S, dtype=np.float32)
    freqs = (t[:, None] * invf[None, :]).astype(np.float32)
    cos = np.cos(freqs.astype(np.float64)).astype(np.float32)
    sin = np.sin(freqs.astype(np.float64)).astype(np.float32)
    nt = S // 128
    return (np.ascontiguousarray(cos.reshape(nt, 128, 32).transpose(1, 0, 2)),
            np.ascontiguousarray(sin.reshape(nt, 128, 32).transpose(1, 0, 2)))


def _host_consts(S):
    p = np.arange(128)
    maskd = np.where(p[:, None] <= p[None, :], 0.0, NEG).astype(np.float32)
    maskl = np.where(p[:, None] > p[None, :], 0.0, NEG).astype(np.float32)
    cg, sg = _rope_tables(S, 1000000.0)
    cl, sl = _rope_tables(S, 10000.0)
    eye = np.eye(128, dtype=BF)
    return {
        "identB": eye,
        "onesB": np.ones((1, 128), dtype=BF),
        "maskdT": np.ascontiguousarray(maskd.T).astype(BF),
        "masklT": np.ascontiguousarray(maskl.T).astype(BF),
        "cosg": cg.astype(BF), "sing": sg.astype(BF),
        "cosl": cl.astype(BF), "sinl": sl.astype(BF),
    }


class P:

    def __init__(self):
        self.sem = {}
        self.cnt = {}
        self.waited = {}
        self.guard = {}

    def add_sem(self, name, handle):
        self.sem[name] = handle
        self.cnt[name] = 0

    def inc(self, name, inst, n=1):
        inst.then_inc(self.sem[name], n)
        self.cnt[name] += n
        return self.cnt[name]

    def wait(self, eng, ename, sname, val=None):
        val = self.cnt[sname] if val is None else val
        if val <= 0:
            return
        k = (ename, sname)
        if self.waited.get(k, 0) < val:
            eng.wait_ge(self.sem[sname], val)
            self.waited[k] = val

    def wguard(self, eng, ename, rname):
        g = self.guard.get(rname)
        if g is not None:
            self.wait(eng, ename, g[0], g[1])

    def setguard(self, rname, sname, val=None):
        v = self.cnt[sname] if val is None else val
        old = self.guard.get(rname)
        if old is not None and old[0] == sname and old[1] >= v:
            return
        self.guard[rname] = (sname, v)


def build_program(S):
    NT = S // 128
    NB = S // 512
    nc = bass.Bass("TRN2", target_bir_lowering=False)

    def din(name, shape, dt=F32):
        return nc.dram_tensor(name, list(shape), dt, kind="ExternalInput")

    x_d = din("x", [S, D])
    gwq_d = din("g_wq", [D, D]); gkvd_d = din("g_kv_down", [D, LAT])
    gkup_d = din("g_k_up", [LAT, LAT]); gvup_d = din("g_v_up", [LAT, LAT])
    gwo_d = din("g_wo", [D, D])
    lwq_d = din("l_wq", [D, D]); lwk_d = din("l_wk", [D, LAT])
    lwv_d = din("l_wv", [D, LAT]); lwo_d = din("l_wo", [D, D])
    w1_d = din("w1", [D, 4 * D]); w2_d = din("w2", [4 * D, D])
    b1_d = din("b1r", [1, 4 * D]); b2_d = din("b2r", [1, D])
    n1_d = din("n1pc", [128, 8]); n2_d = din("n2pc", [128, 8])
    idb_d = din("identB", [128, 128], BF16)
    ones_d = din("onesB", [1, 128], BF16)
    maskd_d = din("maskdT", [128, 128], BF16)
    maskl_d = din("masklT", [128, 128], BF16)
    cosg_d = din("cosg", [128, NT, 32], BF16); sing_d = din("sing", [128, NT, 32], BF16)
    cosl_d = din("cosl", [128, NT, 32], BF16); sinl_d = din("sinl", [128, NT, 32], BF16)

    y1_d = nc.dram_tensor("y1scratch", [S, D], F32, kind="Internal")
    out_d = nc.dram_tensor("out", [S, D], F32, kind="ExternalOutput")
    if _DBG_FULL:
        dbgag_d = nc.dram_tensor("dbg_ag", [S, D], BF16, kind="ExternalOutput")
        dbgal_d = nc.dram_tensor("dbg_al", [S, D], BF16, kind="ExternalOutput")
        dbgy1_d = nc.dram_tensor("dbg_y1f", [S, D], F32, kind="ExternalOutput")
        dbgx3_d = nc.dram_tensor("dbg_x3f", [S, D], F32, kind="ExternalOutput")
        dbgy1b_d = nc.dram_tensor("dbg_y1b", [S, D], BF16, kind="ExternalOutput")

    p = P()
    st = contextlib.ExitStack()
    sb = lambda name, shape, dt: st.enter_context(nc.sbuf_tensor(name, list(shape), dt))
    ps = lambda name, shape, dt: st.enter_context(nc.psum_tensor(name, list(shape), dt))
    for s in ("ldx", "ldw", "gst", "st", "te", "ve", "se",
              "lx0", "lx1", "lx2", "lx3", "lx4", "lx5",
              "ly0", "ly1", "ly2", "st0", "st1",
              "lw0", "lw1", "lw2", "lw3", "lw4", "lw5", "lw6", "lw7"):
        p.add_sem(s, st.enter_context(nc.semaphore(s)))

    T, V, Sc, Sy, G = nc.tensor, nc.vector, nc.scalar, nc.sync, nc.gpsimd

    sIdent = sb("sIdent", [128, 128], BF16)
    sOnes = sb("sOnes", [1, 128], BF16)
    sMaskD = sb("sMaskD", [128, 128], BF16)
    sMaskL = sb("sMaskL", [128, 128], BF16)
    sCosG = sb("sCosG", [128, NT, 32], BF16)
    sSinG = sb("sSinG", [128, NT, 32], BF16)
    sCosL = sb("sCosL", [128, NT, 32], BF16)
    sSinL = sb("sSinL", [128, NT, 32], BF16)
    sN1 = sb("sN1", [128, 8], F32)
    sN2 = sb("sN2", [128, 8], F32)
    sEps = sb("sEps", [128, 4], F32)
    sStats = sb("sStats", [128, 128], F32)
    sDen = sb("sDen", [128, 16], F32)
    sBn = sb("sBn", [128, 16], F32)
    sMv = sb("sMv", [128, 64], F32)

    pT = ps("pT", [128, 1024], BF16)
    pSd = ps("pSd", [128, 1024], F32)
    pS0 = pSd[:, 0:512]
    pS1 = pSd[:, 512:1024]
    pP = ps("pP", [128, 1024], F32)
    pV0 = ps("pV0", [128, 512], F32)
    pV1 = ps("pV1", [128, 512], F32)
    pX = ps("pX", [128, 512], F32)
    pSb = [pS0, pS1]
    pVb = [pV0, pV1]

    EPS_QK_COL, EPS_LN_COL = 0, 1
    V.memset(sEps[:, 0:1], 1e-6)
    V.memset(sEps[:, 1:2], 1e-5)

    for dst, src in ((sIdent, idb_d), (sOnes, ones_d),
                     (sMaskD, maskd_d), (sMaskL, maskl_d),
                     (sCosG, cosg_d), (sSinG, sing_d),
                     (sCosL, cosl_d), (sSinL, sinl_d), (sN1, n1_d), (sN2, n2_d)):
        p.inc("ldx", Sy.dma_start(out=dst[:], in_=src[:]), 16)
    consts_ldx = p.cnt["ldx"]

    rstd1w = sStats[:, 0:16]
    rstd1 = sStats[:, 0:1]
    rstd2w = sStats[:, 16:32]
    rstd2 = sStats[:, 16:17]
    saccw = sStats[:, 32:48]
    sacc = sStats[:, 32:33]
    stmpw = sStats[:, 48:64]
    ssq16 = sStats[:, 64:80]
    rq16 = sStats[:, 80:96]
    ssk4 = sStats[:, 96:112]
    rk4 = sStats[:, 112:128]

    def rmsnorm_stats(src_ap, dst_rstd_w, sq_buf):
        src_ap = _ap(src_ap)
        p.wguard(Sc, "Sc", "sSq")
        i = Sc.activation(out=sq_buf[:, 0:D], in_=src_ap, func=AF.Square,
                          accum_out=sacc)
        c = p.inc("se", i)
        p.wait(Sc, "Sc", "se", c)
        i = Sc.copy(out=saccw[:, :], in_=sacc.broadcast_to([128, 16]))
        i = Sc.activation(out=stmpw, in_=saccw, func=AF.Ln,
                          bias=sEps[:, EPS_QK_COL:EPS_QK_COL + 1], scale=1.0 / D)
        i = Sc.activation(out=dst_rstd_w, in_=stmpw, func=AF.Exp, scale=-0.5)
        p.inc("se", i)
        p.setguard("sSq", "se")

    def transposes(src, chunks, dst3, dst_guard):
        src = _ap(src)
        p.wguard(T, "T", "pT")
        for c in range(chunks):
            i = T.transpose(pT[:, c * 128:(c + 1) * 128],
                            src[:, c * 128:(c + 1) * 128], sIdent[:])
        tc = p.inc("te", i)
        p.wait(V, "V", "te", tc)
        p.wguard(V, "V", dst_guard)
        i = V.tensor_copy(out=dst3,
                          in_=pT[:, 0:chunks * 128].rearrange("p (c n) -> p c n", n=128))
        c2 = p.inc("ve", i)
        p.setguard("pT", "ve", c2)
        return c2

    def rope(srcf, dstf, nh, t, costab, sintab, sq_buf):
        srcf, dstf = _ap(srcf), _ap(dstf)
        src4 = srcf.rearrange("p (h m w) -> p h m w", m=32, w=2)
        dst4 = dstf.rearrange("p (h m w) -> p h m w", m=32, w=2)
        e, o = src4[:, :, :, 0], src4[:, :, :, 1]
        eo, oo = dst4[:, :, :, 0], dst4[:, :, :, 1]
        cosb = costab[:, t, :].unsqueeze(1).broadcast_to([128, nh, 32])
        sinb = sintab[:, t, :].unsqueeze(1).broadcast_to([128, nh, 32])
        n = nh * 32
        tA = sq_buf[:, 0:n].rearrange("p (h m) -> p h m", m=32)
        tB = sq_buf[:, n:2 * n].rearrange("p (h m) -> p h m", m=32)
        V.tensor_tensor(out=tA, in0=e, in1=cosb, op=ALU.mult)
        V.tensor_tensor(out=tB, in0=o, in1=sinb, op=ALU.mult)
        V.tensor_tensor(out=eo, in0=tA, in1=tB, op=ALU.subtract)
        V.tensor_tensor(out=tA, in0=e, in1=sinb, op=ALU.mult)
        V.tensor_tensor(out=tB, in0=o, in1=cosb, op=ALU.mult)
        i = V.tensor_tensor(out=oo, in0=tA, in1=tB, op=ALU.add)
        return p.inc("ve", i)

    def qknorm(srcf, dstb, nh, ssq, rr, sq_buf):
        srcf, dstb = _ap(srcf), _ap(dstb)
        n = nh * HD
        i = V.tensor_tensor(out=sq_buf[:, 0:n], in0=srcf, in1=srcf, op=ALU.mult)
        p.inc("ve", i)
        i = V.reduce_sum(out=ssq[:, 0:nh],
                         in_=sq_buf[:, 0:n].rearrange("p (h d) -> p h d", d=HD),
                         axis=AX.X)
        c = p.inc("ve", i)
        p.wait(V, "V", "ve", c)
        p.wait(Sc, "Sc", "ve", c)
        i = Sc.activation(out=ssq[:, 0:16], in_=ssq[:, 0:16], func=AF.Ln,
                          bias=sEps[:, EPS_QK_COL:EPS_QK_COL + 1], scale=1.0 / HD)
        i = Sc.activation(out=rr[:, 0:16], in_=ssq[:, 0:16], func=AF.Exp, scale=-0.5)
        c = p.inc("se", i)
        p.wait(V, "V", "se", c)
        rb = rr[:, 0:nh].unsqueeze(2).broadcast_to([128, nh, HD])
        i = V.tensor_tensor(out=dstb.rearrange("p (h d) -> p h d", d=HD),
                            in0=srcf.rearrange("p (h d) -> p h d", d=HD),
                            in1=rb, op=ALU.mult)
        return p.inc("ve", i)

    def ln_apply(srcf, dstf):
        srcf, dstf = _ap(srcf), _ap(dstf)
        i = V.bn_stats(out=sBn[:, 0:6], in_=srcf[:, 0:512])
        i = V.bn_stats(out=sBn[:, 6:12], in_=srcf[:, 512:1024])
        c = p.inc("ve", i)
        p.wait(V, "V", "ve", c)
        i = V.bn_aggr(out=sMv[:, 0:2], in_=sBn[:, 0:12])
        c = p.inc("ve", i)
        p.wait(V, "V", "ve", c)
        i = V.tensor_copy(out=sMv[:, 16:32], in_=sMv[:, 1:2].broadcast_to([128, 16]))
        c = p.inc("ve", i)
        p.wait(V, "V", "ve", c)
        p.wait(Sc, "Sc", "ve", c)
        i = Sc.activation(out=sMv[:, 32:48], in_=sMv[:, 16:32], func=AF.Ln,
                          bias=sEps[:, EPS_LN_COL:EPS_LN_COL + 1], scale=1.0)
        i = Sc.activation(out=sMv[:, 48:64], in_=sMv[:, 32:48], func=AF.Exp,
                          scale=-0.5)
        c = p.inc("se", i)
        p.wait(V, "V", "se", c)
        i = V.scalar_tensor_tensor(out=dstf, in0=srcf, scalar=sMv[:, 0:1],
                                   in1=sMv[:, 48:49].broadcast_to([128, D]),
                                   op0=ALU.subtract, op1=ALU.mult)
        return p.inc("ve", i)

    def attn_evac(pv, pvp, out3):
        p.wait(V, "V", "te", p.cnt["te"])
        i = V.tensor_copy(out=sDen[:, 0:4],
                          in_=pv[:, 0:264].rearrange("p (s k) -> p s k", k=66)[:, :, 64:65])
        c = p.inc("ve", i)
        p.wait(V, "V", "ve", c)
        i = V.reciprocal(out=sDen[:, 0:4], in_=sDen[:, 0:4])
        c = p.inc("ve", i)
        p.wait(V, "V", "ve", c)
        i = V.tensor_tensor(
            out=out3,
            in0=pv[:, 0:264].rearrange("p (s k) -> p s k", k=66)[:, :, 0:64],
            in1=sDen[:, 0:4].unsqueeze(2).broadcast_to([128, 4, 64]),
            op=ALU.mult)
        c = p.inc("ve", i)
        p.setguard(f"pV{pvp}", "ve", c)

    def dump(name, ap, t=None):
        if _DBG is None or (t is not None and t != _DBG_TILE):
            return
        ap = _ap(ap)
        d = nc.dram_tensor("dbg_" + name, list(ap.shape), ap.dtype,
                           kind="ExternalOutput")
        for sn in ("te", "ve", "se", "ldx", "ldw", "gst"):
            p.wait(Sy, "Sy", sn)
        p.inc("st", Sy.dma_start(out=d[:], in_=ap), 16)
        for eng, en in ((T, "T"), (V, "V"), (Sc, "Sc")):
            p.wait(eng, en, "st", p.cnt["st"])
        _DBG.append("dbg_" + name)

    stAB = contextlib.ExitStack()
    sbA = lambda name, shape, dt: stAB.enter_context(nc.sbuf_tensor(name, list(shape), dt))

    wWqG = sbA("wWqG", [128, 8, D], BF16)
    wKVd = sbA("wKVd", [128, 8, LAT], BF16)
    wKup = sbA("wKup", [128, 2, LAT], BF16)
    wVup = sbA("wVup", [128, 2, LAT], BF16)
    wWoG = sbA("wWoG", [128, 8, D], BF16)
    wWqL = sbA("wWqL", [128, 8, D], BF16)
    wWkL = sbA("wWkL", [128, 8, LAT], BF16)
    wWvL = sbA("wWvL", [128, 8, LAT], BF16)
    wWoL = sbA("wWoL", [128, 8, D], BF16)

    with nc.sbuf_tensor("wstageA", [128, 8, D], F32) as wsA, \
         nc.sbuf_tensor("wstageB", [128, 8, D], F32) as wsB:
        for dst, src in ((wKup, gkup_d), (wVup, gvup_d), (wWoG, gwo_d), (wWoL, lwo_d)):
            p.inc("ldw", G.dma_start(out=dst[:], in_=src.rearrange("(c p) n -> p c n", p=128)), 16)
        folds = ((gwq_d, wWqG, sN1, 8, D), (gkvd_d, wKVd, sN1, 8, LAT),
                 (lwq_d, wWqL, sN2, 8, D), (lwk_d, wWkL, sN2, 8, LAT),
                 (lwv_d, wWvL, sN2, 8, LAT))
        ws = [wsA, wsB]
        for fi, (src, dst, nw, nch, ncol) in enumerate(folds):
            buf = ws[fi % 2]
            p.wguard(G, "G", f"wstage{fi % 2}")
            lv = p.inc("ldw", G.dma_start(out=buf[:, 0:nch, 0:ncol],
                                          in_=src.rearrange("(c p) n -> p c n", p=128)), 16)
            p.wait(V, "V", "ldw", lv)
            p.wait(V, "V", "ldx", consts_ldx)
            for c in range(nch):
                i = V.tensor_scalar(out=dst[:, c, 0:ncol], in0=buf[:, c, 0:ncol],
                                    scalar1=nw[:, c:c + 1], scalar2=None, op0=ALU.mult)
            p.inc("ve", i)
            p.setguard(f"wstage{fi % 2}", "ve")
    weights_ldw = p.cnt["ldw"]
    setup_ve = p.cnt["ve"]

    sKTg0 = sbA("sKTg0", [128, S], BF16)
    sKTg1 = sbA("sKTg1", [128, S], BF16)
    sVg = sbA("sVg", [128, NT, 260], BF16)
    sXblk = sbA("sXblk", [128, 4, D], F32)
    sX1 = sbA("sX1", [128, 2, D], F32)
    sXb = sbA("sXb", [128, D], BF16)
    sXT = sbA("sXT", [128, 8, 128], BF16)
    sX1T = sbA("sX1T", [128, 8, 128], BF16)
    sLat = sbA("sLat", [128, LAT], BF16)
    sLatT = sbA("sLatT", [128, 2, 128], BF16)
    sKtm = sbA("sKtm", [128, LAT], F32)
    sKbf = sbA("sKbf", [128, LAT], BF16)
    sQtm = sbA("sQtm", [128, D], F32)
    sQrope = sbA("sQrope", [128, D], F32)
    sQb = sbA("sQb", [128, D], BF16)
    sSq = sbA("sSq", [128, D], F32)
    sQTg = sbA("sQTg", [128, 2, 8, 512], BF16)
    sQTl = sbA("sQTl", [128, 8, 128], BF16)
    sKTl = sbA("sKTl", [128, 3, 256], BF16)
    sVl = sbA("sVl", [128, 3, 264], BF16)
    sPT = sbA("sPT", [128, 3, 1024], BF16)
    sAttn = sbA("sAttn", [128, 4, D], BF16)
    sAttnL = sbA("sAttnL", [128, 2, D], BF16)
    sATT = sbA("sATT", [128, 8, 128], BF16)

    for eng, en in ((V, "V"), (Sc, "Sc"), (T, "T"), (Sy, "Sy"), (G, "G")):
        p.wait(eng, en, "ve", setup_ve)
        p.wait(eng, en, "ldw", weights_ldw)
        p.wait(eng, en, "ldx", p.cnt["ldx"])
    p.inc("ve", V.memset(sVg[:], 1.0))
    p.inc("ve", V.memset(sVl[:], 1.0))

    xload_val = {}
    xissue = [0]
    scnt = [0]

    def load_x(t, slot):
        p.wguard(Sy, "Sy", f"xblkA{slot}")
        p.wguard(Sy, "Sy", f"xblkB{slot}")
        sem = f"lx{xissue[0] % 6}"
        xissue[0] += 1
        v = p.inc(sem, Sy.dma_start(out=sXblk[:, slot, :],
                                    in_=x_d[t * 128:(t + 1) * 128, :]), 16)
        xload_val[t] = (sem, v, slot)

    qready_ve = {}

    def part1_stages(t):
        b = t // 4
        ti = t % 4
        qp = b % 2
        st_ = {}

        def s0():
            sem, lv, slot = xload_val[t]
            xa = sXblk[:, slot, :]
            p.wait(V, "V", sem, lv)
            p.wait(Sc, "Sc", sem, lv)
            rmsnorm_stats(xa, rstd1w, sSq)
            p.setguard(f"xblkB{slot}", "se")
            p.wguard(V, "V", "sXb")
            i = V.tensor_copy(out=sXb[:], in_=xa)
            st_["xbc"] = p.inc("ve", i)
            p.setguard(f"xblkA{slot}", "ve", st_["xbc"])
            if ti < 2 and t + 2 < NT:
                load_x(t + 2, 2 + ((t + 2) % 2))

        def s1():
            p.wait(T, "T", "ve", st_["xbc"])
            transposes(sXb, 8, sXT[:, :, :], "sXT")
            p.setguard("sXb", "te")
            p.wait(T, "T", "ve", p.cnt["ve"])
            p.wguard(T, "T", "pX")
            for c in range(8):
                i = T.matmul(pX[:, 0:512], sXT[:, c, :], wWqG[:, c, 0:512],
                             start=(c == 0), stop=(c == 7))
            q0 = p.inc("te", i)
            p.wait(V, "V", "te", q0)
            p.wait(V, "V", "se", p.cnt["se"])
            p.wguard(V, "V", "sQtm")
            i = V.tensor_scalar(out=sQtm[:, 0:512], in0=pX[:, 0:512], scalar1=rstd1,
                                scalar2=None, op0=ALU.mult)
            p.setguard("pX", "ve", p.inc("ve", i))

        def s2():
            p.wguard(T, "T", "pX")
            for c in range(8):
                i = T.matmul(pX[:, 0:512], sXT[:, c, :], wWqG[:, c, 512:1024],
                             start=(c == 0), stop=(c == 7))
            q1 = p.inc("te", i)
            p.wait(V, "V", "te", q1)
            i = V.tensor_scalar(out=sQtm[:, 512:1024], in0=pX[:, 0:512], scalar1=rstd1,
                                scalar2=None, op0=ALU.mult)
            p.setguard("pX", "ve", p.inc("ve", i))

        def s3():
            p.wguard(T, "T", "pX")
            for c in range(8):
                i = T.matmul(pX[:, 0:LAT], sXT[:, c, :], wKVd[:, c, :],
                             start=(c == 0), stop=(c == 7))
            lmm = p.inc("te", i)
            p.setguard("sXT", "te", lmm)
            p.wait(V, "V", "te", lmm)
            p.wguard(V, "V", "sLat")
            i = V.tensor_scalar(out=sLat[:], in0=pX[:, 0:LAT], scalar1=rstd1,
                                scalar2=None, op0=ALU.mult)
            p.setguard("pX", "ve", p.inc("ve", i))
            p.wguard(V, "V", "sSq")
            p.wguard(V, "V", "sQrope")
            rope(sQtm, sQrope, NH, t, sCosG, sSinG, sSq)
            p.setguard("sQtm", "ve")

        def s4():
            p.wait(T, "T", "ve", p.cnt["ve"])
            transposes(sLat, 2, sLatT[:, :, :], "sLatT")
            p.setguard("sLat", "te")
            p.wait(T, "T", "ve", p.cnt["ve"])
            p.wguard(T, "T", "pX")
            for c in range(2):
                i = T.matmul(pX[:, 0:LAT], sLatT[:, c, :], wKup[:, c, :],
                             start=(c == 0), stop=(c == 1))
            kmm = p.inc("te", i)
            p.wait(Sc, "Sc", "te", kmm)
            p.wguard(Sc, "Sc", "sKtm")
            st_["kevc"] = p.inc("se", Sc.copy(out=sKtm[:], in_=pX[:, 0:LAT]))
            p.setguard("pX", "se", st_["kevc"])
            p.wguard(V, "V", "sQb")
            st_["qbc"] = qknorm(sQrope, sQb, NH, ssq16, rq16, sSq)

        def s5():
            p.wguard(T, "T", "pX")
            for c in range(2):
                i = T.matmul(pX[:, 0:LAT], sLatT[:, c, :], wVup[:, c, :],
                             start=(c == 0), stop=(c == 1))
            vmm = p.inc("te", i)
            p.setguard("sLatT", "te", vmm)
            p.wait(V, "V", "te", vmm)
            i = V.tensor_copy(
                out=sVg[:, t, 0:260].rearrange("p (h k) -> p h k", k=65)[:, :, 0:64],
                in_=pX[:, 0:LAT].rearrange("p (h k) -> p h k", k=64))
            p.setguard("pX", "ve", p.inc("ve", i))
            p.wait(V, "V", "se", st_["kevc"])
            p.wguard(V, "V", "sKbf")
            rope(sKtm, sQrope[:, 0:LAT], KVH, t, sCosG, sSinG, sSq)
            p.setguard("sKtm", "ve")
            st_["kbfc"] = qknorm(sQrope[:, 0:LAT], sKbf, KVH, ssk4, rk4, sSq)
            p.setguard("sQrope", "ve")
            p.setguard("sSq", "ve")

        def s6():
            p.wait(T, "T", "ve", st_["qbc"])
            p.wguard(T, "T", "pT")
            for c in range(8):
                i = T.transpose(pT[:, c * 128:(c + 1) * 128],
                                sQb[:, c * 128:(c + 1) * 128], sIdent[:])
            tc = p.inc("te", i)
            p.setguard("sQb", "te", tc)
            p.wait(V, "V", "te", tc)
            p.wguard(V, "V", f"sQTg{qp}")
            i = V.tensor_copy(out=sQTg[:, qp, :, ti * 128:(ti + 1) * 128],
                              in_=pT.rearrange("p (c n) -> p c n", n=128))
            p.setguard("pT", "ve", p.inc("ve", i))
            p.wait(T, "T", "ve", st_["kbfc"])
            p.wguard(T, "T", "pT")
            i = T.transpose(pT[:, 0:128], sKbf[:, 0:128], sIdent[:])
            i = T.transpose(pT[:, 128:256], sKbf[:, 128:256], sIdent[:])
            tc = p.inc("te", i)
            p.setguard("sKbf", "te", tc)
            p.wait(V, "V", "te", tc)
            V.tensor_copy(out=sKTg0[:, t * 128:(t + 1) * 128], in_=pT[:, 0:128])
            i = V.tensor_copy(out=sKTg1[:, t * 128:(t + 1) * 128], in_=pT[:, 128:256])
            p.setguard("pT", "ve", p.inc("ve", i))
            if ti == 3:
                qready_ve[b] = p.cnt["ve"]

        return [s0, s1, s2, s3, s4, s5, s6]

    load_x(0, 2)
    load_x(1, 3)
    for t in range(min(4, NT)):
        for f in part1_stages(t):
            f()

    pending_v4 = [None]
    for b in range(NB):
        qp = b % 2
        nj = 4 * b + 4
        par = 0
        p.wait(T, "T", "ve", qready_ve[b])
        stage_q = []
        if pending_v4[0] is not None:
            stage_q.append(pending_v4[0])
            pending_v4[0] = None
        if b + 1 < NB:
            load_x(4 * (b + 1), 2)
            load_x(4 * (b + 1) + 1, 3)
            for t_ in range(4 * (b + 1), 4 * (b + 1) + 4):
                stage_q.extend(part1_stages(t_))

        def drain_stage():
            if stage_q:
                stage_q.pop(0)()

        pBk = [pSd, pP]
        for hp, h0 in enumerate((0, 1, 2, 3, 8, 9, 10, 11)):
            h1 = h0 + 4
            kv0_, kv1_ = h0 // 4, h1 // 4
            kt = sKTg0 if h0 < 8 else sKTg1
            p.wguard(T, "T", "pV0")
            p.wguard(T, "T", "pV1")

            def emit_pv_g2(pend, final=False):
                jc_, ioff_, sp_, ec_ = pend
                p.wait(T, "T", "se", ec_)
                if final:
                    for _ in range(3):
                        T.ldweights(sIdent[:])
                for hx, pv, kv_ in ((0, pV0, kv0_), (1, pV1, kv1_)):
                    for sub in range(ioff_, 4):
                        i = T.matmul(pv[:, sub * 66:sub * 66 + 65],
                                     sPT[:, sp_, hx * 512 + (sub - ioff_) * 128:
                                         hx * 512 + (sub - ioff_ + 1) * 128],
                                     sVg[:, jc_, kv_ * 65:kv_ * 65 + 65],
                                     start=(jc_ == 0 and sub == 0),
                                     stop=(jc_ == 4 * b + sub))
                pvc = p.inc("te", i)
                p.setguard(f"sPT{sp_}", "te", pvc)

            pend_q = []
            q0, q1 = QSLOT[h0], QSLOT[h1]
            for jc in range(nj):
                ioff = max(0, jc - 4 * b)
                N = 512 - ioff * 128
                bk = par % 2
                sp = scnt[0] % 3
                scnt[0] += 1
                if bk == 0:
                    p.wguard(T, "T", "pS0")
                    p.wguard(T, "T", "pS1")
                else:
                    p.wguard(T, "T", "pP")
                bank = pBk[bk]
                masked = jc >= 4 * b
                T.matmul(bank[:, 0:N],
                         kt[0:64, jc * 128:(jc + 1) * 128],
                         sQTg[0:64, qp, q0 // 2, ioff * 128:512],
                         start=True, stop=not masked, tile_position=(0, 0))
                i = T.matmul(bank[:, 512:512 + N],
                             kt[64:128, jc * 128:(jc + 1) * 128],
                             sQTg[64:128, qp, q1 // 2, ioff * 128:512],
                             start=True, stop=not masked, tile_position=(64, 0))
                if masked:
                    T.matmul(bank[:, 0:128], sMaskD[:], sIdent[:],
                             start=False, stop=False)
                    i = T.matmul(bank[:, 512:640], sMaskD[:], sIdent[:],
                                 start=False, stop=True)
                smm = p.inc("te", i)
                p.wait(Sc, "Sc", "te", smm)
                p.wguard(Sc, "Sc", f"sPT{sp}")
                i = Sc.activation(out=sPT[:, sp, 0:512 + N], in_=bank[:, 0:512 + N],
                                  func=AF.Exp, scale=0.125)
                ec = p.inc("se", i)
                if bk == 0:
                    p.setguard("pS0", "se", ec)
                    p.setguard("pS1", "se", ec)
                else:
                    p.setguard("pP", "se", ec)
                pend_q.append((jc, ioff, sp, ec))
                if len(pend_q) > 2:
                    emit_pv_g2(pend_q.pop(0))
                par += 1
                if nj >= 8 and jc in (nj // 4, nj // 2, 3 * nj // 4):
                    drain_stage()
            while pend_q:
                emit_pv_g2(pend_q.pop(0), final=(len(pend_q) == 0))
            p.wguard(V, "V", "sAttn")
            p.wguard(V, "V", "sAttn_dma")
            attn_evac(pV0, 0, sAttn[:, :, h0 * 64:(h0 + 1) * 64])
            attn_evac(pV1, 1, sAttn[:, :, h1 * 64:(h1 + 1) * 64])
            if hp < 7:
                drain_stage()
        while stage_q:
            stage_q.pop(0)()
        p.setguard(f"sQTg{qp}", "te")
        attn_ve = p.cnt["ve"]

        if b == 0:
            load_x(0, 0)
            load_x(1, 1)
        vs = {}

        def v1(t):
            ti_ = t % 4
            st_ = vs.setdefault(t, {})
            if _DBG_FULL:
                p.wait(Sy, "Sy", "ve")
                p.inc("st", Sy.dma_start(out=dbgag_d[t * 128:(t + 1) * 128, :],
                                         in_=sAttn[:, ti_, :]), 16)
                p.setguard("sAttn_dma", "st")
            p.wait(T, "T", "ve", attn_ve)
            p.wguard(T, "T", "pT")
            for c in range(8):
                i = T.transpose(pT[:, c * 128:(c + 1) * 128],
                                sAttn[:, ti_, c * 128:(c + 1) * 128], sIdent[:])
            tc = p.inc("te", i)
            p.setguard("sAttn", "te", tc)
            p.wait(Sc, "Sc", "te", tc)
            p.wguard(Sc, "Sc", "sATT")
            i = Sc.copy(out=sATT[:, :, :],
                        in_=pT.rearrange("p (c n) -> p c n", n=128))
            attc = p.inc("se", i)
            p.setguard("pT", "se", attc)
            p.wait(T, "T", "se", attc)
            p.wguard(T, "T", "pP")
            for nb_ in range(2):
                for c in range(8):
                    i = T.matmul(pP[:, nb_ * 512:(nb_ + 1) * 512], sATT[:, c, :],
                                 wWoG[:, c, nb_ * 512:(nb_ + 1) * 512],
                                 start=(c == 0), stop=(c == 7))
            woc = p.inc("te", i)
            p.setguard("sATT", "te", woc)
            p.wait(V, "V", "te", woc)
            sem, lv, slot = xload_val[t]
            st_["slot"] = slot
            p.wait(V, "V", sem, lv)
            p.wguard(V, "V", f"sX1_{t % 2}")
            x1a = sX1[:, t % 2, :]
            i = V.tensor_add(out=x1a, in0=pP[:, :], in1=sXblk[:, slot, :])
            x1c = p.inc("ve", i)
            p.setguard("pP", "ve", x1c)
            p.wait(Sc, "Sc", "ve", x1c)
            rmsnorm_stats(x1a, rstd2w, sSq)
            p.wguard(V, "V", "sXb")
            p.wait(V, "V", "ve", x1c)
            i = V.tensor_copy(out=sXb[:], in_=x1a)
            xbc = p.inc("ve", i)
            p.wait(T, "T", "ve", xbc)
            transposes(sXb, 8, sX1T[:, :, :], "sX1T")
            p.setguard("sXb", "te")
            p.wait(T, "T", "ve", p.cnt["ve"])
            p.wguard(T, "T", "pP")
            for nb_ in range(2):
                for c in range(8):
                    i = T.matmul(pP[:, nb_ * 512:(nb_ + 1) * 512], sX1T[:, c, :],
                                 wWqL[:, c, nb_ * 512:(nb_ + 1) * 512],
                                 start=(c == 0), stop=(c == 7))
            st_["qmm"] = p.inc("te", i)
            p.wguard(T, "T", "pX")
            for c in range(8):
                i = T.matmul(pX[:, 0:LAT], sX1T[:, c, :], wWkL[:, c, :],
                             start=(c == 0), stop=(c == 7))
            kmm = p.inc("te", i)
            p.wait(Sc, "Sc", "te", kmm)
            p.wguard(Sc, "Sc", "sKtm")
            i = Sc.mul(out=sKtm[:], in_=pX[:, 0:LAT], mul=rstd2)
            kevc = p.inc("se", i)
            st_["kevc"] = kevc
            p.setguard("pX", "se", kevc)

        def v2a(t):
            st_ = vs[t]
            p.wguard(T, "T", "pX")
            for c in range(8):
                i = T.matmul(pX[:, 0:LAT], sX1T[:, c, :], wWvL[:, c, :],
                             start=(c == 0), stop=(c == 7))
            vmm = p.inc("te", i)
            p.setguard("sX1T", "te", vmm)
            p.wait(Sc, "Sc", "te", vmm)
            p.wguard(Sc, "Sc", f"sVl{t % 3}")
            i = Sc.mul(
                out=sVl[:, t % 3, 0:264].rearrange("p (h k) -> p h k", k=66)[:, :, 0:64],
                in_=pX[:, 0:LAT].rearrange("p (h k) -> p h k", k=64),
                mul=rstd2)
            vlc = p.inc("se", i)
            st_["vlc"] = vlc
            p.setguard("pX", "se", vlc)
            p.wait(Sc, "Sc", "te", st_["qmm"])
            p.wguard(Sc, "Sc", "sQtm")
            i = Sc.mul(out=sQtm[:], in_=pP[:, :], mul=rstd2)
            qtmc = p.inc("se", i)
            p.setguard("pP", "se", qtmc)
            p.wait(V, "V", "se", qtmc)
            p.wguard(V, "V", "sSq")
            p.wguard(V, "V", "sQrope")
            rope(sQtm, sQrope, NH, t, sCosL, sSinL, sSq)
            p.setguard("sQtm", "ve")

        def v2bV(t):
            st_ = vs[t]
            p.wguard(V, "V", "sQb")
            st_["qbc"] = qknorm(sQrope, sQb, NH, ssq16, rq16, sSq)
            p.wguard(V, "V", "sKbf")
            p.wait(V, "V", "se", st_["kevc"])
            rope(sKtm, sQrope[:, 0:LAT], KVH, t, sCosL, sSinL, sSq)
            p.setguard("sKtm", "ve")
            st_["kbfc"] = qknorm(sQrope[:, 0:LAT], sKbf, KVH, ssk4, rk4, sSq)
            p.setguard("sQrope", "ve")
            p.setguard("sSq", "ve")

        def v2bT(t):
            st_ = vs[t]
            qbc = st_["qbc"]
            kbfc = st_["kbfc"]
            p.wait(T, "T", "ve", qbc)
            p.wguard(T, "T", "pT")
            for c in range(8):
                i = T.transpose(pT[:, c * 128:(c + 1) * 128],
                                sQb[:, c * 128:(c + 1) * 128], sIdent[:])
            tc = p.inc("te", i)
            p.setguard("sQb", "te", tc)
            p.wait(Sc, "Sc", "te", tc)
            p.wguard(Sc, "Sc", "sQTl")
            i = Sc.copy(out=sQTl[:, :, :],
                        in_=pT.rearrange("p (c n) -> p c n", n=128))
            st_["qtl"] = p.inc("se", i)
            p.setguard("pT", "se", st_["qtl"])
            p.wait(T, "T", "ve", kbfc)
            p.wguard(T, "T", "pT")
            i = T.transpose(pT[:, 0:128], sKbf[:, 0:128], sIdent[:])
            i = T.transpose(pT[:, 128:256], sKbf[:, 128:256], sIdent[:])
            tc = p.inc("te", i)
            p.setguard("sKbf", "te", tc)
            p.wait(Sc, "Sc", "te", tc)
            p.wguard(Sc, "Sc", f"sKTl{t % 3}")
            i = Sc.copy(out=sKTl[:, t % 3, :], in_=pT[:, 0:256])
            st_["klc"] = p.inc("se", i)
            p.setguard("pT", "se", st_["klc"])

        def v3(t, grange):
            st_ = vs[t]
            slots = [s_ for s_ in (t - 2, t - 1, t) if s_ >= 0]
            p.wait(T, "T", "se", max(st_["klc"], st_["qtl"], st_["vlc"]))
            for g in grange:
                kv = g
                pvp = g % 2
                pv = pVb[pvp]
                lpar = g % 2
                p.wguard(T, "T", f"pV{pvp}")

                def emit_pv_l(pend, final=False):
                    si_, s_, sp_, ec_ = pend
                    p.wait(T, "T", "se", ec_)
                    if final:
                        for _ in range(3):
                            T.ldweights(sIdent[:])
                    for hh in range(4):
                        i = T.matmul(pv[:, hh * 66:hh * 66 + 65],
                                     sPT[:, sp_, hh * 128:(hh + 1) * 128],
                                     sVl[:, s_ % 3, kv * 66:kv * 66 + 65],
                                     start=(si_ == 0 and hh == 0),
                                     stop=(si_ == len(slots) - 1))
                    pvc = p.inc("te", i)
                    p.setguard(f"sPT{sp_}", "te", pvc)

                pend_q = []
                for si, s_ in enumerate(slots):
                    sp = scnt[0] % 3
                    scnt[0] += 1
                    p.wguard(T, "T", f"pS{lpar}")
                    mask = sMaskD if s_ == t else (sMaskL if s_ == t - 2 else None)
                    for hh in range(4):
                        h_ = 4 * g + hh
                        qs = QSLOT[h_]
                        i = T.matmul(pSb[lpar][:, hh * 128:(hh + 1) * 128],
                                     sKTl[(kv % 2) * 64:(kv % 2) * 64 + 64, s_ % 3,
                                          (kv // 2) * 128:(kv // 2) * 128 + 128],
                                     sQTl[(qs % 2) * 64:(qs % 2) * 64 + 64, qs // 2, :],
                                     start=(hh == 0),
                                     stop=(mask is None and hh == 3))
                    if mask is not None:
                        for hh in range(4):
                            i = T.matmul(pSb[lpar][:, hh * 128:(hh + 1) * 128],
                                         mask[:], sIdent[:],
                                         start=False, stop=(hh == 3))
                    smm = p.inc("te", i)
                    p.wait(Sc, "Sc", "te", smm)
                    p.wguard(Sc, "Sc", f"sPT{sp}")
                    i = Sc.activation(out=sPT[:, sp, 0:512], in_=pSb[lpar][:, 0:512],
                                      func=AF.Exp, scale=0.125)
                    ec = p.inc("se", i)
                    p.setguard(f"pS{lpar}", "se", ec)
                    pend_q.append((si, s_, sp, ec))
                    if len(pend_q) > 2:
                        emit_pv_l(pend_q.pop(0))
                    lpar ^= 1
                while pend_q:
                    emit_pv_l(pend_q.pop(0), final=(len(pend_q) == 0))
                p.wguard(V, "V", f"sAttnL{t % 2}")
                p.wguard(V, "V", "sAttnL_dma")
                attn_evac(pv, pvp,
                          sAttnL[:, t % 2, g * 256:(g + 1) * 256].rearrange(
                              "p (s k) -> p s k", k=64))
            if grange[-1] == 3:
                for s_ in slots:
                    p.setguard(f"sVl{s_ % 3}", "te")
                    p.setguard(f"sKTl{s_ % 3}", "te")
                p.setguard("sQTl", "te")
                st_["alve"] = p.cnt["ve"]

        def v4(t):
            st_ = vs.pop(t)
            slot = st_["slot"]
            x1a = sX1[:, t % 2, :]
            if _DBG_FULL:
                p.wait(Sy, "Sy", "ve")
                p.inc("st", Sy.dma_start(out=dbgal_d[t * 128:(t + 1) * 128, :],
                                         in_=sAttnL[:, t % 2, :]), 16)
                p.setguard("sAttnL_dma", "st")
            p.wait(T, "T", "ve", st_["alve"])
            p.wguard(T, "T", "pT")
            for c in range(8):
                i = T.transpose(pT[:, c * 128:(c + 1) * 128],
                                sAttnL[:, t % 2, c * 128:(c + 1) * 128], sIdent[:])
            tc = p.inc("te", i)
            p.setguard(f"sAttnL{t % 2}", "te", tc)
            p.wait(Sc, "Sc", "te", tc)
            p.wguard(Sc, "Sc", "sATT")
            i = Sc.copy(out=sATT[:, :, :],
                        in_=pT.rearrange("p (c n) -> p c n", n=128))
            attc = p.inc("se", i)
            p.setguard("pT", "se", attc)
            p.wait(T, "T", "se", attc)
            p.wguard(T, "T", "pP")
            for nb_ in range(2):
                for c in range(8):
                    i = T.matmul(pP[:, nb_ * 512:(nb_ + 1) * 512], sATT[:, c, :],
                                 wWoL[:, c, nb_ * 512:(nb_ + 1) * 512],
                                 start=(c == 0), stop=(c == 7))
            woc = p.inc("te", i)
            p.setguard("sATT", "te", woc)
            p.wait(V, "V", "te", woc)
            x2a = sXblk[:, slot, :]
            i = V.tensor_add(out=x1a, in0=pP[:, :], in1=x1a)
            x2c = p.inc("ve", i)
            p.setguard("pP", "ve", x2c)
            i = V.tensor_add(out=x2a, in0=x1a, in1=x2a)
            x2c = p.inc("ve", i)
            p.setguard(f"sX1_{t % 2}", "ve", x2c)
            p.wguard(V, "V", "sQtm")
            ln_apply(x2a, sQtm[:, :])
            y1c = p.cnt["ve"]
            p.setguard(f"xblkA{slot}", "ve", y1c)
            p.wait(G, "G", "ve", y1c)
            p.inc("gst", G.dma_start(out=y1_d[t * 128:(t + 1) * 128, :],
                                     in_=sQtm[:, :]), 16)
            p.setguard("sQtm", "gst")
            if t + 2 < NT:
                load_x(t + 2, (t + 2) % 2)

        v1(4 * b)
        v2a(4 * b)
        v2bV(4 * b)
        for ti in range(4):
            t = 4 * b + ti
            if ti >= 1:
                v4(t - 1)
            v2bT(t)
            v3(t, (0, 1))
            if ti < 3:
                v1(t + 1)
            v3(t, (2, 3))
            if ti < 3:
                v2a(t + 1)
                v2bV(t + 1)
        pending_v4[0] = (lambda f=v4, t_=4 * b + 3: f(t_))

    pending_v4[0]()
    pending_v4[0] = None
    ab_te, ab_ve, ab_se = p.cnt["te"], p.cnt["ve"], p.cnt["se"]
    stAB.close()

    wW1 = sb("wW1", [128, 8, 4 * D], BF16)
    wW2 = sb("wW2", [128, 32, D], BF16)
    sB1r = sb("sB1r", [1, 4 * D], BF16)
    sB2r = sb("sB2r", [1, D], BF16)
    sY1c = sb("sY1c", [128, 2, D], F32)
    sY1b = sb("sY1b", [128, 2, D], BF16)
    sY1T = sb("sY1T", [128, 2, 8, 128], BF16)
    sHtm = sb("sHtm", [128, 4 * D], BF16)
    sHT = sb("sHT", [128, 32, 128], BF16)
    sX3 = sb("sX3", [128, D], F32)
    sOut = sb("sOut", [128, 2, D], F32)

    p.wait(G, "G", "te", ab_te)
    p.wait(G, "G", "ve", ab_ve)
    p.wait(G, "G", "se", ab_se)
    p.inc("ldw", G.dma_start(out=sB1r[:], in_=b1_d[:, :]), 16)
    p.inc("ldw", G.dma_start(out=sB2r[:], in_=b2_d[:, :]), 16)
    w12_ldw = p.cnt["ldw"]
    w1p, w2p = [], []
    for w in range(4):
        w1p.append(("lw%d" % w, p.inc("lw%d" % w, G.dma_start(
            out=wW1[:, :, w * 1024:(w + 1) * 1024],
            in_=w1_d[:, w * 1024:(w + 1) * 1024].rearrange("(c p) n -> p c n", p=128)), 16)))
    for w in range(4):
        w2p.append(("lw%d" % (4 + w), p.inc("lw%d" % (4 + w), G.dma_start(
            out=wW2[:, w * 8:(w + 1) * 8, :],
            in_=w2_d[w * 1024:(w + 1) * 1024, :].rearrange("(c p) n -> p c n", p=128)), 16)))

    y1load = {}
    p.wait(G, "G", "gst", 16 * NT)

    def load_y1(t):
        p.wguard(G, "G", f"sY1c{t % 2}")
        sem = f"ly{t % 3}"
        y1load[t] = (sem, p.inc(sem, G.dma_start(out=sY1c[:, t % 2, :],
                                                 in_=y1_d[t * 128:(t + 1) * 128, :]), 16))

    load_y1(0)
    if NT > 1:
        load_y1(1)
    prep_ve = {}

    def prep_y1(t):
        ya_ = sY1c[:, t % 2, :]
        p.wait(V, "V", y1load[t][0], y1load[t][1])
        p.wguard(V, "V", f"sY1b{t % 2}")
        i = V.tensor_copy(out=sY1b[:, t % 2, :], in_=ya_)
        ybc = p.inc("ve", i)
        p.wait(T, "T", "ve", ybc)
        p.wait(T, "T", "ldw", w12_ldw)
        p.wguard(T, "T", "pT")
        for c in range(8):
            i = T.transpose(pT[:, c * 128:(c + 1) * 128],
                            sY1b[:, t % 2, c * 128:(c + 1) * 128], sIdent[:])
        tc = p.inc("te", i)
        p.setguard(f"sY1b{t % 2}", "te", tc)
        p.wait(V, "V", "te", tc)
        p.wguard(V, "V", f"sY1T{t % 2}")
        i = V.tensor_copy(out=sY1T[:, t % 2, :, :],
                          in_=pT.rearrange("p (c n) -> p c n", n=128))
        p.setguard("pT", "ve", p.inc("ve", i))
        prep_ve[t] = p.cnt["ve"]

    prep_y1(0)
    for t in range(NT):
        ya = sY1c[:, t % 2, :]
        yT = sY1T[:, t % 2, :, :]
        p.wait(T, "T", "ve", prep_ve[t])
        hpair = [(pP[:, 0:512], pP[:, 512:1024], "pP"),
                 (pV0[:, 0:512], pV1[:, 0:512], "pVh")]
        hc_w = {}

        def emit_h(w):
            pa, pb, gname = hpair[w % 2]
            p.wait(T, "T", w1p[w][0], w1p[w][1])
            p.wguard(T, "T", gname)
            for nb_, dst in ((0, pa), (1, pb)):
                hcol = w * 1024 + nb_ * 512
                for c in range(8):
                    T.matmul(dst, yT[:, c, :], wW1[:, c, hcol:hcol + 512],
                             start=(c == 0), stop=False)
                i = T.matmul(dst, sOnes[0:1, :], sB1r[0:1, hcol:hcol + 512],
                             start=False, stop=True)
            hmm = p.inc("te", i)
            p.wait(Sc, "Sc", "te", hmm)
            p.wguard(Sc, "Sc", "sHtm")
            Sc.activation(out=sHtm[:, w * 1024:w * 1024 + 512], in_=pa, func=AF.Relu)
            i = Sc.activation(out=sHtm[:, w * 1024 + 512:(w + 1) * 1024], in_=pb,
                              func=AF.Relu)
            hc = p.inc("se", i)
            p.setguard(gname, "se", hc)
            hc_w[w] = hc

        ht_c = {}

        def emit_tailA(w):
            p.wait(T, "T", "se", hc_w[w])
            p.wguard(T, "T", "pT")
            for c in range(8):
                i = T.transpose(pT[:, c * 128:(c + 1) * 128],
                                sHtm[:, w * 1024 + c * 128:w * 1024 + (c + 1) * 128],
                                sIdent[:])
            tc = p.inc("te", i)
            p.wait(V, "V", "te", tc)
            p.wguard(V, "V", "sHT")
            i = V.tensor_copy(out=sHT[:, w * 8:(w + 1) * 8, :],
                              in_=pT.rearrange("p (c n) -> p c n", n=128))
            htc = p.inc("ve", i)
            p.setguard("pT", "ve", htc)
            ht_c[w] = htc

        def emit_tailB(w):
            p.wait(T, "T", "ve", ht_c[w])
            p.wait(T, "T", w2p[w][0], w2p[w][1])
            if w == 0:
                p.wguard(T, "T", "pSf")
            for c in range(8):
                cc = w * 8 + c
                T.matmul(pS0[:, :], sHT[:, cc, :], wW2[:, cc, 0:512],
                         start=(cc == 0), stop=False)
                i = T.matmul(pS1[:, :], sHT[:, cc, :], wW2[:, cc, 512:1024],
                             start=(cc == 0), stop=False)
            p.inc("te", i)

        for w in range(4):
            emit_h(w)
            if w == 1 and t + 1 < NT:
                prep_y1(t + 1)
            if w >= 1:
                emit_tailA(w - 1)
            if w >= 2:
                emit_tailB(w - 2)
        emit_tailA(3)
        emit_tailB(2)
        emit_tailB(3)
        p.setguard("sHtm", "te")
        p.setguard("sHT", "te")
        p.setguard(f"sY1T{t % 2}", "te")
        T.matmul(pS0[:, :], sOnes[0:1, :], sB2r[0:1, 0:512], start=False, stop=True)
        i = T.matmul(pS1[:, :], sOnes[0:1, :], sB2r[0:1, 512:1024], start=False, stop=True)
        fmm = p.inc("te", i)
        p.wait(V, "V", "te", fmm)
        p.wguard(V, "V", "sX3")
        p.wguard(V, "V", "sX3_dma")
        V.tensor_add(out=sX3[:, 0:512], in0=pS0[:, :], in1=ya[:, 0:512])
        i = V.tensor_add(out=sX3[:, 512:1024], in0=pS1[:, :], in1=ya[:, 512:1024])
        x3c = p.inc("ve", i)
        p.setguard("pSf", "ve", x3c)
        p.setguard(f"sY1c{t % 2}", "ve", x3c)
        if t + 2 < NT:
            load_y1(t + 2)
        if _DBG_FULL:
            p.wait(Sy, "Sy", "ve")
            p.inc("st", Sy.dma_start(out=dbgx3_d[t * 128:(t + 1) * 128, :],
                                     in_=sX3[:]), 16)
            p.setguard("sX3_dma", "st")
        p.wguard(V, "V", f"sOut{t % 2}")
        ln_apply(sX3, sOut[:, t % 2, :])
        oc = p.cnt["ve"]
        p.setguard("sX3", "ve", oc)
        p.wait(Sy, "Sy", "ve", oc)
        p.inc(f"st{t % 2}", Sy.dma_start(out=out_d[t * 128:(t + 1) * 128, :],
                                         in_=sOut[:, t % 2, :]), 16)
        p.setguard(f"sOut{t % 2}", f"st{t % 2}")

    if _DBG_FULL:
        p.wait(Sy, "Sy", "gst")
        p.inc("st", Sy.dma_start(out=dbgy1_d[:, :], in_=y1_d[:, :]), 16)
        Sy.wait_ge(p.sem["st"], p.cnt["st"])
    Sy.wait_ge(p.sem["st0"], p.cnt["st0"])
    Sy.wait_ge(p.sem["st1"], p.cnt["st1"])
    st.close()
    return nc


def _perm_wq(w):
    w = np.asarray(w)
    cols = np.concatenate([np.arange(h * 64, (h + 1) * 64) for h in QORDER])
    return np.ascontiguousarray(w[:, cols])


def _prep_inputs(S, inputs):
    c = _host_consts(S)
    common = {
        "g_wq": _perm_wq(inputs["L0_attn_g_wq"]), "g_kv_down": inputs["L0_attn_g_kv_down"],
        "g_k_up": inputs["L0_attn_g_k_up"], "g_v_up": inputs["L0_attn_g_v_up"],
        "g_wo": inputs["L0_attn_g_wo"], "l_wq": _perm_wq(inputs["L0_attn_l_wq"]),
        "l_wk": inputs["L0_attn_l_wk"], "l_wv": inputs["L0_attn_l_wv"],
        "l_wo": inputs["L0_attn_l_wo"], "w1": inputs["L0_ffn_w1"],
        "w2": inputs["L0_ffn_w2"],
        "b1r": np.asarray(inputs["L0_ffn_b1"]).reshape(1, -1),
        "b2r": np.asarray(inputs["L0_ffn_b2"]).reshape(1, -1),
        "n1pc": np.asarray(inputs["L0_attn_norm1_w"]).reshape(8, 128).T,
        "n2pc": np.asarray(inputs["L0_attn_norm2_w"]).reshape(8, 128).T,
    }
    common.update(c)
    return {k: np.ascontiguousarray(v) for k, v in common.items()}


def kernel(**inputs):
    x = np.asarray(inputs["L0_attn_hidden"])
    B, S, _ = x.shape
    common = _prep_inputs(S, inputs)
    nc = build_program(S)
    in_maps = []
    for b in range(B):
        m = dict(common)
        m["x"] = np.ascontiguousarray(x[b])
        in_maps.append(m)
    res = run_bass_kernel_spmd(nc, in_maps, core_ids=list(range(B)))
    out = np.stack([np.asarray(res.results[b]["out"]) for b in range(B)], axis=0)
    return out.astype(np.float32)


# version 24
# speedup vs baseline: 1.1907x; 1.1907x over previous
import contextlib
import numpy as np
import ml_dtypes
import concourse.bass as bass
import concourse.mybir as mybir
from concourse.bass_utils import run_bass_kernel_spmd
from concourse.ap import AP as _AP


def _ap(x):
    return x if isinstance(x, _AP) else x[:]

F32 = mybir.dt.float32
BF16 = mybir.dt.bfloat16
AF = mybir.ActivationFunctionType
ALU = mybir.AluOpType
AX = mybir.AxisListType

D = 1024
NH = 16
KVH = 4
HD = 64
LAT = 256
NEG = -1.0e6
BF = ml_dtypes.bfloat16
QORDER = [0, 4, 1, 5, 2, 6, 3, 7, 8, 12, 9, 13, 10, 14, 11, 15]
QSLOT = {h: s for s, h in enumerate(QORDER)}
_DBG = None
_DBG_TILE = 2
_DBG_FULL = False


def _rope_tables(S, base):
    exps = np.arange(0, HD, 2).astype(np.float32) / np.float32(HD)
    invf = (np.float32(1.0) / np.power(np.float32(base), exps)).astype(np.float32)
    t = np.arange(S, dtype=np.float32)
    freqs = (t[:, None] * invf[None, :]).astype(np.float32)
    cos = np.cos(freqs.astype(np.float64)).astype(np.float32)
    sin = np.sin(freqs.astype(np.float64)).astype(np.float32)
    nt = S // 128
    return (np.ascontiguousarray(cos.reshape(nt, 128, 32).transpose(1, 0, 2)),
            np.ascontiguousarray(sin.reshape(nt, 128, 32).transpose(1, 0, 2)))


def _host_consts(S):
    p = np.arange(128)
    maskd = np.where(p[:, None] <= p[None, :], 0.0, NEG).astype(np.float32)
    maskl = np.where(p[:, None] > p[None, :], 0.0, NEG).astype(np.float32)
    cg, sg = _rope_tables(S, 1000000.0)
    cl, sl = _rope_tables(S, 10000.0)
    eye = np.eye(128, dtype=BF)
    return {
        "identB": eye,
        "ident4": np.ascontiguousarray(np.tile(eye, (1, 4))),
        "onesB": np.ones((1, 128), dtype=BF),
        "maskdT": np.ascontiguousarray(maskd.T).astype(BF),
        "masklT": np.ascontiguousarray(maskl.T).astype(BF),
        "cosg": cg.astype(BF), "sing": sg.astype(BF),
        "cosl": cl.astype(BF), "sinl": sl.astype(BF),
    }


class P:

    def __init__(self):
        self.sem = {}
        self.cnt = {}
        self.waited = {}
        self.guard = {}

    def add_sem(self, name, handle):
        self.sem[name] = handle
        self.cnt[name] = 0

    def inc(self, name, inst, n=1):
        inst.then_inc(self.sem[name], n)
        self.cnt[name] += n
        return self.cnt[name]

    def wait(self, eng, ename, sname, val=None):
        val = self.cnt[sname] if val is None else val
        if val <= 0:
            return
        k = (ename, sname)
        if self.waited.get(k, 0) < val:
            eng.wait_ge(self.sem[sname], val)
            self.waited[k] = val

    def wguard(self, eng, ename, rname):
        g = self.guard.get(rname)
        if g is not None:
            self.wait(eng, ename, g[0], g[1])

    def setguard(self, rname, sname, val=None):
        v = self.cnt[sname] if val is None else val
        old = self.guard.get(rname)
        if old is not None and old[0] == sname and old[1] >= v:
            return
        self.guard[rname] = (sname, v)


def build_program(S):
    NT = S // 128
    NB = S // 512
    nc = bass.Bass("TRN2", target_bir_lowering=False)

    def din(name, shape, dt=F32):
        return nc.dram_tensor(name, list(shape), dt, kind="ExternalInput")

    x_d = din("x", [S, D])
    gwq_d = din("g_wq", [D, D]); gkvd_d = din("g_kv_down", [D, LAT])
    gkup_d = din("g_k_up", [LAT, LAT]); gvup_d = din("g_v_up", [LAT, LAT])
    gwo_d = din("g_wo", [D, D])
    lwq_d = din("l_wq", [D, D]); lwk_d = din("l_wk", [D, LAT])
    lwv_d = din("l_wv", [D, LAT]); lwo_d = din("l_wo", [D, D])
    w1_d = din("w1", [D, 4 * D]); w2_d = din("w2", [4 * D, D])
    b1_d = din("b1r", [1, 4 * D]); b2_d = din("b2r", [1, D])
    n1_d = din("n1pc", [128, 8]); n2_d = din("n2pc", [128, 8])
    idb_d = din("identB", [128, 128], BF16)
    id4_d = din("ident4", [128, 512], BF16)
    ones_d = din("onesB", [1, 128], BF16)
    maskd_d = din("maskdT", [128, 128], BF16)
    maskl_d = din("masklT", [128, 128], BF16)
    cosg_d = din("cosg", [128, NT, 32], BF16); sing_d = din("sing", [128, NT, 32], BF16)
    cosl_d = din("cosl", [128, NT, 32], BF16); sinl_d = din("sinl", [128, NT, 32], BF16)

    y1_d = nc.dram_tensor("y1scratch", [S, D], F32, kind="Internal")
    out_d = nc.dram_tensor("out", [S, D], F32, kind="ExternalOutput")
    if _DBG_FULL:
        dbgag_d = nc.dram_tensor("dbg_ag", [S, D], BF16, kind="ExternalOutput")
        dbgal_d = nc.dram_tensor("dbg_al", [S, D], BF16, kind="ExternalOutput")
        dbgy1_d = nc.dram_tensor("dbg_y1f", [S, D], F32, kind="ExternalOutput")
        dbgx3_d = nc.dram_tensor("dbg_x3f", [S, D], F32, kind="ExternalOutput")
        dbgy1b_d = nc.dram_tensor("dbg_y1b", [S, D], BF16, kind="ExternalOutput")

    p = P()
    st = contextlib.ExitStack()
    sb = lambda name, shape, dt: st.enter_context(nc.sbuf_tensor(name, list(shape), dt))
    ps = lambda name, shape, dt: st.enter_context(nc.psum_tensor(name, list(shape), dt))
    for s in ("ldx", "ldw", "gst", "st", "te", "ve", "se",
              "lx0", "lx1", "lx2", "lx3", "lx4", "lx5",
              "ly0", "ly1", "ly2", "st0", "st1",
              "lw0", "lw1", "lw2", "lw3", "lw4", "lw5", "lw6", "lw7"):
        p.add_sem(s, st.enter_context(nc.semaphore(s)))

    T, V, Sc, Sy, G = nc.tensor, nc.vector, nc.scalar, nc.sync, nc.gpsimd

    sIdent = sb("sIdent", [128, 128], BF16)
    sId4 = sb("sId4", [128, 512], BF16)
    sOnes = sb("sOnes", [1, 128], BF16)
    sMaskD = sb("sMaskD", [128, 128], BF16)
    sMaskL = sb("sMaskL", [128, 128], BF16)
    sCosG = sb("sCosG", [128, NT, 32], BF16)
    sSinG = sb("sSinG", [128, NT, 32], BF16)
    sCosL = sb("sCosL", [128, NT, 32], BF16)
    sSinL = sb("sSinL", [128, NT, 32], BF16)
    sN1 = sb("sN1", [128, 8], F32)
    sN2 = sb("sN2", [128, 8], F32)
    sEps = sb("sEps", [128, 4], F32)
    sStats = sb("sStats", [128, 128], F32)
    sDen = sb("sDen", [128, 16], F32)
    sBn = sb("sBn", [128, 16], F32)
    sMv = sb("sMv", [128, 64], F32)

    pT = ps("pT", [128, 1024], BF16)
    pSd = ps("pSd", [128, 1024], F32)
    pS0 = pSd[:, 0:512]
    pS1 = pSd[:, 512:1024]
    pP = ps("pP", [128, 1024], F32)
    pV0 = ps("pV0", [128, 512], F32)
    pV1 = ps("pV1", [128, 512], F32)
    pX = ps("pX", [128, 512], F32)
    pSb = [pS0, pS1]
    pVb = [pV0, pV1]

    EPS_QK_COL, EPS_LN_COL = 0, 1
    V.memset(sEps[:, 0:1], 1e-6)
    V.memset(sEps[:, 1:2], 1e-5)

    for dst, src in ((sIdent, idb_d), (sId4, id4_d), (sOnes, ones_d),
                     (sMaskD, maskd_d), (sMaskL, maskl_d),
                     (sCosG, cosg_d), (sSinG, sing_d),
                     (sCosL, cosl_d), (sSinL, sinl_d), (sN1, n1_d), (sN2, n2_d)):
        p.inc("ldx", Sy.dma_start(out=dst[:], in_=src[:]), 16)
    consts_ldx = p.cnt["ldx"]

    rstd1w = sStats[:, 0:16]
    rstd1 = sStats[:, 0:1]
    rstd2w = sStats[:, 16:32]
    rstd2 = sStats[:, 16:17]
    saccw = sStats[:, 32:48]
    sacc = sStats[:, 32:33]
    stmpw = sStats[:, 48:64]
    ssq16 = sStats[:, 64:80]
    rq16 = sStats[:, 80:96]
    ssk4 = sStats[:, 96:112]
    rk4 = sStats[:, 112:128]

    def rmsnorm_stats(src_ap, dst_rstd_w, sq_buf):
        src_ap = _ap(src_ap)
        p.wguard(Sc, "Sc", "sSq")
        i = Sc.activation(out=sq_buf[:, 0:D], in_=src_ap, func=AF.Square,
                          accum_out=sacc)
        c = p.inc("se", i)
        p.wait(Sc, "Sc", "se", c)
        i = Sc.copy(out=saccw[:, :], in_=sacc.broadcast_to([128, 16]))
        i = Sc.activation(out=stmpw, in_=saccw, func=AF.Ln,
                          bias=sEps[:, EPS_QK_COL:EPS_QK_COL + 1], scale=1.0 / D)
        i = Sc.activation(out=dst_rstd_w, in_=stmpw, func=AF.Exp, scale=-0.5)
        p.inc("se", i)
        p.setguard("sSq", "se")

    def transposes(src, chunks, dst3, dst_guard):
        src = _ap(src)
        p.wguard(T, "T", "pT")
        for c in range(chunks):
            i = T.transpose(pT[:, c * 128:(c + 1) * 128],
                            src[:, c * 128:(c + 1) * 128], sIdent[:])
        tc = p.inc("te", i)
        p.wait(V, "V", "te", tc)
        p.wguard(V, "V", dst_guard)
        i = V.tensor_copy(out=dst3,
                          in_=pT[:, 0:chunks * 128].rearrange("p (c n) -> p c n", n=128))
        c2 = p.inc("ve", i)
        p.setguard("pT", "ve", c2)
        return c2

    def rope(srcf, dstf, nh, t, costab, sintab, sq_buf):
        srcf, dstf = _ap(srcf), _ap(dstf)
        src4 = srcf.rearrange("p (h w m) -> p h w m", m=32, w=2)
        dst4 = dstf.rearrange("p (h w m) -> p h w m", m=32, w=2)
        e, o = src4[:, :, 0, :], src4[:, :, 1, :]
        eo, oo = dst4[:, :, 0, :], dst4[:, :, 1, :]
        cosb = costab[:, t, :].unsqueeze(1).broadcast_to([128, nh, 32])
        sinb = sintab[:, t, :].unsqueeze(1).broadcast_to([128, nh, 32])
        n = nh * 32
        tA = sq_buf[:, 0:n].rearrange("p (h m) -> p h m", m=32)
        tB = sq_buf[:, n:2 * n].rearrange("p (h m) -> p h m", m=32)
        V.tensor_tensor(out=tA, in0=e, in1=cosb, op=ALU.mult)
        V.tensor_tensor(out=tB, in0=o, in1=sinb, op=ALU.mult)
        V.tensor_tensor(out=eo, in0=tA, in1=tB, op=ALU.subtract)
        V.tensor_tensor(out=tA, in0=e, in1=sinb, op=ALU.mult)
        V.tensor_tensor(out=tB, in0=o, in1=cosb, op=ALU.mult)
        i = V.tensor_tensor(out=oo, in0=tA, in1=tB, op=ALU.add)
        return p.inc("ve", i)

    def qknorm(srcf, dstb, nh, ssq, rr, sq_buf):
        srcf, dstb = _ap(srcf), _ap(dstb)
        n = nh * HD
        i = V.tensor_tensor(out=sq_buf[:, 0:n], in0=srcf, in1=srcf, op=ALU.mult)
        p.inc("ve", i)
        i = V.reduce_sum(out=ssq[:, 0:nh],
                         in_=sq_buf[:, 0:n].rearrange("p (h d) -> p h d", d=HD),
                         axis=AX.X)
        c = p.inc("ve", i)
        p.wait(V, "V", "ve", c)
        p.wait(Sc, "Sc", "ve", c)
        i = Sc.activation(out=ssq[:, 0:16], in_=ssq[:, 0:16], func=AF.Ln,
                          bias=sEps[:, EPS_QK_COL:EPS_QK_COL + 1], scale=1.0 / HD)
        i = Sc.activation(out=rr[:, 0:16], in_=ssq[:, 0:16], func=AF.Exp, scale=-0.5)
        c = p.inc("se", i)
        p.wait(V, "V", "se", c)
        rb = rr[:, 0:nh].unsqueeze(2).broadcast_to([128, nh, HD])
        i = V.tensor_tensor(out=dstb.rearrange("p (h d) -> p h d", d=HD),
                            in0=srcf.rearrange("p (h d) -> p h d", d=HD),
                            in1=rb, op=ALU.mult)
        return p.inc("ve", i)

    def ln_apply(srcf, dstf):
        srcf, dstf = _ap(srcf), _ap(dstf)
        i = V.bn_stats(out=sBn[:, 0:6], in_=srcf[:, 0:512])
        i = V.bn_stats(out=sBn[:, 6:12], in_=srcf[:, 512:1024])
        c = p.inc("ve", i)
        p.wait(V, "V", "ve", c)
        i = V.bn_aggr(out=sMv[:, 0:2], in_=sBn[:, 0:12])
        c = p.inc("ve", i)
        p.wait(V, "V", "ve", c)
        i = V.tensor_copy(out=sMv[:, 16:32], in_=sMv[:, 1:2].broadcast_to([128, 16]))
        c = p.inc("ve", i)
        p.wait(V, "V", "ve", c)
        p.wait(Sc, "Sc", "ve", c)
        i = Sc.activation(out=sMv[:, 32:48], in_=sMv[:, 16:32], func=AF.Ln,
                          bias=sEps[:, EPS_LN_COL:EPS_LN_COL + 1], scale=1.0)
        i = Sc.activation(out=sMv[:, 48:64], in_=sMv[:, 32:48], func=AF.Exp,
                          scale=-0.5)
        c = p.inc("se", i)
        p.wait(V, "V", "se", c)
        i = V.scalar_tensor_tensor(out=dstf, in0=srcf, scalar=sMv[:, 0:1],
                                   in1=sMv[:, 48:49].broadcast_to([128, D]),
                                   op0=ALU.subtract, op1=ALU.mult)
        return p.inc("ve", i)

    def attn_evac(pv, pvp, out3):
        p.wait(V, "V", "te", p.cnt["te"])
        i = V.tensor_copy(out=sDen[:, 0:4],
                          in_=pv[:, 0:264].rearrange("p (s k) -> p s k", k=66)[:, :, 64:65])
        c = p.inc("ve", i)
        p.wait(V, "V", "ve", c)
        i = V.reciprocal(out=sDen[:, 0:4], in_=sDen[:, 0:4])
        c = p.inc("ve", i)
        p.wait(V, "V", "ve", c)
        i = V.tensor_tensor(
            out=out3,
            in0=pv[:, 0:264].rearrange("p (s k) -> p s k", k=66)[:, :, 0:64],
            in1=sDen[:, 0:4].unsqueeze(2).broadcast_to([128, 4, 64]),
            op=ALU.mult)
        c = p.inc("ve", i)
        p.setguard(f"pV{pvp}", "ve", c)

    def dump(name, ap, t=None):
        if _DBG is None or (t is not None and t != _DBG_TILE):
            return
        ap = _ap(ap)
        d = nc.dram_tensor("dbg_" + name, list(ap.shape), ap.dtype,
                           kind="ExternalOutput")
        for sn in ("te", "ve", "se", "ldx", "ldw", "gst"):
            p.wait(Sy, "Sy", sn)
        p.inc("st", Sy.dma_start(out=d[:], in_=ap), 16)
        for eng, en in ((T, "T"), (V, "V"), (Sc, "Sc")):
            p.wait(eng, en, "st", p.cnt["st"])
        _DBG.append("dbg_" + name)

    stAB = contextlib.ExitStack()
    sbA = lambda name, shape, dt: stAB.enter_context(nc.sbuf_tensor(name, list(shape), dt))

    wWqG = sbA("wWqG", [128, 8, D], BF16)
    wKVd = sbA("wKVd", [128, 8, LAT], BF16)
    wKup = sbA("wKup", [128, 2, LAT], BF16)
    wVup = sbA("wVup", [128, 2, LAT], BF16)
    wWoG = sbA("wWoG", [128, 8, D], BF16)
    wWqL = sbA("wWqL", [128, 8, D], BF16)
    wWkL = sbA("wWkL", [128, 8, LAT], BF16)
    wWvL = sbA("wWvL", [128, 8, LAT], BF16)
    wWoL = sbA("wWoL", [128, 8, D], BF16)

    with nc.sbuf_tensor("wstageA", [128, 8, D], F32) as wsA, \
         nc.sbuf_tensor("wstageB", [128, 8, D], F32) as wsB:
        for dst, src in ((wKup, gkup_d), (wVup, gvup_d), (wWoG, gwo_d), (wWoL, lwo_d)):
            p.inc("ldw", G.dma_start(out=dst[:], in_=src.rearrange("(c p) n -> p c n", p=128)), 16)
        folds = ((gwq_d, wWqG, sN1, 8, D), (gkvd_d, wKVd, sN1, 8, LAT),
                 (lwq_d, wWqL, sN2, 8, D), (lwk_d, wWkL, sN2, 8, LAT),
                 (lwv_d, wWvL, sN2, 8, LAT))
        ws = [wsA, wsB]
        for fi, (src, dst, nw, nch, ncol) in enumerate(folds):
            buf = ws[fi % 2]
            p.wguard(G, "G", f"wstage{fi % 2}")
            lv = p.inc("ldw", G.dma_start(out=buf[:, 0:nch, 0:ncol],
                                          in_=src.rearrange("(c p) n -> p c n", p=128)), 16)
            p.wait(V, "V", "ldw", lv)
            p.wait(V, "V", "ldx", consts_ldx)
            for c in range(nch):
                i = V.tensor_scalar(out=dst[:, c, 0:ncol], in0=buf[:, c, 0:ncol],
                                    scalar1=nw[:, c:c + 1], scalar2=None, op0=ALU.mult)
            p.inc("ve", i)
            p.setguard(f"wstage{fi % 2}", "ve")
    weights_ldw = p.cnt["ldw"]
    setup_ve = p.cnt["ve"]

    sKTg0 = sbA("sKTg0", [128, S], BF16)
    sKTg1 = sbA("sKTg1", [128, S], BF16)
    sVg = sbA("sVg", [128, NT, 260], BF16)
    sXblk = sbA("sXblk", [128, 4, D], F32)
    sX1 = sbA("sX1", [128, 2, D], F32)
    sXb = sbA("sXb", [128, D], BF16)
    sXT = sbA("sXT", [128, 8, 128], BF16)
    sLat = sbA("sLat", [128, LAT], BF16)
    sLatT = sbA("sLatT", [128, 2, 128], BF16)
    sKtm = sbA("sKtm", [128, LAT], F32)
    sKbf = sbA("sKbf", [128, LAT], BF16)
    sQtm = sbA("sQtm", [128, D], F32)
    sQrope = sbA("sQrope", [128, D], F32)
    sQb = sbA("sQb", [128, D], BF16)
    sSq = sbA("sSq", [128, D], F32)
    sQTg = sbA("sQTg", [128, 2, 8, 512], BF16)
    sQTl = sbA("sQTl", [128, 8, 128], BF16)
    sKTl = sbA("sKTl", [128, 3, 256], BF16)
    sVl = sbA("sVl", [128, 3, 264], BF16)
    sPT = sbA("sPT", [128, 3, 1024], BF16)
    sAttn = sbA("sAttn", [128, 4, D], BF16)
    sAttnL = sbA("sAttnL", [128, 2, D], BF16)
    sATT = sbA("sATT", [128, 8, 128], BF16)

    for eng, en in ((V, "V"), (Sc, "Sc"), (T, "T"), (Sy, "Sy"), (G, "G")):
        p.wait(eng, en, "ve", setup_ve)
        p.wait(eng, en, "ldw", weights_ldw)
        p.wait(eng, en, "ldx", p.cnt["ldx"])
    p.inc("ve", V.memset(sVg[:], 1.0))
    p.inc("ve", V.memset(sVl[:], 1.0))

    xload_val = {}
    xissue = [0]
    scnt = [0]

    def load_x(t, slot):
        p.wguard(Sy, "Sy", f"xblkA{slot}")
        p.wguard(Sy, "Sy", f"xblkB{slot}")
        sem = f"lx{xissue[0] % 6}"
        xissue[0] += 1
        v = p.inc(sem, Sy.dma_start(out=sXblk[:, slot, :],
                                    in_=x_d[t * 128:(t + 1) * 128, :]), 16)
        xload_val[t] = (sem, v, slot)

    qready_ve = {}

    def part1_stages(t):
        b = t // 4
        ti = t % 4
        qp = b % 2
        st_ = {}

        def s0():
            sem, lv, slot = xload_val[t]
            xa = sXblk[:, slot, :]
            p.wait(V, "V", sem, lv)
            p.wait(Sc, "Sc", sem, lv)
            rmsnorm_stats(xa, rstd1w, sSq)
            p.setguard(f"xblkB{slot}", "se")
            p.wguard(V, "V", "sXb")
            i = V.tensor_copy(out=sXb[:], in_=xa)
            st_["xbc"] = p.inc("ve", i)
            p.setguard(f"xblkA{slot}", "ve", st_["xbc"])
            if ti < 2 and t + 2 < NT:
                load_x(t + 2, 2 + ((t + 2) % 2))

        def s1():
            p.wait(T, "T", "ve", st_["xbc"])
            transposes(sXb, 8, sXT[:, :, :], "sXT")
            p.setguard("sXb", "te")
            p.wait(T, "T", "ve", p.cnt["ve"])
            p.wguard(T, "T", "pX")
            for c in range(8):
                i = T.matmul(pX[:, 0:512], sXT[:, c, :], wWqG[:, c, 0:512],
                             start=(c == 0), stop=(c == 7))
            q0 = p.inc("te", i)
            p.wait(V, "V", "te", q0)
            p.wait(V, "V", "se", p.cnt["se"])
            p.wguard(V, "V", "sQtm")
            i = V.tensor_scalar(out=sQtm[:, 0:512], in0=pX[:, 0:512], scalar1=rstd1,
                                scalar2=None, op0=ALU.mult)
            p.setguard("pX", "ve", p.inc("ve", i))

        def s2():
            p.wguard(T, "T", "pX")
            for c in range(8):
                i = T.matmul(pX[:, 0:512], sXT[:, c, :], wWqG[:, c, 512:1024],
                             start=(c == 0), stop=(c == 7))
            q1 = p.inc("te", i)
            p.wait(V, "V", "te", q1)
            i = V.tensor_scalar(out=sQtm[:, 512:1024], in0=pX[:, 0:512], scalar1=rstd1,
                                scalar2=None, op0=ALU.mult)
            p.setguard("pX", "ve", p.inc("ve", i))

        def s3():
            p.wguard(T, "T", "pX")
            for c in range(8):
                i = T.matmul(pX[:, 0:LAT], sXT[:, c, :], wKVd[:, c, :],
                             start=(c == 0), stop=(c == 7))
            lmm = p.inc("te", i)
            p.setguard("sXT", "te", lmm)
            p.wait(V, "V", "te", lmm)
            p.wguard(V, "V", "sLat")
            i = V.tensor_scalar(out=sLat[:], in0=pX[:, 0:LAT], scalar1=rstd1,
                                scalar2=None, op0=ALU.mult)
            p.setguard("pX", "ve", p.inc("ve", i))
            p.wguard(V, "V", "sSq")
            p.wguard(V, "V", "sQrope")
            rope(sQtm, sQrope, NH, t, sCosG, sSinG, sSq)
            p.setguard("sQtm", "ve")

        def s4():
            p.wait(T, "T", "ve", p.cnt["ve"])
            transposes(sLat, 2, sLatT[:, :, :], "sLatT")
            p.setguard("sLat", "te")
            p.wait(T, "T", "ve", p.cnt["ve"])
            p.wguard(T, "T", "pX")
            for c in range(2):
                i = T.matmul(pX[:, 0:LAT], sLatT[:, c, :], wKup[:, c, :],
                             start=(c == 0), stop=(c == 1))
            kmm = p.inc("te", i)
            p.wait(Sc, "Sc", "te", kmm)
            p.wguard(Sc, "Sc", "sKtm")
            st_["kevc"] = p.inc("se", Sc.copy(out=sKtm[:], in_=pX[:, 0:LAT]))
            p.setguard("pX", "se", st_["kevc"])
            p.wguard(V, "V", "sQb")
            st_["qbc"] = qknorm(sQrope, sQb, NH, ssq16, rq16, sSq)

        def s5():
            p.wguard(T, "T", "pX")
            for c in range(2):
                i = T.matmul(pX[:, 0:LAT], sLatT[:, c, :], wVup[:, c, :],
                             start=(c == 0), stop=(c == 1))
            vmm = p.inc("te", i)
            p.setguard("sLatT", "te", vmm)
            p.wait(V, "V", "te", vmm)
            i = V.tensor_copy(
                out=sVg[:, t, 0:260].rearrange("p (h k) -> p h k", k=65)[:, :, 0:64],
                in_=pX[:, 0:LAT].rearrange("p (h k) -> p h k", k=64))
            p.setguard("pX", "ve", p.inc("ve", i))
            p.wait(V, "V", "se", st_["kevc"])
            p.wguard(V, "V", "sKbf")
            rope(sKtm, sQrope[:, 0:LAT], KVH, t, sCosG, sSinG, sSq)
            p.setguard("sKtm", "ve")
            st_["kbfc"] = qknorm(sQrope[:, 0:LAT], sKbf, KVH, ssk4, rk4, sSq)
            p.setguard("sQrope", "ve")
            p.setguard("sSq", "ve")

        def s6():
            p.wait(T, "T", "ve", st_["qbc"])
            p.wguard(T, "T", "pT")
            for c in range(8):
                i = T.transpose(pT[:, c * 128:(c + 1) * 128],
                                sQb[:, c * 128:(c + 1) * 128], sIdent[:])
            tc = p.inc("te", i)
            p.setguard("sQb", "te", tc)
            p.wait(V, "V", "te", tc)
            p.wguard(V, "V", f"sQTg{qp}")
            i = V.tensor_copy(out=sQTg[:, qp, :, ti * 128:(ti + 1) * 128],
                              in_=pT.rearrange("p (c n) -> p c n", n=128))
            p.setguard("pT", "ve", p.inc("ve", i))
            p.wait(T, "T", "ve", st_["kbfc"])
            p.wguard(T, "T", "pT")
            i = T.transpose(pT[:, 0:128], sKbf[:, 0:128], sIdent[:])
            i = T.transpose(pT[:, 128:256], sKbf[:, 128:256], sIdent[:])
            tc = p.inc("te", i)
            p.setguard("sKbf", "te", tc)
            p.wait(V, "V", "te", tc)
            V.tensor_copy(out=sKTg0[:, t * 128:(t + 1) * 128], in_=pT[:, 0:128])
            i = V.tensor_copy(out=sKTg1[:, t * 128:(t + 1) * 128], in_=pT[:, 128:256])
            p.setguard("pT", "ve", p.inc("ve", i))
            if ti == 3:
                qready_ve[b] = p.cnt["ve"]

        return [s0, s1, s2, s3, s4, s5, s6]

    load_x(0, 2)
    load_x(1, 3)
    for t in range(min(4, NT)):
        for f in part1_stages(t):
            f()

    pending_v4 = [None]
    for b in range(NB):
        qp = b % 2
        nj = 4 * b + 4
        par = 0
        p.wait(T, "T", "ve", qready_ve[b])
        stage_q = []
        if pending_v4[0] is not None:
            stage_q.append(pending_v4[0])
            pending_v4[0] = None
        if b + 1 < NB:
            load_x(4 * (b + 1), 2)
            load_x(4 * (b + 1) + 1, 3)
            for t_ in range(4 * (b + 1), 4 * (b + 1) + 4):
                stage_q.extend(part1_stages(t_))

        def drain_stage():
            if stage_q:
                stage_q.pop(0)()

        pBk = [pSd, pP]
        for hp, h0 in enumerate((0, 1, 2, 3, 8, 9, 10, 11)):
            h1 = h0 + 4
            kv0_, kv1_ = h0 // 4, h1 // 4
            kt = sKTg0 if h0 < 8 else sKTg1
            p.wguard(T, "T", "pV0")
            p.wguard(T, "T", "pV1")

            def emit_pv_g2(pend, final=False):
                jc_, ioff_, sp_, ec_ = pend
                p.wait(T, "T", "se", ec_)
                if final:
                    for _ in range(3):
                        T.ldweights(sIdent[:])
                for hx, pv, kv_ in ((0, pV0, kv0_), (1, pV1, kv1_)):
                    for sub in range(ioff_, 4):
                        i = T.matmul(pv[:, sub * 66:sub * 66 + 65],
                                     sPT[:, sp_, hx * 512 + (sub - ioff_) * 128:
                                         hx * 512 + (sub - ioff_ + 1) * 128],
                                     sVg[:, jc_, kv_ * 65:kv_ * 65 + 65],
                                     start=(jc_ == 0 and sub == 0),
                                     stop=(jc_ == 4 * b + sub))
                pvc = p.inc("te", i)
                p.setguard(f"sPT{sp_}", "te", pvc)

            pend_q = []
            q0, q1 = QSLOT[h0], QSLOT[h1]
            for jc in range(nj):
                ioff = max(0, jc - 4 * b)
                N = 512 - ioff * 128
                bk = par % 2
                sp = scnt[0] % 3
                scnt[0] += 1
                if bk == 0:
                    p.wguard(T, "T", "pS0")
                    p.wguard(T, "T", "pS1")
                else:
                    p.wguard(T, "T", "pP")
                bank = pBk[bk]
                masked = jc >= 4 * b
                T.matmul(bank[:, 0:N],
                         kt[0:64, jc * 128:(jc + 1) * 128],
                         sQTg[0:64, qp, q0 // 2, ioff * 128:512],
                         start=True, stop=not masked, tile_position=(0, 0))
                i = T.matmul(bank[:, 512:512 + N],
                             kt[64:128, jc * 128:(jc + 1) * 128],
                             sQTg[64:128, qp, q1 // 2, ioff * 128:512],
                             start=True, stop=not masked, tile_position=(64, 0))
                if masked:
                    T.matmul(bank[:, 0:128], sMaskD[:], sIdent[:],
                             start=False, stop=False)
                    i = T.matmul(bank[:, 512:640], sMaskD[:], sIdent[:],
                                 start=False, stop=True)
                smm = p.inc("te", i)
                p.wait(Sc, "Sc", "te", smm)
                p.wguard(Sc, "Sc", f"sPT{sp}")
                i = Sc.activation(out=sPT[:, sp, 0:512 + N], in_=bank[:, 0:512 + N],
                                  func=AF.Exp, scale=0.125)
                ec = p.inc("se", i)
                if bk == 0:
                    p.setguard("pS0", "se", ec)
                    p.setguard("pS1", "se", ec)
                else:
                    p.setguard("pP", "se", ec)
                pend_q.append((jc, ioff, sp, ec))
                if len(pend_q) > 2:
                    emit_pv_g2(pend_q.pop(0))
                par += 1
                if nj >= 8 and jc in (nj // 4, nj // 2, 3 * nj // 4):
                    drain_stage()
            while pend_q:
                emit_pv_g2(pend_q.pop(0), final=(len(pend_q) == 0))
            p.wguard(V, "V", "sAttn")
            p.wguard(V, "V", "sAttn_dma")
            attn_evac(pV0, 0, sAttn[:, :, h0 * 64:(h0 + 1) * 64])
            attn_evac(pV1, 1, sAttn[:, :, h1 * 64:(h1 + 1) * 64])
            if hp < 7:
                drain_stage()
        while stage_q:
            stage_q.pop(0)()
        p.setguard(f"sQTg{qp}", "te")
        attn_ve = p.cnt["ve"]

        if b == 0:
            load_x(0, 0)
            load_x(1, 1)
        vs = {}

        def v1(t):
            ti_ = t % 4
            st_ = vs.setdefault(t, {})
            if _DBG_FULL:
                p.wait(Sy, "Sy", "ve")
                p.inc("st", Sy.dma_start(out=dbgag_d[t * 128:(t + 1) * 128, :],
                                         in_=sAttn[:, ti_, :]), 16)
                p.setguard("sAttn_dma", "st")
            p.wait(T, "T", "ve", attn_ve)
            p.wguard(T, "T", "pT")
            for c in range(8):
                i = T.transpose(pT[:, c * 128:(c + 1) * 128],
                                sAttn[:, ti_, c * 128:(c + 1) * 128], sIdent[:])
            tc = p.inc("te", i)
            p.setguard("sAttn", "te", tc)
            p.wait(Sc, "Sc", "te", tc)
            p.wguard(Sc, "Sc", "sATT")
            i = Sc.copy(out=sATT[:, :, :],
                        in_=pT.rearrange("p (c n) -> p c n", n=128))
            attc = p.inc("se", i)
            p.setguard("pT", "se", attc)
            p.wait(T, "T", "se", attc)
            p.wguard(T, "T", "pP")
            for nb_ in range(2):
                for c in range(8):
                    i = T.matmul(pP[:, nb_ * 512:(nb_ + 1) * 512], sATT[:, c, :],
                                 wWoG[:, c, nb_ * 512:(nb_ + 1) * 512],
                                 start=(c == 0), stop=(c == 7))
            woc = p.inc("te", i)
            p.setguard("sATT", "te", woc)
            p.wait(V, "V", "te", woc)
            sem, lv, slot = xload_val[t]
            st_["slot"] = slot
            p.wait(V, "V", sem, lv)
            p.wguard(V, "V", f"sX1_{t % 2}")
            x1a = sX1[:, t % 2, :]
            i = V.tensor_add(out=x1a, in0=pP[:, :], in1=sXblk[:, slot, :])
            x1c = p.inc("ve", i)
            p.setguard("pP", "ve", x1c)
            p.wait(Sc, "Sc", "ve", x1c)
            rmsnorm_stats(x1a, rstd2w, sSq)
            p.wguard(V, "V", "sXb")
            p.wait(V, "V", "ve", x1c)
            i = V.tensor_copy(out=sXb[:], in_=x1a)
            xbc = p.inc("ve", i)
            p.wait(T, "T", "ve", xbc)
            transposes(sXb, 8, sXT[:, :, :], "sXT")
            p.setguard("sXb", "te")
            p.wait(T, "T", "ve", p.cnt["ve"])
            p.wguard(T, "T", "pP")
            for nb_ in range(2):
                for c in range(8):
                    i = T.matmul(pP[:, nb_ * 512:(nb_ + 1) * 512], sXT[:, c, :],
                                 wWqL[:, c, nb_ * 512:(nb_ + 1) * 512],
                                 start=(c == 0), stop=(c == 7))
            st_["qmm"] = p.inc("te", i)
            p.wguard(T, "T", "pX")
            for c in range(8):
                i = T.matmul(pX[:, 0:LAT], sXT[:, c, :], wWkL[:, c, :],
                             start=(c == 0), stop=(c == 7))
            kmm = p.inc("te", i)
            p.wait(Sc, "Sc", "te", kmm)
            p.wguard(Sc, "Sc", "sKtm")
            i = Sc.mul(out=sKtm[:], in_=pX[:, 0:LAT], mul=rstd2)
            kevc = p.inc("se", i)
            st_["kevc"] = kevc
            p.setguard("pX", "se", kevc)

        def v2a(t):
            st_ = vs[t]
            p.wguard(T, "T", "pX")
            for c in range(8):
                i = T.matmul(pX[:, 0:LAT], sXT[:, c, :], wWvL[:, c, :],
                             start=(c == 0), stop=(c == 7))
            vmm = p.inc("te", i)
            p.setguard("sXT", "te", vmm)
            p.wait(Sc, "Sc", "te", vmm)
            p.wguard(Sc, "Sc", f"sVl{t % 3}")
            i = Sc.mul(
                out=sVl[:, t % 3, 0:264].rearrange("p (h k) -> p h k", k=66)[:, :, 0:64],
                in_=pX[:, 0:LAT].rearrange("p (h k) -> p h k", k=64),
                mul=rstd2)
            vlc = p.inc("se", i)
            st_["vlc"] = vlc
            p.setguard("pX", "se", vlc)
            p.wait(Sc, "Sc", "te", st_["qmm"])
            p.wguard(Sc, "Sc", "sQtm")
            i = Sc.mul(out=sQtm[:], in_=pP[:, :], mul=rstd2)
            qtmc = p.inc("se", i)
            p.setguard("pP", "se", qtmc)
            p.wait(V, "V", "se", qtmc)
            p.wguard(V, "V", "sSq")
            p.wguard(V, "V", "sQrope")
            rope(sQtm, sQrope, NH, t, sCosL, sSinL, sSq)
            p.setguard("sQtm", "ve")

        def v2bV(t):
            st_ = vs[t]
            p.wguard(V, "V", "sQb")
            st_["qbc"] = qknorm(sQrope, sQb, NH, ssq16, rq16, sSq)
            p.wguard(V, "V", "sKbf")
            p.wait(V, "V", "se", st_["kevc"])
            rope(sKtm, sQrope[:, 0:LAT], KVH, t, sCosL, sSinL, sSq)
            p.setguard("sKtm", "ve")
            st_["kbfc"] = qknorm(sQrope[:, 0:LAT], sKbf, KVH, ssk4, rk4, sSq)
            p.setguard("sQrope", "ve")
            p.setguard("sSq", "ve")

        def v2bT(t):
            st_ = vs[t]
            qbc = st_["qbc"]
            kbfc = st_["kbfc"]
            p.wait(T, "T", "ve", qbc)
            p.wguard(T, "T", "pT")
            for c in range(8):
                i = T.transpose(pT[:, c * 128:(c + 1) * 128],
                                sQb[:, c * 128:(c + 1) * 128], sIdent[:])
            tc = p.inc("te", i)
            p.setguard("sQb", "te", tc)
            p.wait(Sc, "Sc", "te", tc)
            p.wguard(Sc, "Sc", "sQTl")
            i = Sc.copy(out=sQTl[:, :, :],
                        in_=pT.rearrange("p (c n) -> p c n", n=128))
            st_["qtl"] = p.inc("se", i)
            p.setguard("pT", "se", st_["qtl"])
            p.wait(T, "T", "ve", kbfc)
            p.wguard(T, "T", "pT")
            i = T.transpose(pT[:, 0:128], sKbf[:, 0:128], sIdent[:])
            i = T.transpose(pT[:, 128:256], sKbf[:, 128:256], sIdent[:])
            tc = p.inc("te", i)
            p.setguard("sKbf", "te", tc)
            p.wait(Sc, "Sc", "te", tc)
            p.wguard(Sc, "Sc", f"sKTl{t % 3}")
            i = Sc.copy(out=sKTl[:, t % 3, :], in_=pT[:, 0:256])
            st_["klc"] = p.inc("se", i)
            p.setguard("pT", "se", st_["klc"])

        def v3(t, grange):
            st_ = vs[t]
            slots = [s_ for s_ in (t - 2, t - 1, t) if s_ >= 0]
            p.wait(T, "T", "se", max(st_["klc"], st_["qtl"], st_["vlc"]))
            for g in grange:
                kv = g
                pvp = g % 2
                pv = pVb[pvp]
                lpar = g % 2
                p.wguard(T, "T", f"pV{pvp}")

                def emit_pv_l(pend, final=False):
                    si_, s_, sp_, ec_ = pend
                    p.wait(T, "T", "se", ec_)
                    if final:
                        for _ in range(3):
                            T.ldweights(sIdent[:])
                    for hh in range(4):
                        i = T.matmul(pv[:, hh * 66:hh * 66 + 65],
                                     sPT[:, sp_, hh * 128:(hh + 1) * 128],
                                     sVl[:, s_ % 3, kv * 66:kv * 66 + 65],
                                     start=(si_ == 0 and hh == 0),
                                     stop=(si_ == len(slots) - 1))
                    pvc = p.inc("te", i)
                    p.setguard(f"sPT{sp_}", "te", pvc)

                pend_q = []
                for si, s_ in enumerate(slots):
                    sp = scnt[0] % 3
                    scnt[0] += 1
                    p.wguard(T, "T", f"pS{lpar}")
                    mask = sMaskD if s_ == t else (sMaskL if s_ == t - 2 else None)
                    for hh in range(4):
                        h_ = 4 * g + hh
                        qs = QSLOT[h_]
                        i = T.matmul(pSb[lpar][:, hh * 128:(hh + 1) * 128],
                                     sKTl[(kv % 2) * 64:(kv % 2) * 64 + 64, s_ % 3,
                                          (kv // 2) * 128:(kv // 2) * 128 + 128],
                                     sQTl[(qs % 2) * 64:(qs % 2) * 64 + 64, qs // 2, :],
                                     start=(hh == 0),
                                     stop=(mask is None and hh == 3))
                    if mask is not None:
                        i = T.matmul(pSb[lpar][:, 0:512], mask[:], sId4[:],
                                     start=False, stop=True)
                    smm = p.inc("te", i)
                    p.wait(Sc, "Sc", "te", smm)
                    p.wguard(Sc, "Sc", f"sPT{sp}")
                    i = Sc.activation(out=sPT[:, sp, 0:512], in_=pSb[lpar][:, 0:512],
                                      func=AF.Exp, scale=0.125)
                    ec = p.inc("se", i)
                    p.setguard(f"pS{lpar}", "se", ec)
                    pend_q.append((si, s_, sp, ec))
                    if len(pend_q) > 2:
                        emit_pv_l(pend_q.pop(0))
                    lpar ^= 1
                while pend_q:
                    emit_pv_l(pend_q.pop(0), final=(len(pend_q) == 0))
                p.wguard(V, "V", f"sAttnL{t % 2}")
                p.wguard(V, "V", "sAttnL_dma")
                attn_evac(pv, pvp,
                          sAttnL[:, t % 2, g * 256:(g + 1) * 256].rearrange(
                              "p (s k) -> p s k", k=64))
            if grange[-1] == 3:
                for s_ in slots:
                    p.setguard(f"sVl{s_ % 3}", "te")
                    p.setguard(f"sKTl{s_ % 3}", "te")
                p.setguard("sQTl", "te")
                st_["alve"] = p.cnt["ve"]

        def v4(t):
            st_ = vs.pop(t)
            slot = st_["slot"]
            x1a = sX1[:, t % 2, :]
            if _DBG_FULL:
                p.wait(Sy, "Sy", "ve")
                p.inc("st", Sy.dma_start(out=dbgal_d[t * 128:(t + 1) * 128, :],
                                         in_=sAttnL[:, t % 2, :]), 16)
                p.setguard("sAttnL_dma", "st")
            p.wait(T, "T", "ve", st_["alve"])
            p.wguard(T, "T", "pT")
            for c in range(8):
                i = T.transpose(pT[:, c * 128:(c + 1) * 128],
                                sAttnL[:, t % 2, c * 128:(c + 1) * 128], sIdent[:])
            tc = p.inc("te", i)
            p.setguard(f"sAttnL{t % 2}", "te", tc)
            p.wait(Sc, "Sc", "te", tc)
            p.wguard(Sc, "Sc", "sATT")
            i = Sc.copy(out=sATT[:, :, :],
                        in_=pT.rearrange("p (c n) -> p c n", n=128))
            attc = p.inc("se", i)
            p.setguard("pT", "se", attc)
            p.wait(T, "T", "se", attc)
            p.wguard(T, "T", "pP")
            for nb_ in range(2):
                for c in range(8):
                    i = T.matmul(pP[:, nb_ * 512:(nb_ + 1) * 512], sATT[:, c, :],
                                 wWoL[:, c, nb_ * 512:(nb_ + 1) * 512],
                                 start=(c == 0), stop=(c == 7))
            woc = p.inc("te", i)
            p.setguard("sATT", "te", woc)
            p.wait(V, "V", "te", woc)
            x2a = sXblk[:, slot, :]
            i = V.tensor_add(out=x1a, in0=pP[:, :], in1=x1a)
            x2c = p.inc("ve", i)
            p.setguard("pP", "ve", x2c)
            i = V.tensor_add(out=x2a, in0=x1a, in1=x2a)
            x2c = p.inc("ve", i)
            p.setguard(f"sX1_{t % 2}", "ve", x2c)
            p.wguard(V, "V", "sQtm")
            ln_apply(x2a, sQtm[:, :])
            y1c = p.cnt["ve"]
            p.setguard(f"xblkA{slot}", "ve", y1c)
            p.wait(G, "G", "ve", y1c)
            p.inc("gst", G.dma_start(out=y1_d[t * 128:(t + 1) * 128, :],
                                     in_=sQtm[:, :]), 16)
            p.setguard("sQtm", "gst")
            if t + 2 < NT:
                load_x(t + 2, (t + 2) % 2)

        v1(4 * b)
        v2a(4 * b)
        v2bV(4 * b)
        for ti in range(4):
            t = 4 * b + ti
            if ti >= 1:
                v4(t - 1)
            v2bT(t)
            v3(t, (0, 1))
            if ti < 3:
                v1(t + 1)
            v3(t, (2, 3))
            if ti < 3:
                v2a(t + 1)
                v2bV(t + 1)
        pending_v4[0] = (lambda f=v4, t_=4 * b + 3: f(t_))

    pending_v4[0]()
    pending_v4[0] = None
    ab_te, ab_ve, ab_se = p.cnt["te"], p.cnt["ve"], p.cnt["se"]
    stAB.close()

    wW1 = sb("wW1", [128, 8, 4 * D], BF16)
    wW2 = sb("wW2", [128, 32, D], BF16)
    sB1r = sb("sB1r", [1, 4 * D], BF16)
    sB2r = sb("sB2r", [1, D], BF16)
    sY1c = sb("sY1c", [128, 2, D], F32)
    sY1b = sb("sY1b", [128, 2, D], BF16)
    sY1T = sb("sY1T", [128, 2, 8, 128], BF16)
    sHtm = sb("sHtm", [128, 4 * D], BF16)
    sHT = sb("sHT", [128, 32, 128], BF16)
    sX3 = sb("sX3", [128, D], F32)
    sOut = sb("sOut", [128, 2, D], F32)

    p.wait(G, "G", "te", ab_te)
    p.wait(G, "G", "ve", ab_ve)
    p.wait(G, "G", "se", ab_se)
    p.inc("ldw", G.dma_start(out=sB1r[:], in_=b1_d[:, :]), 16)
    p.inc("ldw", G.dma_start(out=sB2r[:], in_=b2_d[:, :]), 16)
    w12_ldw = p.cnt["ldw"]
    w1p, w2p = [], []
    for w in range(4):
        w1p.append(("lw%d" % w, p.inc("lw%d" % w, G.dma_start(
            out=wW1[:, :, w * 1024:(w + 1) * 1024],
            in_=w1_d[:, w * 1024:(w + 1) * 1024].rearrange("(c p) n -> p c n", p=128)), 16)))
    for w in range(4):
        w2p.append(("lw%d" % (4 + w), p.inc("lw%d" % (4 + w), G.dma_start(
            out=wW2[:, w * 8:(w + 1) * 8, :],
            in_=w2_d[w * 1024:(w + 1) * 1024, :].rearrange("(c p) n -> p c n", p=128)), 16)))

    y1load = {}
    p.wait(G, "G", "gst", 16 * NT)

    def load_y1(t):
        p.wguard(G, "G", f"sY1c{t % 2}")
        sem = f"ly{t % 3}"
        y1load[t] = (sem, p.inc(sem, G.dma_start(out=sY1c[:, t % 2, :],
                                                 in_=y1_d[t * 128:(t + 1) * 128, :]), 16))

    load_y1(0)
    if NT > 1:
        load_y1(1)
    prep_ve = {}

    def prep_y1(t):
        ya_ = sY1c[:, t % 2, :]
        p.wait(V, "V", y1load[t][0], y1load[t][1])
        p.wguard(V, "V", f"sY1b{t % 2}")
        i = V.tensor_copy(out=sY1b[:, t % 2, :], in_=ya_)
        ybc = p.inc("ve", i)
        p.wait(T, "T", "ve", ybc)
        p.wait(T, "T", "ldw", w12_ldw)
        p.wguard(T, "T", "pT")
        for c in range(8):
            i = T.transpose(pT[:, c * 128:(c + 1) * 128],
                            sY1b[:, t % 2, c * 128:(c + 1) * 128], sIdent[:])
        tc = p.inc("te", i)
        p.setguard(f"sY1b{t % 2}", "te", tc)
        p.wait(V, "V", "te", tc)
        p.wguard(V, "V", f"sY1T{t % 2}")
        i = V.tensor_copy(out=sY1T[:, t % 2, :, :],
                          in_=pT.rearrange("p (c n) -> p c n", n=128))
        p.setguard("pT", "ve", p.inc("ve", i))
        prep_ve[t] = p.cnt["ve"]

    prep_y1(0)
    for t in range(NT):
        ya = sY1c[:, t % 2, :]
        yT = sY1T[:, t % 2, :, :]
        p.wait(T, "T", "ve", prep_ve[t])
        hpair = [(pP[:, 0:512], pP[:, 512:1024], "pP"),
                 (pV0[:, 0:512], pV1[:, 0:512], "pVh")]
        hc_w = {}

        def emit_h(w):
            pa, pb, gname = hpair[w % 2]
            p.wait(T, "T", w1p[w][0], w1p[w][1])
            p.wguard(T, "T", gname)
            for nb_, dst in ((0, pa), (1, pb)):
                hcol = w * 1024 + nb_ * 512
                for c in range(8):
                    T.matmul(dst, yT[:, c, :], wW1[:, c, hcol:hcol + 512],
                             start=(c == 0), stop=False)
                i = T.matmul(dst, sOnes[0:1, :], sB1r[0:1, hcol:hcol + 512],
                             start=False, stop=True)
            hmm = p.inc("te", i)
            p.wait(Sc, "Sc", "te", hmm)
            p.wguard(Sc, "Sc", "sHtm")
            Sc.activation(out=sHtm[:, w * 1024:w * 1024 + 512], in_=pa, func=AF.Relu)
            i = Sc.activation(out=sHtm[:, w * 1024 + 512:(w + 1) * 1024], in_=pb,
                              func=AF.Relu)
            hc = p.inc("se", i)
            p.setguard(gname, "se", hc)
            hc_w[w] = hc

        ht_c = {}

        def emit_tailA(w):
            p.wait(T, "T", "se", hc_w[w])
            p.wguard(T, "T", "pT")
            for c in range(8):
                i = T.transpose(pT[:, c * 128:(c + 1) * 128],
                                sHtm[:, w * 1024 + c * 128:w * 1024 + (c + 1) * 128],
                                sIdent[:])
            tc = p.inc("te", i)
            p.wait(V, "V", "te", tc)
            p.wguard(V, "V", "sHT")
            i = V.tensor_copy(out=sHT[:, w * 8:(w + 1) * 8, :],
                              in_=pT.rearrange("p (c n) -> p c n", n=128))
            htc = p.inc("ve", i)
            p.setguard("pT", "ve", htc)
            ht_c[w] = htc

        def emit_tailB(w):
            p.wait(T, "T", "ve", ht_c[w])
            p.wait(T, "T", w2p[w][0], w2p[w][1])
            if w == 0:
                p.wguard(T, "T", "pSf")
            for c in range(8):
                cc = w * 8 + c
                T.matmul(pS0[:, :], sHT[:, cc, :], wW2[:, cc, 0:512],
                         start=(cc == 0), stop=False)
                i = T.matmul(pS1[:, :], sHT[:, cc, :], wW2[:, cc, 512:1024],
                             start=(cc == 0), stop=False)
            p.inc("te", i)

        for w in range(4):
            emit_h(w)
            if w == 1 and t + 1 < NT:
                prep_y1(t + 1)
            if w >= 1:
                emit_tailA(w - 1)
            if w >= 2:
                emit_tailB(w - 2)
        emit_tailA(3)
        emit_tailB(2)
        emit_tailB(3)
        p.setguard("sHtm", "te")
        p.setguard("sHT", "te")
        p.setguard(f"sY1T{t % 2}", "te")
        T.matmul(pS0[:, :], sOnes[0:1, :], sB2r[0:1, 0:512], start=False, stop=True)
        i = T.matmul(pS1[:, :], sOnes[0:1, :], sB2r[0:1, 512:1024], start=False, stop=True)
        fmm = p.inc("te", i)
        p.wait(V, "V", "te", fmm)
        p.wguard(V, "V", "sX3")
        p.wguard(V, "V", "sX3_dma")
        V.tensor_add(out=sX3[:, 0:512], in0=pS0[:, :], in1=ya[:, 0:512])
        i = V.tensor_add(out=sX3[:, 512:1024], in0=pS1[:, :], in1=ya[:, 512:1024])
        x3c = p.inc("ve", i)
        p.setguard("pSf", "ve", x3c)
        p.setguard(f"sY1c{t % 2}", "ve", x3c)
        if t + 2 < NT:
            load_y1(t + 2)
        if _DBG_FULL:
            p.wait(Sy, "Sy", "ve")
            p.inc("st", Sy.dma_start(out=dbgx3_d[t * 128:(t + 1) * 128, :],
                                     in_=sX3[:]), 16)
            p.setguard("sX3_dma", "st")
        p.wguard(V, "V", f"sOut{t % 2}")
        ln_apply(sX3, sOut[:, t % 2, :])
        oc = p.cnt["ve"]
        p.setguard("sX3", "ve", oc)
        p.wait(Sy, "Sy", "ve", oc)
        p.inc(f"st{t % 2}", Sy.dma_start(out=out_d[t * 128:(t + 1) * 128, :],
                                         in_=sOut[:, t % 2, :]), 16)
        p.setguard(f"sOut{t % 2}", f"st{t % 2}")

    if _DBG_FULL:
        p.wait(Sy, "Sy", "gst")
        p.inc("st", Sy.dma_start(out=dbgy1_d[:, :], in_=y1_d[:, :]), 16)
        Sy.wait_ge(p.sem["st"], p.cnt["st"])
    Sy.wait_ge(p.sem["st0"], p.cnt["st0"])
    Sy.wait_ge(p.sem["st1"], p.cnt["st1"])
    st.close()
    return nc


_EO = np.concatenate([np.arange(0, 64, 2), np.arange(1, 64, 2)])


def _perm_wq(w):
    w = np.asarray(w)
    cols = np.concatenate([h * 64 + _EO for h in QORDER])
    return np.ascontiguousarray(w[:, cols])


def _perm_eo(w, nheads):
    w = np.asarray(w)
    cols = np.concatenate([h * 64 + _EO for h in range(nheads)])
    return np.ascontiguousarray(w[:, cols])


def _prep_inputs(S, inputs):
    c = _host_consts(S)
    common = {
        "g_wq": _perm_wq(inputs["L0_attn_g_wq"]), "g_kv_down": inputs["L0_attn_g_kv_down"],
        "g_k_up": _perm_eo(inputs["L0_attn_g_k_up"], KVH),
        "g_v_up": inputs["L0_attn_g_v_up"],
        "g_wo": inputs["L0_attn_g_wo"], "l_wq": _perm_wq(inputs["L0_attn_l_wq"]),
        "l_wk": _perm_eo(inputs["L0_attn_l_wk"], KVH), "l_wv": inputs["L0_attn_l_wv"],
        "l_wo": inputs["L0_attn_l_wo"], "w1": inputs["L0_ffn_w1"],
        "w2": inputs["L0_ffn_w2"],
        "b1r": np.asarray(inputs["L0_ffn_b1"]).reshape(1, -1),
        "b2r": np.asarray(inputs["L0_ffn_b2"]).reshape(1, -1),
        "n1pc": np.asarray(inputs["L0_attn_norm1_w"]).reshape(8, 128).T,
        "n2pc": np.asarray(inputs["L0_attn_norm2_w"]).reshape(8, 128).T,
    }
    common.update(c)
    return {k: np.ascontiguousarray(v) for k, v in common.items()}


def kernel(**inputs):
    x = np.asarray(inputs["L0_attn_hidden"])
    B, S, _ = x.shape
    common = _prep_inputs(S, inputs)
    nc = build_program(S)
    in_maps = []
    for b in range(B):
        m = dict(common)
        m["x"] = np.ascontiguousarray(x[b])
        in_maps.append(m)
    res = run_bass_kernel_spmd(nc, in_maps, core_ids=list(range(B)))
    out = np.stack([np.asarray(res.results[b]["out"]) for b in range(B)], axis=0)
    return out.astype(np.float32)


# version 25
# speedup vs baseline: 1.1924x; 1.0015x over previous
import contextlib
import numpy as np
import ml_dtypes
import concourse.bass as bass
import concourse.mybir as mybir
from concourse.bass_utils import run_bass_kernel_spmd
from concourse.ap import AP as _AP


def _ap(x):
    return x if isinstance(x, _AP) else x[:]

F32 = mybir.dt.float32
BF16 = mybir.dt.bfloat16
AF = mybir.ActivationFunctionType
ALU = mybir.AluOpType
AX = mybir.AxisListType

D = 1024
NH = 16
KVH = 4
HD = 64
LAT = 256
NEG = -1.0e6
BF = ml_dtypes.bfloat16
QORDER = [0, 4, 1, 5, 2, 6, 3, 7, 8, 12, 9, 13, 10, 14, 11, 15]
QSLOT = {h: s for s, h in enumerate(QORDER)}
_DBG = None
_DBG_TILE = 2
_DBG_FULL = False


def _rope_tables(S, base):
    exps = np.arange(0, HD, 2).astype(np.float32) / np.float32(HD)
    invf = (np.float32(1.0) / np.power(np.float32(base), exps)).astype(np.float32)
    t = np.arange(S, dtype=np.float32)
    freqs = (t[:, None] * invf[None, :]).astype(np.float32)
    cos = np.cos(freqs.astype(np.float64)).astype(np.float32)
    sin = np.sin(freqs.astype(np.float64)).astype(np.float32)
    nt = S // 128
    return (np.ascontiguousarray(cos.reshape(nt, 128, 32).transpose(1, 0, 2)),
            np.ascontiguousarray(sin.reshape(nt, 128, 32).transpose(1, 0, 2)))


def _host_consts(S):
    p = np.arange(128)
    maskd = np.where(p[:, None] <= p[None, :], 0.0, NEG).astype(np.float32)
    maskl = np.where(p[:, None] > p[None, :], 0.0, NEG).astype(np.float32)
    cg, sg = _rope_tables(S, 1000000.0)
    cl, sl = _rope_tables(S, 10000.0)
    eye = np.eye(128, dtype=BF)
    return {
        "identB": eye,
        "ident4": np.ascontiguousarray(np.tile(eye, (1, 4))),
        "onesB": np.ones((1, 128), dtype=BF),
        "maskdT": np.ascontiguousarray(maskd.T).astype(BF),
        "masklT": np.ascontiguousarray(maskl.T).astype(BF),
        "cosg": cg.astype(BF), "sing": sg.astype(BF),
        "cosl": cl.astype(BF), "sinl": sl.astype(BF),
    }


class P:

    def __init__(self):
        self.sem = {}
        self.cnt = {}
        self.waited = {}
        self.guard = {}

    def add_sem(self, name, handle):
        self.sem[name] = handle
        self.cnt[name] = 0

    def inc(self, name, inst, n=1):
        inst.then_inc(self.sem[name], n)
        self.cnt[name] += n
        return self.cnt[name]

    def wait(self, eng, ename, sname, val=None):
        val = self.cnt[sname] if val is None else val
        if val <= 0:
            return
        k = (ename, sname)
        if self.waited.get(k, 0) < val:
            eng.wait_ge(self.sem[sname], val)
            self.waited[k] = val

    def wguard(self, eng, ename, rname):
        g = self.guard.get(rname)
        if g is not None:
            self.wait(eng, ename, g[0], g[1])

    def setguard(self, rname, sname, val=None):
        v = self.cnt[sname] if val is None else val
        old = self.guard.get(rname)
        if old is not None and old[0] == sname and old[1] >= v:
            return
        self.guard[rname] = (sname, v)


def build_program(S):
    NT = S // 128
    NB = S // 512
    nc = bass.Bass("TRN2", target_bir_lowering=False)

    def din(name, shape, dt=F32):
        return nc.dram_tensor(name, list(shape), dt, kind="ExternalInput")

    x_d = din("x", [S, D])
    gwq_d = din("g_wq", [D, D]); gkvd_d = din("g_kv_down", [D, LAT])
    gkup_d = din("g_k_up", [LAT, LAT]); gvup_d = din("g_v_up", [LAT, LAT])
    gwo_d = din("g_wo", [D, D])
    lwq_d = din("l_wq", [D, D]); lwk_d = din("l_wk", [D, LAT])
    lwv_d = din("l_wv", [D, LAT]); lwo_d = din("l_wo", [D, D])
    w1_d = din("w1", [D, 4 * D]); w2_d = din("w2", [4 * D, D])
    b1_d = din("b1r", [1, 4 * D]); b2_d = din("b2r", [1, D])
    n1_d = din("n1pc", [128, 8]); n2_d = din("n2pc", [128, 8])
    idb_d = din("identB", [128, 128], BF16)
    id4_d = din("ident4", [128, 512], BF16)
    ones_d = din("onesB", [1, 128], BF16)
    maskd_d = din("maskdT", [128, 128], BF16)
    maskl_d = din("masklT", [128, 128], BF16)
    cosg_d = din("cosg", [128, NT, 32], BF16); sing_d = din("sing", [128, NT, 32], BF16)
    cosl_d = din("cosl", [128, NT, 32], BF16); sinl_d = din("sinl", [128, NT, 32], BF16)

    y1_d = nc.dram_tensor("y1scratch", [S, D], F32, kind="Internal")
    out_d = nc.dram_tensor("out", [S, D], F32, kind="ExternalOutput")
    if _DBG_FULL:
        dbgag_d = nc.dram_tensor("dbg_ag", [S, D], BF16, kind="ExternalOutput")
        dbgal_d = nc.dram_tensor("dbg_al", [S, D], BF16, kind="ExternalOutput")
        dbgy1_d = nc.dram_tensor("dbg_y1f", [S, D], F32, kind="ExternalOutput")
        dbgx3_d = nc.dram_tensor("dbg_x3f", [S, D], F32, kind="ExternalOutput")
        dbgy1b_d = nc.dram_tensor("dbg_y1b", [S, D], BF16, kind="ExternalOutput")

    p = P()
    st = contextlib.ExitStack()
    sb = lambda name, shape, dt: st.enter_context(nc.sbuf_tensor(name, list(shape), dt))
    ps = lambda name, shape, dt: st.enter_context(nc.psum_tensor(name, list(shape), dt))
    for s in ("ldx", "ldw", "gst", "st", "te", "ve", "se",
              "lx0", "lx1", "lx2", "lx3", "lx4", "lx5",
              "ly0", "ly1", "ly2", "st0", "st1",
              "lw0", "lw1", "lw2", "lw3", "lw4", "lw5", "lw6", "lw7"):
        p.add_sem(s, st.enter_context(nc.semaphore(s)))

    T, V, Sc, Sy, G = nc.tensor, nc.vector, nc.scalar, nc.sync, nc.gpsimd

    sIdent = sb("sIdent", [128, 128], BF16)
    sId4 = sb("sId4", [128, 512], BF16)
    sOnes = sb("sOnes", [1, 128], BF16)
    sMaskD = sb("sMaskD", [128, 128], BF16)
    sMaskL = sb("sMaskL", [128, 128], BF16)
    sCosG = sb("sCosG", [128, NT, 32], BF16)
    sSinG = sb("sSinG", [128, NT, 32], BF16)
    sCosL = sb("sCosL", [128, NT, 32], BF16)
    sSinL = sb("sSinL", [128, NT, 32], BF16)
    sN1 = sb("sN1", [128, 8], F32)
    sN2 = sb("sN2", [128, 8], F32)
    sEps = sb("sEps", [128, 4], F32)
    sStats = sb("sStats", [128, 128], F32)
    sDen = sb("sDen", [128, 16], F32)
    sBn = sb("sBn", [128, 16], F32)
    sMv = sb("sMv", [128, 64], F32)

    pT = ps("pT", [128, 1024], BF16)
    pSd = ps("pSd", [128, 1024], F32)
    pS0 = pSd[:, 0:512]
    pS1 = pSd[:, 512:1024]
    pP = ps("pP", [128, 1024], F32)
    pV0 = ps("pV0", [128, 512], F32)
    pV1 = ps("pV1", [128, 512], F32)
    pX = ps("pX", [128, 512], F32)
    pSb = [pS0, pS1]
    pVb = [pV0, pV1]

    EPS_QK_COL, EPS_LN_COL = 0, 1
    V.memset(sEps[:, 0:1], 1e-6)
    V.memset(sEps[:, 1:2], 1e-5)

    for dst, src in ((sIdent, idb_d), (sId4, id4_d), (sOnes, ones_d),
                     (sMaskD, maskd_d), (sMaskL, maskl_d),
                     (sCosG, cosg_d), (sSinG, sing_d),
                     (sCosL, cosl_d), (sSinL, sinl_d), (sN1, n1_d), (sN2, n2_d)):
        p.inc("ldx", Sy.dma_start(out=dst[:], in_=src[:]), 16)
    consts_ldx = p.cnt["ldx"]

    rstd1w = sStats[:, 0:16]
    rstd1 = sStats[:, 0:1]
    rstd2w = sStats[:, 16:32]
    rstd2 = sStats[:, 16:17]
    saccw = sStats[:, 32:48]
    sacc = sStats[:, 32:33]
    stmpw = sStats[:, 48:64]
    ssq16 = sStats[:, 64:80]
    rq16 = sStats[:, 80:96]
    ssk4 = sStats[:, 96:112]
    rk4 = sStats[:, 112:128]

    def rmsnorm_stats(src_ap, dst_rstd_w, sq_buf):
        src_ap = _ap(src_ap)
        p.wguard(Sc, "Sc", "sSq")
        i = Sc.activation(out=sq_buf[:, 0:D], in_=src_ap, func=AF.Square,
                          accum_out=sacc)
        c = p.inc("se", i)
        p.wait(Sc, "Sc", "se", c)
        i = Sc.copy(out=saccw[:, :], in_=sacc.broadcast_to([128, 16]))
        i = Sc.activation(out=stmpw, in_=saccw, func=AF.Ln,
                          bias=sEps[:, EPS_QK_COL:EPS_QK_COL + 1], scale=1.0 / D)
        i = Sc.activation(out=dst_rstd_w, in_=stmpw, func=AF.Exp, scale=-0.5)
        p.inc("se", i)
        p.setguard("sSq", "se")

    def transposes(src, chunks, dst3, dst_guard):
        src = _ap(src)
        p.wguard(T, "T", "pT")
        for c in range(chunks):
            i = T.transpose(pT[:, c * 128:(c + 1) * 128],
                            src[:, c * 128:(c + 1) * 128], sIdent[:])
        tc = p.inc("te", i)
        p.wait(V, "V", "te", tc)
        p.wguard(V, "V", dst_guard)
        i = V.tensor_copy(out=dst3,
                          in_=pT[:, 0:chunks * 128].rearrange("p (c n) -> p c n", n=128))
        c2 = p.inc("ve", i)
        p.setguard("pT", "ve", c2)
        return c2

    def rope(srcf, dstf, nh, t, costab, sintab, sq_buf):
        srcf, dstf = _ap(srcf), _ap(dstf)
        src4 = srcf.rearrange("p (h w m) -> p h w m", m=32, w=2)
        dst4 = dstf.rearrange("p (h w m) -> p h w m", m=32, w=2)
        e, o = src4[:, :, 0, :], src4[:, :, 1, :]
        eo, oo = dst4[:, :, 0, :], dst4[:, :, 1, :]
        cosb = costab[:, t, :].unsqueeze(1).broadcast_to([128, nh, 32])
        sinb = sintab[:, t, :].unsqueeze(1).broadcast_to([128, nh, 32])
        n = nh * 32
        tA = sq_buf[:, 0:n].rearrange("p (h m) -> p h m", m=32)
        tB = sq_buf[:, n:2 * n].rearrange("p (h m) -> p h m", m=32)
        V.tensor_tensor(out=tA, in0=e, in1=cosb, op=ALU.mult)
        V.tensor_tensor(out=tB, in0=o, in1=sinb, op=ALU.mult)
        V.tensor_tensor(out=eo, in0=tA, in1=tB, op=ALU.subtract)
        V.tensor_tensor(out=tA, in0=e, in1=sinb, op=ALU.mult)
        V.tensor_tensor(out=tB, in0=o, in1=cosb, op=ALU.mult)
        i = V.tensor_tensor(out=oo, in0=tA, in1=tB, op=ALU.add)
        return p.inc("ve", i)

    def qknorm(srcf, dstb, nh, ssq, rr, sq_buf):
        srcf, dstb = _ap(srcf), _ap(dstb)
        n = nh * HD
        i = V.tensor_tensor(out=sq_buf[:, 0:n], in0=srcf, in1=srcf, op=ALU.mult)
        p.inc("ve", i)
        i = V.reduce_sum(out=ssq[:, 0:nh],
                         in_=sq_buf[:, 0:n].rearrange("p (h d) -> p h d", d=HD),
                         axis=AX.X)
        c = p.inc("ve", i)
        p.wait(V, "V", "ve", c)
        p.wait(Sc, "Sc", "ve", c)
        i = Sc.activation(out=ssq[:, 0:16], in_=ssq[:, 0:16], func=AF.Ln,
                          bias=sEps[:, EPS_QK_COL:EPS_QK_COL + 1], scale=1.0 / HD)
        i = Sc.activation(out=rr[:, 0:16], in_=ssq[:, 0:16], func=AF.Exp, scale=-0.5)
        c = p.inc("se", i)
        p.wait(V, "V", "se", c)
        rb = rr[:, 0:nh].unsqueeze(2).broadcast_to([128, nh, HD])
        i = V.tensor_tensor(out=dstb.rearrange("p (h d) -> p h d", d=HD),
                            in0=srcf.rearrange("p (h d) -> p h d", d=HD),
                            in1=rb, op=ALU.mult)
        return p.inc("ve", i)

    def ln_apply(srcf, dstf):
        srcf, dstf = _ap(srcf), _ap(dstf)
        i = V.bn_stats(out=sBn[:, 0:6], in_=srcf[:, 0:512])
        i = V.bn_stats(out=sBn[:, 6:12], in_=srcf[:, 512:1024])
        c = p.inc("ve", i)
        p.wait(V, "V", "ve", c)
        i = V.bn_aggr(out=sMv[:, 0:2], in_=sBn[:, 0:12])
        c = p.inc("ve", i)
        p.wait(V, "V", "ve", c)
        i = V.tensor_copy(out=sMv[:, 16:32], in_=sMv[:, 1:2].broadcast_to([128, 16]))
        c = p.inc("ve", i)
        p.wait(V, "V", "ve", c)
        p.wait(Sc, "Sc", "ve", c)
        i = Sc.activation(out=sMv[:, 32:48], in_=sMv[:, 16:32], func=AF.Ln,
                          bias=sEps[:, EPS_LN_COL:EPS_LN_COL + 1], scale=1.0)
        i = Sc.activation(out=sMv[:, 48:64], in_=sMv[:, 32:48], func=AF.Exp,
                          scale=-0.5)
        c = p.inc("se", i)
        p.wait(V, "V", "se", c)
        i = V.scalar_tensor_tensor(out=dstf, in0=srcf, scalar=sMv[:, 0:1],
                                   in1=sMv[:, 48:49].broadcast_to([128, D]),
                                   op0=ALU.subtract, op1=ALU.mult)
        return p.inc("ve", i)

    def attn_evac(pv, pvp, out3):
        p.wait(V, "V", "te", p.cnt["te"])
        i = V.tensor_copy(out=sDen[:, 0:4],
                          in_=pv[:, 0:264].rearrange("p (s k) -> p s k", k=66)[:, :, 64:65])
        c = p.inc("ve", i)
        p.wait(V, "V", "ve", c)
        i = V.reciprocal(out=sDen[:, 0:4], in_=sDen[:, 0:4])
        c = p.inc("ve", i)
        p.wait(V, "V", "ve", c)
        i = V.tensor_tensor(
            out=out3,
            in0=pv[:, 0:264].rearrange("p (s k) -> p s k", k=66)[:, :, 0:64],
            in1=sDen[:, 0:4].unsqueeze(2).broadcast_to([128, 4, 64]),
            op=ALU.mult)
        c = p.inc("ve", i)
        p.setguard(f"pV{pvp}", "ve", c)

    def dump(name, ap, t=None):
        if _DBG is None or (t is not None and t != _DBG_TILE):
            return
        ap = _ap(ap)
        d = nc.dram_tensor("dbg_" + name, list(ap.shape), ap.dtype,
                           kind="ExternalOutput")
        for sn in ("te", "ve", "se", "ldx", "ldw", "gst"):
            p.wait(Sy, "Sy", sn)
        p.inc("st", Sy.dma_start(out=d[:], in_=ap), 16)
        for eng, en in ((T, "T"), (V, "V"), (Sc, "Sc")):
            p.wait(eng, en, "st", p.cnt["st"])
        _DBG.append("dbg_" + name)

    stAB = contextlib.ExitStack()
    sbA = lambda name, shape, dt: stAB.enter_context(nc.sbuf_tensor(name, list(shape), dt))

    wWqG = sbA("wWqG", [128, 8, D], BF16)
    wKVd = sbA("wKVd", [128, 8, LAT], BF16)
    wKup = sbA("wKup", [128, 2, LAT], BF16)
    wVup = sbA("wVup", [128, 2, LAT], BF16)
    wWoG = sbA("wWoG", [128, 8, D], BF16)
    wWqL = sbA("wWqL", [128, 8, D], BF16)
    wWkL = sbA("wWkL", [128, 8, LAT], BF16)
    wWvL = sbA("wWvL", [128, 8, LAT], BF16)
    wWoL = sbA("wWoL", [128, 8, D], BF16)

    with nc.sbuf_tensor("wstageA", [128, 8, D], F32) as wsA, \
         nc.sbuf_tensor("wstageB", [128, 8, D], F32) as wsB:
        for dst, src in ((wKup, gkup_d), (wVup, gvup_d), (wWoG, gwo_d), (wWoL, lwo_d)):
            p.inc("ldw", G.dma_start(out=dst[:], in_=src.rearrange("(c p) n -> p c n", p=128)), 16)
        folds = ((gwq_d, wWqG, sN1, 8, D), (gkvd_d, wKVd, sN1, 8, LAT),
                 (lwq_d, wWqL, sN2, 8, D), (lwk_d, wWkL, sN2, 8, LAT),
                 (lwv_d, wWvL, sN2, 8, LAT))
        ws = [wsA, wsB]
        for fi, (src, dst, nw, nch, ncol) in enumerate(folds):
            buf = ws[fi % 2]
            p.wguard(G, "G", f"wstage{fi % 2}")
            lv = p.inc("ldw", G.dma_start(out=buf[:, 0:nch, 0:ncol],
                                          in_=src.rearrange("(c p) n -> p c n", p=128)), 16)
            p.wait(V, "V", "ldw", lv)
            p.wait(V, "V", "ldx", consts_ldx)
            for c in range(nch):
                i = V.tensor_scalar(out=dst[:, c, 0:ncol], in0=buf[:, c, 0:ncol],
                                    scalar1=nw[:, c:c + 1], scalar2=None, op0=ALU.mult)
            p.inc("ve", i)
            p.setguard(f"wstage{fi % 2}", "ve")
    weights_ldw = p.cnt["ldw"]
    setup_ve = p.cnt["ve"]

    sKTg0 = sbA("sKTg0", [128, S], BF16)
    sKTg1 = sbA("sKTg1", [128, S], BF16)
    sVg = sbA("sVg", [128, NT, 260], BF16)
    sXblk = sbA("sXblk", [128, 4, D], F32)
    sX1 = sbA("sX1", [128, 2, D], F32)
    sXb = sbA("sXb", [128, D], BF16)
    sXT = sbA("sXT", [128, 8, 128], BF16)
    sLat = sbA("sLat", [128, LAT], BF16)
    sLatT = sbA("sLatT", [128, 2, 128], BF16)
    sKtm = sbA("sKtm", [128, LAT], F32)
    sKbf = sbA("sKbf", [128, LAT], BF16)
    sQtm = sbA("sQtm", [128, D], F32)
    sQrope = sbA("sQrope", [128, D], F32)
    sQb = sbA("sQb", [128, D], BF16)
    sSq = sbA("sSq", [128, D], F32)
    sQTg = sbA("sQTg", [128, 2, 8, 512], BF16)
    sQTl = sbA("sQTl", [128, 8, 128], BF16)
    sKTl = sbA("sKTl", [128, 3, 256], BF16)
    sVl = sbA("sVl", [128, 3, 264], BF16)
    sPT = sbA("sPT", [128, 3, 1024], BF16)
    sAttn = sbA("sAttn", [128, 4, D], BF16)
    sAttnL = sbA("sAttnL", [128, 2, D], BF16)
    sATT = sbA("sATT", [128, 8, 128], BF16)

    for eng, en in ((V, "V"), (Sc, "Sc"), (T, "T"), (Sy, "Sy"), (G, "G")):
        p.wait(eng, en, "ve", setup_ve)
        p.wait(eng, en, "ldw", weights_ldw)
        p.wait(eng, en, "ldx", p.cnt["ldx"])
    p.inc("ve", V.memset(sVg[:], 1.0))
    p.inc("ve", V.memset(sVl[:], 1.0))

    xload_val = {}
    xissue = [0]
    scnt = [0]

    def load_x(t, slot):
        p.wguard(Sy, "Sy", f"xblkA{slot}")
        p.wguard(Sy, "Sy", f"xblkB{slot}")
        sem = f"lx{xissue[0] % 6}"
        xissue[0] += 1
        v = p.inc(sem, Sy.dma_start(out=sXblk[:, slot, :],
                                    in_=x_d[t * 128:(t + 1) * 128, :]), 16)
        xload_val[t] = (sem, v, slot)

    qready_ve = {}

    def part1_stages(t):
        b = t // 4
        ti = t % 4
        qp = b % 2
        st_ = {}

        def s0():
            sem, lv, slot = xload_val[t]
            xa = sXblk[:, slot, :]
            p.wait(V, "V", sem, lv)
            p.wait(Sc, "Sc", sem, lv)
            rmsnorm_stats(xa, rstd1w, sSq)
            p.setguard(f"xblkB{slot}", "se")
            p.wguard(V, "V", "sXb")
            i = V.tensor_copy(out=sXb[:], in_=xa)
            st_["xbc"] = p.inc("ve", i)
            p.setguard(f"xblkA{slot}", "ve", st_["xbc"])
            if ti < 2 and t + 2 < NT:
                load_x(t + 2, 2 + ((t + 2) % 2))

        def s1():
            p.wait(T, "T", "ve", st_["xbc"])
            transposes(sXb, 8, sXT[:, :, :], "sXT")
            p.setguard("sXb", "te")
            p.wait(T, "T", "ve", p.cnt["ve"])
            p.wguard(T, "T", "pX")
            for c in range(8):
                i = T.matmul(pX[:, 0:512], sXT[:, c, :], wWqG[:, c, 0:512],
                             start=(c == 0), stop=(c == 7))
            q0 = p.inc("te", i)
            p.wait(V, "V", "te", q0)
            p.wait(V, "V", "se", p.cnt["se"])
            p.wguard(V, "V", "sQtm")
            i = V.tensor_scalar(out=sQtm[:, 0:512], in0=pX[:, 0:512], scalar1=rstd1,
                                scalar2=None, op0=ALU.mult)
            p.setguard("pX", "ve", p.inc("ve", i))

        def s2():
            p.wguard(T, "T", "pX")
            for c in range(8):
                i = T.matmul(pX[:, 0:512], sXT[:, c, :], wWqG[:, c, 512:1024],
                             start=(c == 0), stop=(c == 7))
            q1 = p.inc("te", i)
            p.wait(V, "V", "te", q1)
            i = V.tensor_scalar(out=sQtm[:, 512:1024], in0=pX[:, 0:512], scalar1=rstd1,
                                scalar2=None, op0=ALU.mult)
            p.setguard("pX", "ve", p.inc("ve", i))

        def s3():
            p.wguard(T, "T", "pX")
            for c in range(8):
                i = T.matmul(pX[:, 0:LAT], sXT[:, c, :], wKVd[:, c, :],
                             start=(c == 0), stop=(c == 7))
            lmm = p.inc("te", i)
            p.setguard("sXT", "te", lmm)
            p.wait(V, "V", "te", lmm)
            p.wguard(V, "V", "sLat")
            i = V.tensor_scalar(out=sLat[:], in0=pX[:, 0:LAT], scalar1=rstd1,
                                scalar2=None, op0=ALU.mult)
            p.setguard("pX", "ve", p.inc("ve", i))
            p.wguard(V, "V", "sSq")
            p.wguard(V, "V", "sQrope")
            rope(sQtm, sQrope, NH, t, sCosG, sSinG, sSq)
            p.setguard("sQtm", "ve")

        def s4():
            p.wait(T, "T", "ve", p.cnt["ve"])
            transposes(sLat, 2, sLatT[:, :, :], "sLatT")
            p.setguard("sLat", "te")
            p.wait(T, "T", "ve", p.cnt["ve"])
            p.wguard(T, "T", "pX")
            for c in range(2):
                i = T.matmul(pX[:, 0:LAT], sLatT[:, c, :], wKup[:, c, :],
                             start=(c == 0), stop=(c == 1))
            kmm = p.inc("te", i)
            p.wait(Sc, "Sc", "te", kmm)
            p.wguard(Sc, "Sc", "sKtm")
            st_["kevc"] = p.inc("se", Sc.copy(out=sKtm[:], in_=pX[:, 0:LAT]))
            p.setguard("pX", "se", st_["kevc"])
            p.wguard(V, "V", "sQb")
            st_["qbc"] = qknorm(sQrope, sQb, NH, ssq16, rq16, sSq)

        def s5():
            p.wguard(T, "T", "pX")
            for c in range(2):
                i = T.matmul(pX[:, 0:LAT], sLatT[:, c, :], wVup[:, c, :],
                             start=(c == 0), stop=(c == 1))
            vmm = p.inc("te", i)
            p.setguard("sLatT", "te", vmm)
            p.wait(V, "V", "te", vmm)
            i = V.tensor_copy(
                out=sVg[:, t, 0:260].rearrange("p (h k) -> p h k", k=65)[:, :, 0:64],
                in_=pX[:, 0:LAT].rearrange("p (h k) -> p h k", k=64))
            p.setguard("pX", "ve", p.inc("ve", i))
            p.wait(V, "V", "se", st_["kevc"])
            p.wguard(V, "V", "sKbf")
            rope(sKtm, sQrope[:, 0:LAT], KVH, t, sCosG, sSinG, sSq)
            p.setguard("sKtm", "ve")
            st_["kbfc"] = qknorm(sQrope[:, 0:LAT], sKbf, KVH, ssk4, rk4, sSq)
            p.setguard("sQrope", "ve")
            p.setguard("sSq", "ve")

        def s6():
            p.wait(T, "T", "ve", st_["qbc"])
            p.wguard(T, "T", "pT")
            for c in range(8):
                i = T.transpose(pT[:, c * 128:(c + 1) * 128],
                                sQb[:, c * 128:(c + 1) * 128], sIdent[:])
            tc = p.inc("te", i)
            p.setguard("sQb", "te", tc)
            p.wait(V, "V", "te", tc)
            p.wguard(V, "V", f"sQTg{qp}")
            i = V.tensor_copy(out=sQTg[:, qp, :, ti * 128:(ti + 1) * 128],
                              in_=pT.rearrange("p (c n) -> p c n", n=128))
            p.setguard("pT", "ve", p.inc("ve", i))
            p.wait(T, "T", "ve", st_["kbfc"])
            p.wguard(T, "T", "pT")
            i = T.transpose(pT[:, 0:128], sKbf[:, 0:128], sIdent[:])
            i = T.transpose(pT[:, 128:256], sKbf[:, 128:256], sIdent[:])
            tc = p.inc("te", i)
            p.setguard("sKbf", "te", tc)
            p.wait(V, "V", "te", tc)
            V.tensor_copy(out=sKTg0[:, t * 128:(t + 1) * 128], in_=pT[:, 0:128])
            i = V.tensor_copy(out=sKTg1[:, t * 128:(t + 1) * 128], in_=pT[:, 128:256])
            p.setguard("pT", "ve", p.inc("ve", i))
            if ti == 3:
                qready_ve[b] = p.cnt["ve"]

        return [s0, s1, s2, s3, s4, s5, s6]

    load_x(0, 2)
    load_x(1, 3)
    for t in range(min(4, NT)):
        for f in part1_stages(t):
            f()

    pending_v4 = [None]
    for b in range(NB):
        qp = b % 2
        nj = 4 * b + 4
        par = 0
        p.wait(T, "T", "ve", qready_ve[b])
        stage_q = []
        if pending_v4[0] is not None:
            stage_q.append(pending_v4[0])
            pending_v4[0] = None
        if b + 1 < NB:
            load_x(4 * (b + 1), 2)
            load_x(4 * (b + 1) + 1, 3)
            for t_ in range(4 * (b + 1), 4 * (b + 1) + 4):
                stage_q.extend(part1_stages(t_))

        def drain_stage():
            if stage_q:
                stage_q.pop(0)()

        pBk = [pSd, pP]
        for hp, h0 in enumerate((0, 1, 2, 3, 8, 9, 10, 11)):
            h1 = h0 + 4
            kv0_, kv1_ = h0 // 4, h1 // 4
            kt = sKTg0 if h0 < 8 else sKTg1
            p.wguard(T, "T", "pV0")
            p.wguard(T, "T", "pV1")

            def emit_pv_g2(pend, final=False):
                jc_, ioff_, sp_, ec_ = pend
                p.wait(T, "T", "se", ec_)
                if final:
                    for _ in range(3):
                        T.ldweights(sIdent[:])
                for hx, pv, kv_ in ((0, pV0, kv0_), (1, pV1, kv1_)):
                    for sub in range(ioff_, 4):
                        i = T.matmul(pv[:, sub * 66:sub * 66 + 65],
                                     sPT[:, sp_, hx * 512 + (sub - ioff_) * 128:
                                         hx * 512 + (sub - ioff_ + 1) * 128],
                                     sVg[:, jc_, kv_ * 65:kv_ * 65 + 65],
                                     start=(jc_ == 0 and sub == 0),
                                     stop=(jc_ == 4 * b + sub))
                pvc = p.inc("te", i)
                p.setguard(f"sPT{sp_}", "te", pvc)

            pend_q = []
            q0, q1 = QSLOT[h0], QSLOT[h1]
            for jc in range(nj):
                ioff = max(0, jc - 4 * b)
                N = 512 - ioff * 128
                bk = par % 2
                sp = scnt[0] % 3
                scnt[0] += 1
                if bk == 0:
                    p.wguard(T, "T", "pS0")
                    p.wguard(T, "T", "pS1")
                else:
                    p.wguard(T, "T", "pP")
                bank = pBk[bk]
                masked = jc >= 4 * b
                T.matmul(bank[:, 0:N],
                         kt[0:64, jc * 128:(jc + 1) * 128],
                         sQTg[0:64, qp, q0 // 2, ioff * 128:512],
                         start=True, stop=not masked, tile_position=(0, 0))
                i = T.matmul(bank[:, 512:512 + N],
                             kt[64:128, jc * 128:(jc + 1) * 128],
                             sQTg[64:128, qp, q1 // 2, ioff * 128:512],
                             start=True, stop=not masked, tile_position=(64, 0))
                if masked:
                    T.matmul(bank[:, 0:128], sMaskD[:], sIdent[:],
                             start=False, stop=False)
                    i = T.matmul(bank[:, 512:640], sMaskD[:], sIdent[:],
                                 start=False, stop=True)
                smm = p.inc("te", i)
                p.wait(Sc, "Sc", "te", smm)
                p.wguard(Sc, "Sc", f"sPT{sp}")
                i = Sc.activation(out=sPT[:, sp, 0:512 + N], in_=bank[:, 0:512 + N],
                                  func=AF.Exp, scale=0.125)
                ec = p.inc("se", i)
                if bk == 0:
                    p.setguard("pS0", "se", ec)
                    p.setguard("pS1", "se", ec)
                else:
                    p.setguard("pP", "se", ec)
                pend_q.append((jc, ioff, sp, ec))
                if len(pend_q) > 2:
                    emit_pv_g2(pend_q.pop(0))
                par += 1
                if nj >= 8 and jc in (nj // 4, nj // 2, 3 * nj // 4):
                    drain_stage()
            while pend_q:
                emit_pv_g2(pend_q.pop(0), final=(len(pend_q) == 0))
            p.wguard(V, "V", "sAttn")
            p.wguard(V, "V", "sAttn_dma")
            attn_evac(pV0, 0, sAttn[:, :, h0 * 64:(h0 + 1) * 64])
            attn_evac(pV1, 1, sAttn[:, :, h1 * 64:(h1 + 1) * 64])
            if hp < 7:
                drain_stage()
        while stage_q:
            stage_q.pop(0)()
        p.setguard(f"sQTg{qp}", "te")
        attn_ve = p.cnt["ve"]

        if b == 0:
            load_x(0, 0)
            load_x(1, 1)
        vs = {}

        def v1(t):
            ti_ = t % 4
            st_ = vs.setdefault(t, {})
            if _DBG_FULL:
                p.wait(Sy, "Sy", "ve")
                p.inc("st", Sy.dma_start(out=dbgag_d[t * 128:(t + 1) * 128, :],
                                         in_=sAttn[:, ti_, :]), 16)
                p.setguard("sAttn_dma", "st")
            p.wait(T, "T", "ve", attn_ve)
            p.wguard(T, "T", "pT")
            for c in range(8):
                i = T.transpose(pT[:, c * 128:(c + 1) * 128],
                                sAttn[:, ti_, c * 128:(c + 1) * 128], sIdent[:])
            tc = p.inc("te", i)
            p.setguard("sAttn", "te", tc)
            p.wait(Sc, "Sc", "te", tc)
            p.wguard(Sc, "Sc", "sATT")
            i = Sc.copy(out=sATT[:, :, :],
                        in_=pT.rearrange("p (c n) -> p c n", n=128))
            attc = p.inc("se", i)
            p.setguard("pT", "se", attc)
            p.wait(T, "T", "se", attc)
            p.wguard(T, "T", "pP")
            for nb_ in range(2):
                for c in range(8):
                    i = T.matmul(pP[:, nb_ * 512:(nb_ + 1) * 512], sATT[:, c, :],
                                 wWoG[:, c, nb_ * 512:(nb_ + 1) * 512],
                                 start=(c == 0), stop=(c == 7))
            woc = p.inc("te", i)
            p.setguard("sATT", "te", woc)
            p.wait(V, "V", "te", woc)
            sem, lv, slot = xload_val[t]
            st_["slot"] = slot
            p.wait(V, "V", sem, lv)
            p.wguard(V, "V", f"sX1_{t % 2}")
            x1a = sX1[:, t % 2, :]
            i = V.tensor_add(out=x1a, in0=pP[:, :], in1=sXblk[:, slot, :])
            x1c = p.inc("ve", i)
            p.setguard("pP", "ve", x1c)
            p.wait(Sc, "Sc", "ve", x1c)
            rmsnorm_stats(x1a, rstd2w, sSq)
            p.wguard(V, "V", "sXb")
            p.wait(V, "V", "ve", x1c)
            i = V.tensor_copy(out=sXb[:], in_=x1a)
            xbc = p.inc("ve", i)
            p.wait(T, "T", "ve", xbc)
            transposes(sXb, 8, sXT[:, :, :], "sXT")
            p.setguard("sXb", "te")
            p.wait(T, "T", "ve", p.cnt["ve"])
            p.wguard(T, "T", "pP")
            for nb_ in range(2):
                for c in range(8):
                    i = T.matmul(pP[:, nb_ * 512:(nb_ + 1) * 512], sXT[:, c, :],
                                 wWqL[:, c, nb_ * 512:(nb_ + 1) * 512],
                                 start=(c == 0), stop=(c == 7))
            st_["qmm"] = p.inc("te", i)
            p.wguard(T, "T", "pX")
            for c in range(8):
                i = T.matmul(pX[:, 0:LAT], sXT[:, c, :], wWkL[:, c, :],
                             start=(c == 0), stop=(c == 7))
            kmm = p.inc("te", i)
            p.wait(Sc, "Sc", "te", kmm)
            p.wguard(Sc, "Sc", "sKtm")
            i = Sc.mul(out=sKtm[:], in_=pX[:, 0:LAT], mul=rstd2)
            kevc = p.inc("se", i)
            st_["kevc"] = kevc
            p.setguard("pX", "se", kevc)
            p.wait(Sc, "Sc", "te", st_["qmm"])
            p.wguard(Sc, "Sc", "sQtm")
            i = Sc.mul(out=sQtm[:], in_=pP[:, :], mul=rstd2)
            qtmc = p.inc("se", i)
            st_["qtmc"] = qtmc
            p.setguard("pP", "se", qtmc)

        def v2a(t):
            st_ = vs[t]
            p.wguard(T, "T", "pX")
            for c in range(8):
                i = T.matmul(pX[:, 0:LAT], sXT[:, c, :], wWvL[:, c, :],
                             start=(c == 0), stop=(c == 7))
            vmm = p.inc("te", i)
            p.setguard("sXT", "te", vmm)
            p.wait(Sc, "Sc", "te", vmm)
            p.wguard(Sc, "Sc", f"sVl{t % 3}")
            i = Sc.mul(
                out=sVl[:, t % 3, 0:264].rearrange("p (h k) -> p h k", k=66)[:, :, 0:64],
                in_=pX[:, 0:LAT].rearrange("p (h k) -> p h k", k=64),
                mul=rstd2)
            vlc = p.inc("se", i)
            st_["vlc"] = vlc
            p.setguard("pX", "se", vlc)
            p.wait(V, "V", "se", st_["qtmc"])
            p.wguard(V, "V", "sSq")
            p.wguard(V, "V", "sQrope")
            rope(sQtm, sQrope, NH, t, sCosL, sSinL, sSq)
            p.setguard("sQtm", "ve")

        def v2bV(t):
            st_ = vs[t]
            p.wguard(V, "V", "sQb")
            st_["qbc"] = qknorm(sQrope, sQb, NH, ssq16, rq16, sSq)
            p.wguard(V, "V", "sKbf")
            p.wait(V, "V", "se", st_["kevc"])
            rope(sKtm, sQrope[:, 0:LAT], KVH, t, sCosL, sSinL, sSq)
            p.setguard("sKtm", "ve")
            st_["kbfc"] = qknorm(sQrope[:, 0:LAT], sKbf, KVH, ssk4, rk4, sSq)
            p.setguard("sQrope", "ve")
            p.setguard("sSq", "ve")

        def v2bT(t):
            st_ = vs[t]
            qbc = st_["qbc"]
            kbfc = st_["kbfc"]
            p.wait(T, "T", "ve", qbc)
            p.wguard(T, "T", "pT")
            for c in range(8):
                i = T.transpose(pT[:, c * 128:(c + 1) * 128],
                                sQb[:, c * 128:(c + 1) * 128], sIdent[:])
            tc = p.inc("te", i)
            p.setguard("sQb", "te", tc)
            p.wait(Sc, "Sc", "te", tc)
            p.wguard(Sc, "Sc", "sQTl")
            i = Sc.copy(out=sQTl[:, :, :],
                        in_=pT.rearrange("p (c n) -> p c n", n=128))
            st_["qtl"] = p.inc("se", i)
            p.setguard("pT", "se", st_["qtl"])
            p.wait(T, "T", "ve", kbfc)
            p.wguard(T, "T", "pT")
            i = T.transpose(pT[:, 0:128], sKbf[:, 0:128], sIdent[:])
            i = T.transpose(pT[:, 128:256], sKbf[:, 128:256], sIdent[:])
            tc = p.inc("te", i)
            p.setguard("sKbf", "te", tc)
            p.wait(Sc, "Sc", "te", tc)
            p.wguard(Sc, "Sc", f"sKTl{t % 3}")
            i = Sc.copy(out=sKTl[:, t % 3, :], in_=pT[:, 0:256])
            st_["klc"] = p.inc("se", i)
            p.setguard("pT", "se", st_["klc"])

        def v3(t, grange):
            st_ = vs[t]
            slots = [s_ for s_ in (t - 2, t - 1, t) if s_ >= 0]
            p.wait(T, "T", "se", max(st_["klc"], st_["qtl"], st_["vlc"]))
            for g in grange:
                kv = g
                pvp = g % 2
                pv = pVb[pvp]
                lpar = g % 2
                p.wguard(T, "T", f"pV{pvp}")

                def emit_pv_l(pend, final=False):
                    si_, s_, sp_, ec_ = pend
                    p.wait(T, "T", "se", ec_)
                    if final:
                        for _ in range(3):
                            T.ldweights(sIdent[:])
                    for hh in range(4):
                        i = T.matmul(pv[:, hh * 66:hh * 66 + 65],
                                     sPT[:, sp_, hh * 128:(hh + 1) * 128],
                                     sVl[:, s_ % 3, kv * 66:kv * 66 + 65],
                                     start=(si_ == 0 and hh == 0),
                                     stop=(si_ == len(slots) - 1))
                    pvc = p.inc("te", i)
                    p.setguard(f"sPT{sp_}", "te", pvc)

                pend_q = []
                for si, s_ in enumerate(slots):
                    sp = scnt[0] % 3
                    scnt[0] += 1
                    p.wguard(T, "T", f"pS{lpar}")
                    mask = sMaskD if s_ == t else (sMaskL if s_ == t - 2 else None)
                    for hh in range(4):
                        h_ = 4 * g + hh
                        qs = QSLOT[h_]
                        i = T.matmul(pSb[lpar][:, hh * 128:(hh + 1) * 128],
                                     sKTl[(kv % 2) * 64:(kv % 2) * 64 + 64, s_ % 3,
                                          (kv // 2) * 128:(kv // 2) * 128 + 128],
                                     sQTl[(qs % 2) * 64:(qs % 2) * 64 + 64, qs // 2, :],
                                     start=(hh == 0),
                                     stop=(mask is None and hh == 3))
                    if mask is not None:
                        i = T.matmul(pSb[lpar][:, 0:512], mask[:], sId4[:],
                                     start=False, stop=True)
                    smm = p.inc("te", i)
                    p.wait(Sc, "Sc", "te", smm)
                    p.wguard(Sc, "Sc", f"sPT{sp}")
                    i = Sc.activation(out=sPT[:, sp, 0:512], in_=pSb[lpar][:, 0:512],
                                      func=AF.Exp, scale=0.125)
                    ec = p.inc("se", i)
                    p.setguard(f"pS{lpar}", "se", ec)
                    pend_q.append((si, s_, sp, ec))
                    if len(pend_q) > 2:
                        emit_pv_l(pend_q.pop(0))
                    lpar ^= 1
                while pend_q:
                    emit_pv_l(pend_q.pop(0), final=(len(pend_q) == 0))
                p.wguard(V, "V", f"sAttnL{t % 2}")
                p.wguard(V, "V", "sAttnL_dma")
                attn_evac(pv, pvp,
                          sAttnL[:, t % 2, g * 256:(g + 1) * 256].rearrange(
                              "p (s k) -> p s k", k=64))
            if grange[-1] == 3:
                for s_ in slots:
                    p.setguard(f"sVl{s_ % 3}", "te")
                    p.setguard(f"sKTl{s_ % 3}", "te")
                p.setguard("sQTl", "te")
                st_["alve"] = p.cnt["ve"]

        def v4(t):
            st_ = vs.pop(t)
            slot = st_["slot"]
            x1a = sX1[:, t % 2, :]
            if _DBG_FULL:
                p.wait(Sy, "Sy", "ve")
                p.inc("st", Sy.dma_start(out=dbgal_d[t * 128:(t + 1) * 128, :],
                                         in_=sAttnL[:, t % 2, :]), 16)
                p.setguard("sAttnL_dma", "st")
            p.wait(T, "T", "ve", st_["alve"])
            p.wguard(T, "T", "pT")
            for c in range(8):
                i = T.transpose(pT[:, c * 128:(c + 1) * 128],
                                sAttnL[:, t % 2, c * 128:(c + 1) * 128], sIdent[:])
            tc = p.inc("te", i)
            p.setguard(f"sAttnL{t % 2}", "te", tc)
            p.wait(Sc, "Sc", "te", tc)
            p.wguard(Sc, "Sc", "sATT")
            i = Sc.copy(out=sATT[:, :, :],
                        in_=pT.rearrange("p (c n) -> p c n", n=128))
            attc = p.inc("se", i)
            p.setguard("pT", "se", attc)
            p.wait(T, "T", "se", attc)
            p.wguard(T, "T", "pP")
            for nb_ in range(2):
                for c in range(8):
                    i = T.matmul(pP[:, nb_ * 512:(nb_ + 1) * 512], sATT[:, c, :],
                                 wWoL[:, c, nb_ * 512:(nb_ + 1) * 512],
                                 start=(c == 0), stop=(c == 7))
            woc = p.inc("te", i)
            p.setguard("sATT", "te", woc)
            p.wait(V, "V", "te", woc)
            x2a = sXblk[:, slot, :]
            i = V.tensor_add(out=x1a, in0=pP[:, :], in1=x1a)
            x2c = p.inc("ve", i)
            p.setguard("pP", "ve", x2c)
            i = V.tensor_add(out=x2a, in0=x1a, in1=x2a)
            x2c = p.inc("ve", i)
            p.setguard(f"sX1_{t % 2}", "ve", x2c)
            ln_apply(x2a, x2a)
            y1c = p.cnt["ve"]
            p.wait(G, "G", "ve", y1c)
            p.inc("gst", G.dma_start(out=y1_d[t * 128:(t + 1) * 128, :],
                                     in_=x2a), 16)
            p.setguard(f"xblkA{slot}", "gst")
            if t + 2 < NT:
                load_x(t + 2, (t + 2) % 2)

        v1(4 * b)
        for ti in range(4):
            t = 4 * b + ti
            if ti >= 1:
                v4(t - 1)
            v2a(t)
            v2bV(t)
            if ti < 3:
                v1(t + 1)
            v2bT(t)
            v3(t, (0, 1))
            v3(t, (2, 3))
        pending_v4[0] = (lambda f=v4, t_=4 * b + 3: f(t_))

    pending_v4[0]()
    pending_v4[0] = None
    ab_te, ab_ve, ab_se = p.cnt["te"], p.cnt["ve"], p.cnt["se"]
    stAB.close()

    wW1 = sb("wW1", [128, 8, 4 * D], BF16)
    wW2 = sb("wW2", [128, 32, D], BF16)
    sB1r = sb("sB1r", [1, 4 * D], BF16)
    sB2r = sb("sB2r", [1, D], BF16)
    sY1c = sb("sY1c", [128, 2, D], F32)
    sY1b = sb("sY1b", [128, 2, D], BF16)
    sY1T = sb("sY1T", [128, 2, 8, 128], BF16)
    sHtm = sb("sHtm", [128, 4 * D], BF16)
    sHT = sb("sHT", [128, 32, 128], BF16)
    sX3 = sb("sX3", [128, D], F32)
    sOut = sb("sOut", [128, 2, D], F32)

    p.wait(G, "G", "te", ab_te)
    p.wait(G, "G", "ve", ab_ve)
    p.wait(G, "G", "se", ab_se)
    p.inc("ldw", G.dma_start(out=sB1r[:], in_=b1_d[:, :]), 16)
    p.inc("ldw", G.dma_start(out=sB2r[:], in_=b2_d[:, :]), 16)
    w12_ldw = p.cnt["ldw"]
    w1p, w2p = [], []
    for w in range(4):
        w1p.append(("lw%d" % w, p.inc("lw%d" % w, G.dma_start(
            out=wW1[:, :, w * 1024:(w + 1) * 1024],
            in_=w1_d[:, w * 1024:(w + 1) * 1024].rearrange("(c p) n -> p c n", p=128)), 16)))
    for w in range(4):
        w2p.append(("lw%d" % (4 + w), p.inc("lw%d" % (4 + w), G.dma_start(
            out=wW2[:, w * 8:(w + 1) * 8, :],
            in_=w2_d[w * 1024:(w + 1) * 1024, :].rearrange("(c p) n -> p c n", p=128)), 16)))

    y1load = {}
    p.wait(G, "G", "gst", 16 * NT)

    def load_y1(t):
        p.wguard(G, "G", f"sY1c{t % 2}")
        sem = f"ly{t % 3}"
        y1load[t] = (sem, p.inc(sem, G.dma_start(out=sY1c[:, t % 2, :],
                                                 in_=y1_d[t * 128:(t + 1) * 128, :]), 16))

    load_y1(0)
    if NT > 1:
        load_y1(1)
    prep_ve = {}

    def prep_y1(t):
        ya_ = sY1c[:, t % 2, :]
        p.wait(V, "V", y1load[t][0], y1load[t][1])
        p.wguard(V, "V", f"sY1b{t % 2}")
        i = V.tensor_copy(out=sY1b[:, t % 2, :], in_=ya_)
        ybc = p.inc("ve", i)
        p.wait(T, "T", "ve", ybc)
        p.wait(T, "T", "ldw", w12_ldw)
        p.wguard(T, "T", "pT")
        for c in range(8):
            i = T.transpose(pT[:, c * 128:(c + 1) * 128],
                            sY1b[:, t % 2, c * 128:(c + 1) * 128], sIdent[:])
        tc = p.inc("te", i)
        p.setguard(f"sY1b{t % 2}", "te", tc)
        p.wait(V, "V", "te", tc)
        p.wguard(V, "V", f"sY1T{t % 2}")
        i = V.tensor_copy(out=sY1T[:, t % 2, :, :],
                          in_=pT.rearrange("p (c n) -> p c n", n=128))
        p.setguard("pT", "ve", p.inc("ve", i))
        prep_ve[t] = p.cnt["ve"]

    prep_y1(0)
    for t in range(NT):
        ya = sY1c[:, t % 2, :]
        yT = sY1T[:, t % 2, :, :]
        p.wait(T, "T", "ve", prep_ve[t])
        hpair = [(pP[:, 0:512], pP[:, 512:1024], "pP"),
                 (pV0[:, 0:512], pV1[:, 0:512], "pVh")]
        hc_w = {}

        def emit_h(w):
            pa, pb, gname = hpair[w % 2]
            p.wait(T, "T", w1p[w][0], w1p[w][1])
            p.wguard(T, "T", gname)
            for nb_, dst in ((0, pa), (1, pb)):
                hcol = w * 1024 + nb_ * 512
                for c in range(8):
                    T.matmul(dst, yT[:, c, :], wW1[:, c, hcol:hcol + 512],
                             start=(c == 0), stop=False)
                i = T.matmul(dst, sOnes[0:1, :], sB1r[0:1, hcol:hcol + 512],
                             start=False, stop=True)
            hmm = p.inc("te", i)
            p.wait(Sc, "Sc", "te", hmm)
            p.wguard(Sc, "Sc", "sHtm")
            Sc.activation(out=sHtm[:, w * 1024:w * 1024 + 512], in_=pa, func=AF.Relu)
            i = Sc.activation(out=sHtm[:, w * 1024 + 512:(w + 1) * 1024], in_=pb,
                              func=AF.Relu)
            hc = p.inc("se", i)
            p.setguard(gname, "se", hc)
            hc_w[w] = hc

        ht_c = {}

        def emit_tailA(w):
            p.wait(T, "T", "se", hc_w[w])
            p.wguard(T, "T", "pT")
            for c in range(8):
                i = T.transpose(pT[:, c * 128:(c + 1) * 128],
                                sHtm[:, w * 1024 + c * 128:w * 1024 + (c + 1) * 128],
                                sIdent[:])
            tc = p.inc("te", i)
            p.wait(V, "V", "te", tc)
            p.wguard(V, "V", "sHT")
            i = V.tensor_copy(out=sHT[:, w * 8:(w + 1) * 8, :],
                              in_=pT.rearrange("p (c n) -> p c n", n=128))
            htc = p.inc("ve", i)
            p.setguard("pT", "ve", htc)
            ht_c[w] = htc

        def emit_tailB(w):
            p.wait(T, "T", "ve", ht_c[w])
            p.wait(T, "T", w2p[w][0], w2p[w][1])
            if w == 0:
                p.wguard(T, "T", "pSf")
            for c in range(8):
                cc = w * 8 + c
                T.matmul(pS0[:, :], sHT[:, cc, :], wW2[:, cc, 0:512],
                         start=(cc == 0), stop=False)
                i = T.matmul(pS1[:, :], sHT[:, cc, :], wW2[:, cc, 512:1024],
                             start=(cc == 0), stop=False)
            p.inc("te", i)

        for w in range(4):
            emit_h(w)
            if w == 1 and t + 1 < NT:
                prep_y1(t + 1)
            if w >= 1:
                emit_tailA(w - 1)
            if w >= 2:
                emit_tailB(w - 2)
        emit_tailA(3)
        emit_tailB(2)
        emit_tailB(3)
        p.setguard("sHtm", "te")
        p.setguard("sHT", "te")
        p.setguard(f"sY1T{t % 2}", "te")
        T.matmul(pS0[:, :], sOnes[0:1, :], sB2r[0:1, 0:512], start=False, stop=True)
        i = T.matmul(pS1[:, :], sOnes[0:1, :], sB2r[0:1, 512:1024], start=False, stop=True)
        fmm = p.inc("te", i)
        p.wait(V, "V", "te", fmm)
        p.wguard(V, "V", "sX3")
        p.wguard(V, "V", "sX3_dma")
        V.tensor_add(out=sX3[:, 0:512], in0=pS0[:, :], in1=ya[:, 0:512])
        i = V.tensor_add(out=sX3[:, 512:1024], in0=pS1[:, :], in1=ya[:, 512:1024])
        x3c = p.inc("ve", i)
        p.setguard("pSf", "ve", x3c)
        p.setguard(f"sY1c{t % 2}", "ve", x3c)
        if t + 2 < NT:
            load_y1(t + 2)
        if _DBG_FULL:
            p.wait(Sy, "Sy", "ve")
            p.inc("st", Sy.dma_start(out=dbgx3_d[t * 128:(t + 1) * 128, :],
                                     in_=sX3[:]), 16)
            p.setguard("sX3_dma", "st")
        p.wguard(V, "V", f"sOut{t % 2}")
        ln_apply(sX3, sOut[:, t % 2, :])
        oc = p.cnt["ve"]
        p.setguard("sX3", "ve", oc)
        p.wait(Sy, "Sy", "ve", oc)
        p.inc(f"st{t % 2}", Sy.dma_start(out=out_d[t * 128:(t + 1) * 128, :],
                                         in_=sOut[:, t % 2, :]), 16)
        p.setguard(f"sOut{t % 2}", f"st{t % 2}")

    if _DBG_FULL:
        p.wait(Sy, "Sy", "gst")
        p.inc("st", Sy.dma_start(out=dbgy1_d[:, :], in_=y1_d[:, :]), 16)
        Sy.wait_ge(p.sem["st"], p.cnt["st"])
    Sy.wait_ge(p.sem["st0"], p.cnt["st0"])
    Sy.wait_ge(p.sem["st1"], p.cnt["st1"])
    st.close()
    return nc


_EO = np.concatenate([np.arange(0, 64, 2), np.arange(1, 64, 2)])


def _perm_wq(w):
    w = np.asarray(w)
    cols = np.concatenate([h * 64 + _EO for h in QORDER])
    return np.ascontiguousarray(w[:, cols])


def _perm_eo(w, nheads):
    w = np.asarray(w)
    cols = np.concatenate([h * 64 + _EO for h in range(nheads)])
    return np.ascontiguousarray(w[:, cols])


def _prep_inputs(S, inputs):
    c = _host_consts(S)
    common = {
        "g_wq": _perm_wq(inputs["L0_attn_g_wq"]), "g_kv_down": inputs["L0_attn_g_kv_down"],
        "g_k_up": _perm_eo(inputs["L0_attn_g_k_up"], KVH),
        "g_v_up": inputs["L0_attn_g_v_up"],
        "g_wo": inputs["L0_attn_g_wo"], "l_wq": _perm_wq(inputs["L0_attn_l_wq"]),
        "l_wk": _perm_eo(inputs["L0_attn_l_wk"], KVH), "l_wv": inputs["L0_attn_l_wv"],
        "l_wo": inputs["L0_attn_l_wo"], "w1": inputs["L0_ffn_w1"],
        "w2": inputs["L0_ffn_w2"],
        "b1r": np.asarray(inputs["L0_ffn_b1"]).reshape(1, -1),
        "b2r": np.asarray(inputs["L0_ffn_b2"]).reshape(1, -1),
        "n1pc": np.asarray(inputs["L0_attn_norm1_w"]).reshape(8, 128).T,
        "n2pc": np.asarray(inputs["L0_attn_norm2_w"]).reshape(8, 128).T,
    }
    common.update(c)
    return {k: np.ascontiguousarray(v) for k, v in common.items()}


def kernel(**inputs):
    x = np.asarray(inputs["L0_attn_hidden"])
    B, S, _ = x.shape
    common = _prep_inputs(S, inputs)
    nc = build_program(S)
    in_maps = []
    for b in range(B):
        m = dict(common)
        m["x"] = np.ascontiguousarray(x[b])
        in_maps.append(m)
    res = run_bass_kernel_spmd(nc, in_maps, core_ids=list(range(B)))
    out = np.stack([np.asarray(res.results[b]["out"]) for b in range(B)], axis=0)
    return out.astype(np.float32)


# version 27
# speedup vs baseline: 1.2034x; 1.0092x over previous
import contextlib
import numpy as np
import ml_dtypes
import concourse.bass as bass
import concourse.mybir as mybir
from concourse.bass_utils import run_bass_kernel_spmd
from concourse.ap import AP as _AP


def _ap(x):
    return x if isinstance(x, _AP) else x[:]

F32 = mybir.dt.float32
BF16 = mybir.dt.bfloat16
AF = mybir.ActivationFunctionType
ALU = mybir.AluOpType
AX = mybir.AxisListType

D = 1024
NH = 16
KVH = 4
HD = 64
LAT = 256
NEG = -1.0e6
BF = ml_dtypes.bfloat16
QORDER = [0, 4, 1, 5, 2, 6, 3, 7, 8, 12, 9, 13, 10, 14, 11, 15]
QSLOT = {h: s for s, h in enumerate(QORDER)}
_DBG = None
_DBG_TILE = 2
_DBG_FULL = False


def _rope_tables(S, base):
    exps = np.arange(0, HD, 2).astype(np.float32) / np.float32(HD)
    invf = (np.float32(1.0) / np.power(np.float32(base), exps)).astype(np.float32)
    t = np.arange(S, dtype=np.float32)
    freqs = (t[:, None] * invf[None, :]).astype(np.float32)
    cos = np.cos(freqs.astype(np.float64)).astype(np.float32)
    sin = np.sin(freqs.astype(np.float64)).astype(np.float32)
    nt = S // 128
    return (np.ascontiguousarray(cos.reshape(nt, 128, 32).transpose(1, 0, 2)),
            np.ascontiguousarray(sin.reshape(nt, 128, 32).transpose(1, 0, 2)))


def _host_consts(S):
    p = np.arange(128)
    maskd = np.where(p[:, None] <= p[None, :], 0.0, NEG).astype(np.float32)
    maskl = np.where(p[:, None] > p[None, :], 0.0, NEG).astype(np.float32)
    cg, sg = _rope_tables(S, 1000000.0)
    cl, sl = _rope_tables(S, 10000.0)
    eye = np.eye(128, dtype=BF)
    return {
        "identB": eye,
        "ident4": np.ascontiguousarray(np.tile(eye, (1, 4))),
        "onesB": np.ones((1, 128), dtype=BF),
        "maskdT": np.ascontiguousarray(maskd.T).astype(BF),
        "masklT": np.ascontiguousarray(maskl.T).astype(BF),
        "cosg": cg.astype(BF), "sing": sg.astype(BF),
        "cosl": cl.astype(BF), "sinl": sl.astype(BF),
    }


class P:

    def __init__(self):
        self.sem = {}
        self.cnt = {}
        self.waited = {}
        self.guard = {}

    def add_sem(self, name, handle):
        self.sem[name] = handle
        self.cnt[name] = 0

    def inc(self, name, inst, n=1):
        inst.then_inc(self.sem[name], n)
        self.cnt[name] += n
        return self.cnt[name]

    def wait(self, eng, ename, sname, val=None):
        val = self.cnt[sname] if val is None else val
        if val <= 0:
            return
        k = (ename, sname)
        if self.waited.get(k, 0) < val:
            eng.wait_ge(self.sem[sname], val)
            self.waited[k] = val

    def wguard(self, eng, ename, rname):
        g = self.guard.get(rname)
        if g is not None:
            self.wait(eng, ename, g[0], g[1])

    def setguard(self, rname, sname, val=None):
        v = self.cnt[sname] if val is None else val
        old = self.guard.get(rname)
        if old is not None and old[0] == sname and old[1] >= v:
            return
        self.guard[rname] = (sname, v)


def build_program(S):
    NT = S // 128
    NB = S // 512
    nc = bass.Bass("TRN2", target_bir_lowering=False)

    def din(name, shape, dt=F32):
        return nc.dram_tensor(name, list(shape), dt, kind="ExternalInput")

    x_d = din("x", [S, D])
    gwq_d = din("g_wq", [D, D]); gkvd_d = din("g_kv_down", [D, LAT])
    gkup_d = din("g_k_up", [LAT, LAT]); gvup_d = din("g_v_up", [LAT, LAT])
    gwo_d = din("g_wo", [D, D])
    lwq_d = din("l_wq", [D, D]); lwk_d = din("l_wk", [D, LAT])
    lwv_d = din("l_wv", [D, LAT]); lwo_d = din("l_wo", [D, D])
    w1_d = din("w1", [D, 4 * D]); w2_d = din("w2", [4 * D, D])
    b1_d = din("b1r", [1, 4 * D]); b2_d = din("b2r", [1, D])
    n1_d = din("n1pc", [128, 8]); n2_d = din("n2pc", [128, 8])
    idb_d = din("identB", [128, 128], BF16)
    id4_d = din("ident4", [128, 512], BF16)
    ones_d = din("onesB", [1, 128], BF16)
    maskd_d = din("maskdT", [128, 128], BF16)
    maskl_d = din("masklT", [128, 128], BF16)
    cosg_d = din("cosg", [128, NT, 32], BF16); sing_d = din("sing", [128, NT, 32], BF16)
    cosl_d = din("cosl", [128, NT, 32], BF16); sinl_d = din("sinl", [128, NT, 32], BF16)

    y1_d = nc.dram_tensor("y1scratch", [S, D], F32, kind="Internal")
    out_d = nc.dram_tensor("out", [S, D], F32, kind="ExternalOutput")
    if _DBG_FULL:
        dbgag_d = nc.dram_tensor("dbg_ag", [S, D], BF16, kind="ExternalOutput")
        dbgal_d = nc.dram_tensor("dbg_al", [S, D], BF16, kind="ExternalOutput")
        dbgy1_d = nc.dram_tensor("dbg_y1f", [S, D], F32, kind="ExternalOutput")
        dbgx3_d = nc.dram_tensor("dbg_x3f", [S, D], F32, kind="ExternalOutput")
        dbgy1b_d = nc.dram_tensor("dbg_y1b", [S, D], BF16, kind="ExternalOutput")

    p = P()
    st = contextlib.ExitStack()
    sb = lambda name, shape, dt: st.enter_context(nc.sbuf_tensor(name, list(shape), dt))
    ps = lambda name, shape, dt: st.enter_context(nc.psum_tensor(name, list(shape), dt))
    for s in ("ldx", "ldw", "gst", "st", "te", "ve", "se",
              "lx0", "lx1", "lx2", "lx3", "lx4", "lx5",
              "ly0", "ly1", "ly2", "st0", "st1",
              "lw0", "lw1", "lw2", "lw3", "lw4", "lw5", "lw6", "lw7"):
        p.add_sem(s, st.enter_context(nc.semaphore(s)))

    T, V, Sc, Sy, G = nc.tensor, nc.vector, nc.scalar, nc.sync, nc.gpsimd

    sIdent = sb("sIdent", [128, 128], BF16)
    sId4 = sb("sId4", [128, 512], BF16)
    sOnes = sb("sOnes", [1, 128], BF16)
    sMaskD = sb("sMaskD", [128, 128], BF16)
    sMaskL = sb("sMaskL", [128, 128], BF16)
    sCosG = sb("sCosG", [128, NT, 32], BF16)
    sSinG = sb("sSinG", [128, NT, 32], BF16)
    sCosL = sb("sCosL", [128, NT, 32], BF16)
    sSinL = sb("sSinL", [128, NT, 32], BF16)
    sN1 = sb("sN1", [128, 8], F32)
    sN2 = sb("sN2", [128, 8], F32)
    sEps = sb("sEps", [128, 4], F32)
    sStats = sb("sStats", [128, 128], F32)
    sDen = sb("sDen", [128, 16], F32)
    sBn = sb("sBn", [128, 16], F32)
    sMv = sb("sMv", [128, 64], F32)

    pT = ps("pT", [128, 1024], BF16)
    pSd = ps("pSd", [128, 1024], F32)
    pS0 = pSd[:, 0:512]
    pS1 = pSd[:, 512:1024]
    pP = ps("pP", [128, 1024], F32)
    pV0 = ps("pV0", [128, 512], F32)
    pV1 = ps("pV1", [128, 512], F32)
    pX = ps("pX", [128, 512], F32)
    pSb = [pS0, pS1]
    pVb = [pV0, pV1]

    EPS_QK_COL, EPS_LN_COL = 0, 1
    V.memset(sEps[:, 0:1], 1e-6)
    V.memset(sEps[:, 1:2], 1e-5)

    for dst, src in ((sIdent, idb_d), (sId4, id4_d), (sOnes, ones_d),
                     (sMaskD, maskd_d), (sMaskL, maskl_d),
                     (sCosG, cosg_d), (sSinG, sing_d),
                     (sCosL, cosl_d), (sSinL, sinl_d), (sN1, n1_d), (sN2, n2_d)):
        p.inc("ldx", Sy.dma_start(out=dst[:], in_=src[:]), 16)
    consts_ldx = p.cnt["ldx"]

    rstd1w = sStats[:, 0:16]
    rstd1 = sStats[:, 0:1]
    rstd2w = sStats[:, 16:32]
    rstd2 = sStats[:, 16:17]
    saccw = sStats[:, 32:48]
    sacc = sStats[:, 32:33]
    stmpw = sStats[:, 48:64]
    ssq16 = sStats[:, 64:80]
    rq16 = sStats[:, 80:96]
    ssk4 = sStats[:, 96:112]
    rk4 = sStats[:, 112:128]

    def rmsnorm_stats(src_ap, dst_rstd_w, sq_buf):
        src_ap = _ap(src_ap)
        p.wguard(Sc, "Sc", "sSq")
        i = Sc.activation(out=sq_buf[:, 0:D], in_=src_ap, func=AF.Square,
                          accum_out=sacc)
        c = p.inc("se", i)
        p.wait(Sc, "Sc", "se", c)
        i = Sc.copy(out=saccw[:, :], in_=sacc.broadcast_to([128, 16]))
        i = Sc.activation(out=stmpw, in_=saccw, func=AF.Ln,
                          bias=sEps[:, EPS_QK_COL:EPS_QK_COL + 1], scale=1.0 / D)
        i = Sc.activation(out=dst_rstd_w, in_=stmpw, func=AF.Exp, scale=-0.5)
        p.inc("se", i)
        p.setguard("sSq", "se")

    def transposes(src, chunks, dst3, dst_guard):
        src = _ap(src)
        p.wguard(T, "T", "pT")
        for c in range(chunks):
            i = T.transpose(pT[:, c * 128:(c + 1) * 128],
                            src[:, c * 128:(c + 1) * 128], sIdent[:])
        tc = p.inc("te", i)
        p.wait(V, "V", "te", tc)
        p.wguard(V, "V", dst_guard)
        i = V.tensor_copy(out=dst3,
                          in_=pT[:, 0:chunks * 128].rearrange("p (c n) -> p c n", n=128))
        c2 = p.inc("ve", i)
        p.setguard("pT", "ve", c2)
        return c2

    def rope(srcf, dstf, nh, t, costab, sintab, sq_buf):
        srcf, dstf = _ap(srcf), _ap(dstf)
        src4 = srcf.rearrange("p (h w m) -> p h w m", m=32, w=2)
        dst4 = dstf.rearrange("p (h w m) -> p h w m", m=32, w=2)
        e, o = src4[:, :, 0, :], src4[:, :, 1, :]
        eo, oo = dst4[:, :, 0, :], dst4[:, :, 1, :]
        cosb = costab[:, t, :].unsqueeze(1).broadcast_to([128, nh, 32])
        sinb = sintab[:, t, :].unsqueeze(1).broadcast_to([128, nh, 32])
        n = nh * 32
        tA = sq_buf[:, 0:n].rearrange("p (h m) -> p h m", m=32)
        tB = sq_buf[:, n:2 * n].rearrange("p (h m) -> p h m", m=32)
        V.tensor_tensor(out=tA, in0=e, in1=cosb, op=ALU.mult)
        V.tensor_tensor(out=tB, in0=o, in1=sinb, op=ALU.mult)
        V.tensor_tensor(out=eo, in0=tA, in1=tB, op=ALU.subtract)
        V.tensor_tensor(out=tA, in0=e, in1=sinb, op=ALU.mult)
        V.tensor_tensor(out=tB, in0=o, in1=cosb, op=ALU.mult)
        i = V.tensor_tensor(out=oo, in0=tA, in1=tB, op=ALU.add)
        return p.inc("ve", i)

    def qknorm(srcf, dstb, nh, ssq, rr, sq_buf):
        srcf, dstb = _ap(srcf), _ap(dstb)
        n = nh * HD
        i = V.tensor_tensor(out=sq_buf[:, 0:n], in0=srcf, in1=srcf, op=ALU.mult)
        p.inc("ve", i)
        i = V.reduce_sum(out=ssq[:, 0:nh],
                         in_=sq_buf[:, 0:n].rearrange("p (h d) -> p h d", d=HD),
                         axis=AX.X)
        c = p.inc("ve", i)
        p.wait(V, "V", "ve", c)
        p.wait(Sc, "Sc", "ve", c)
        i = Sc.activation(out=ssq[:, 0:16], in_=ssq[:, 0:16], func=AF.Ln,
                          bias=sEps[:, EPS_QK_COL:EPS_QK_COL + 1], scale=1.0 / HD)
        i = Sc.activation(out=rr[:, 0:16], in_=ssq[:, 0:16], func=AF.Exp, scale=-0.5)
        c = p.inc("se", i)
        p.wait(V, "V", "se", c)
        rb = rr[:, 0:nh].unsqueeze(2).broadcast_to([128, nh, HD])
        i = V.tensor_tensor(out=dstb.rearrange("p (h d) -> p h d", d=HD),
                            in0=srcf.rearrange("p (h d) -> p h d", d=HD),
                            in1=rb, op=ALU.mult)
        return p.inc("ve", i)

    def ln_apply(srcf, dstf):
        srcf, dstf = _ap(srcf), _ap(dstf)
        i = V.bn_stats(out=sBn[:, 0:6], in_=srcf[:, 0:512])
        i = V.bn_stats(out=sBn[:, 6:12], in_=srcf[:, 512:1024])
        c = p.inc("ve", i)
        p.wait(V, "V", "ve", c)
        i = V.bn_aggr(out=sMv[:, 0:2], in_=sBn[:, 0:12])
        c = p.inc("ve", i)
        p.wait(V, "V", "ve", c)
        i = V.tensor_copy(out=sMv[:, 16:32], in_=sMv[:, 1:2].broadcast_to([128, 16]))
        c = p.inc("ve", i)
        p.wait(V, "V", "ve", c)
        p.wait(Sc, "Sc", "ve", c)
        i = Sc.activation(out=sMv[:, 32:48], in_=sMv[:, 16:32], func=AF.Ln,
                          bias=sEps[:, EPS_LN_COL:EPS_LN_COL + 1], scale=1.0)
        i = Sc.activation(out=sMv[:, 48:64], in_=sMv[:, 32:48], func=AF.Exp,
                          scale=-0.5)
        c = p.inc("se", i)
        p.wait(V, "V", "se", c)
        i = V.scalar_tensor_tensor(out=dstf, in0=srcf, scalar=sMv[:, 0:1],
                                   in1=sMv[:, 48:49].broadcast_to([128, D]),
                                   op0=ALU.subtract, op1=ALU.mult)
        return p.inc("ve", i)

    def attn_evac(pv, pvp, out3):
        p.wait(V, "V", "te", p.cnt["te"])
        i = V.tensor_copy(out=sDen[:, 0:4],
                          in_=pv[:, 0:264].rearrange("p (s k) -> p s k", k=66)[:, :, 64:65])
        c = p.inc("ve", i)
        p.wait(V, "V", "ve", c)
        i = V.reciprocal(out=sDen[:, 0:4], in_=sDen[:, 0:4])
        c = p.inc("ve", i)
        p.wait(V, "V", "ve", c)
        i = V.tensor_tensor(
            out=out3,
            in0=pv[:, 0:264].rearrange("p (s k) -> p s k", k=66)[:, :, 0:64],
            in1=sDen[:, 0:4].unsqueeze(2).broadcast_to([128, 4, 64]),
            op=ALU.mult)
        c = p.inc("ve", i)
        p.setguard(f"pV{pvp}", "ve", c)

    def dump(name, ap, t=None):
        if _DBG is None or (t is not None and t != _DBG_TILE):
            return
        ap = _ap(ap)
        d = nc.dram_tensor("dbg_" + name, list(ap.shape), ap.dtype,
                           kind="ExternalOutput")
        for sn in ("te", "ve", "se", "ldx", "ldw", "gst"):
            p.wait(Sy, "Sy", sn)
        p.inc("st", Sy.dma_start(out=d[:], in_=ap), 16)
        for eng, en in ((T, "T"), (V, "V"), (Sc, "Sc")):
            p.wait(eng, en, "st", p.cnt["st"])
        _DBG.append("dbg_" + name)

    stAB = contextlib.ExitStack()
    sbA = lambda name, shape, dt: stAB.enter_context(nc.sbuf_tensor(name, list(shape), dt))

    wWqG = sbA("wWqG", [128, 8, D], BF16)
    wKVd = sbA("wKVd", [128, 8, LAT], BF16)
    wKup = sbA("wKup", [128, 2, LAT], BF16)
    wVup = sbA("wVup", [128, 2, LAT], BF16)
    wWoG = sbA("wWoG", [128, 8, D], BF16)
    wWqL = sbA("wWqL", [128, 8, D], BF16)
    wWkL = sbA("wWkL", [128, 8, LAT], BF16)
    wWvL = sbA("wWvL", [128, 8, LAT], BF16)
    wWoL = sbA("wWoL", [128, 8, D], BF16)

    with nc.sbuf_tensor("wstageA", [128, 8, D], F32) as wsA, \
         nc.sbuf_tensor("wstageB", [128, 8, D], F32) as wsB:
        for dst, src in ((wKup, gkup_d), (wVup, gvup_d), (wWoG, gwo_d), (wWoL, lwo_d)):
            p.inc("ldw", G.dma_start(out=dst[:], in_=src.rearrange("(c p) n -> p c n", p=128)), 16)
        folds = ((gwq_d, wWqG, sN1, 8, D), (gkvd_d, wKVd, sN1, 8, LAT),
                 (lwq_d, wWqL, sN2, 8, D), (lwk_d, wWkL, sN2, 8, LAT),
                 (lwv_d, wWvL, sN2, 8, LAT))
        ws = [wsA, wsB]
        for fi, (src, dst, nw, nch, ncol) in enumerate(folds):
            buf = ws[fi % 2]
            p.wguard(G, "G", f"wstage{fi % 2}")
            lv = p.inc("ldw", G.dma_start(out=buf[:, 0:nch, 0:ncol],
                                          in_=src.rearrange("(c p) n -> p c n", p=128)), 16)
            p.wait(V, "V", "ldw", lv)
            p.wait(V, "V", "ldx", consts_ldx)
            for c in range(nch):
                i = V.tensor_scalar(out=dst[:, c, 0:ncol], in0=buf[:, c, 0:ncol],
                                    scalar1=nw[:, c:c + 1], scalar2=None, op0=ALU.mult)
            p.inc("ve", i)
            p.setguard(f"wstage{fi % 2}", "ve")
    weights_ldw = p.cnt["ldw"]
    setup_ve = p.cnt["ve"]

    sKTg0 = sbA("sKTg0", [128, S], BF16)
    sKTg1 = sbA("sKTg1", [128, S], BF16)
    sVg = sbA("sVg", [128, NT, 260], BF16)
    sXblk = sbA("sXblk", [128, 4, D], F32)
    sX1 = sbA("sX1", [128, 2, D], F32)
    sXb = sbA("sXb", [128, D], BF16)
    sXT = sbA("sXT", [128, 8, 128], BF16)
    sLat = sbA("sLat", [128, LAT], BF16)
    sLatT = sbA("sLatT", [128, 2, 128], BF16)
    sKtm = sbA("sKtm", [128, LAT], F32)
    sKbf = sbA("sKbf", [128, LAT], BF16)
    sQtm = sbA("sQtm", [128, D], F32)
    sQrope = sbA("sQrope", [128, D], F32)
    sQb = sbA("sQb", [128, D], BF16)
    sSq = sbA("sSq", [128, D], F32)
    sQTg = sbA("sQTg", [128, 2, 8, 512], BF16)
    sQTl = sbA("sQTl", [128, 8, 128], BF16)
    sKTl = sbA("sKTl", [128, 3, 256], BF16)
    sVl = sbA("sVl", [128, 3, 264], BF16)
    sPT = sbA("sPT", [128, 3, 1024], BF16)
    sAttn = sbA("sAttn", [128, 4, D], BF16)
    sAttnL = sbA("sAttnL", [128, 2, D], BF16)
    sATT = sbA("sATT", [128, 8, 128], BF16)

    for eng, en in ((V, "V"), (Sc, "Sc"), (T, "T"), (Sy, "Sy"), (G, "G")):
        p.wait(eng, en, "ve", setup_ve)
        p.wait(eng, en, "ldw", weights_ldw)
        p.wait(eng, en, "ldx", p.cnt["ldx"])
    p.inc("ve", V.memset(sVg[:], 1.0))
    p.inc("ve", V.memset(sVl[:], 1.0))

    xload_val = {}
    xissue = [0]
    scnt = [0]

    def load_x(t, slot):
        p.wguard(Sy, "Sy", f"xblkA{slot}")
        p.wguard(Sy, "Sy", f"xblkB{slot}")
        sem = f"lx{xissue[0] % 6}"
        xissue[0] += 1
        v = p.inc(sem, Sy.dma_start(out=sXblk[:, slot, :],
                                    in_=x_d[t * 128:(t + 1) * 128, :]), 16)
        xload_val[t] = (sem, v, slot)

    qready_ve = {}

    def part1_stages(t):
        b = t // 4
        ti = t % 4
        qp = b % 2
        st_ = {}

        def s0():
            sem, lv, slot = xload_val[t]
            xa = sXblk[:, slot, :]
            p.wait(V, "V", sem, lv)
            p.wait(Sc, "Sc", sem, lv)
            rmsnorm_stats(xa, rstd1w, sSq)
            p.setguard(f"xblkB{slot}", "se")
            p.wguard(V, "V", "sXb")
            i = V.tensor_copy(out=sXb[:], in_=xa)
            st_["xbc"] = p.inc("ve", i)
            p.setguard(f"xblkA{slot}", "ve", st_["xbc"])
            if ti < 2 and t + 2 < NT:
                load_x(t + 2, 2 + ((t + 2) % 2))

        def s1():
            p.wait(T, "T", "ve", st_["xbc"])
            transposes(sXb, 8, sXT[:, :, :], "sXT")
            p.setguard("sXb", "te")
            p.wait(T, "T", "ve", p.cnt["ve"])
            p.wguard(T, "T", "pX")
            for c in range(8):
                i = T.matmul(pX[:, 0:512], sXT[:, c, :], wWqG[:, c, 0:512],
                             start=(c == 0), stop=(c == 7))
            q0 = p.inc("te", i)
            p.wait(V, "V", "te", q0)
            p.wait(V, "V", "se", p.cnt["se"])
            p.wguard(V, "V", "sQtm")
            i = V.tensor_scalar(out=sQtm[:, 0:512], in0=pX[:, 0:512], scalar1=rstd1,
                                scalar2=None, op0=ALU.mult)
            p.setguard("pX", "ve", p.inc("ve", i))

        def s2():
            p.wguard(T, "T", "pX")
            for c in range(8):
                i = T.matmul(pX[:, 0:512], sXT[:, c, :], wWqG[:, c, 512:1024],
                             start=(c == 0), stop=(c == 7))
            q1 = p.inc("te", i)
            p.wait(V, "V", "te", q1)
            i = V.tensor_scalar(out=sQtm[:, 512:1024], in0=pX[:, 0:512], scalar1=rstd1,
                                scalar2=None, op0=ALU.mult)
            p.setguard("pX", "ve", p.inc("ve", i))

        def s3():
            p.wguard(T, "T", "pX")
            for c in range(8):
                i = T.matmul(pX[:, 0:LAT], sXT[:, c, :], wKVd[:, c, :],
                             start=(c == 0), stop=(c == 7))
            lmm = p.inc("te", i)
            p.setguard("sXT", "te", lmm)
            p.wait(V, "V", "te", lmm)
            p.wguard(V, "V", "sLat")
            i = V.tensor_scalar(out=sLat[:], in0=pX[:, 0:LAT], scalar1=rstd1,
                                scalar2=None, op0=ALU.mult)
            p.setguard("pX", "ve", p.inc("ve", i))
            p.wguard(V, "V", "sSq")
            p.wguard(V, "V", "sQrope")
            rope(sQtm, sQrope, NH, t, sCosG, sSinG, sSq)
            p.setguard("sQtm", "ve")

        def s4():
            p.wait(T, "T", "ve", p.cnt["ve"])
            transposes(sLat, 2, sLatT[:, :, :], "sLatT")
            p.setguard("sLat", "te")
            p.wait(T, "T", "ve", p.cnt["ve"])
            p.wguard(T, "T", "pX")
            for c in range(2):
                i = T.matmul(pX[:, 0:LAT], sLatT[:, c, :], wKup[:, c, :],
                             start=(c == 0), stop=(c == 1))
            kmm = p.inc("te", i)
            p.wait(Sc, "Sc", "te", kmm)
            p.wguard(Sc, "Sc", "sKtm")
            st_["kevc"] = p.inc("se", Sc.copy(out=sKtm[:], in_=pX[:, 0:LAT]))
            p.setguard("pX", "se", st_["kevc"])
            p.wguard(V, "V", "sQb")
            st_["qbc"] = qknorm(sQrope, sQb, NH, ssq16, rq16, sSq)

        def s5():
            p.wguard(T, "T", "pX")
            for c in range(2):
                i = T.matmul(pX[:, 0:LAT], sLatT[:, c, :], wVup[:, c, :],
                             start=(c == 0), stop=(c == 1))
            vmm = p.inc("te", i)
            p.setguard("sLatT", "te", vmm)
            p.wait(V, "V", "te", vmm)
            i = V.tensor_copy(
                out=sVg[:, t, 0:260].rearrange("p (h k) -> p h k", k=65)[:, :, 0:64],
                in_=pX[:, 0:LAT].rearrange("p (h k) -> p h k", k=64))
            p.setguard("pX", "ve", p.inc("ve", i))
            p.wait(V, "V", "se", st_["kevc"])
            p.wguard(V, "V", "sKbf")
            rope(sKtm, sQrope[:, 0:LAT], KVH, t, sCosG, sSinG, sSq)
            p.setguard("sKtm", "ve")
            st_["kbfc"] = qknorm(sQrope[:, 0:LAT], sKbf, KVH, ssk4, rk4, sSq)
            p.setguard("sQrope", "ve")
            p.setguard("sSq", "ve")

        def s6():
            p.wait(T, "T", "ve", st_["qbc"])
            p.wguard(T, "T", "pT")
            for c in range(8):
                i = T.transpose(pT[:, c * 128:(c + 1) * 128],
                                sQb[:, c * 128:(c + 1) * 128], sIdent[:])
            tc = p.inc("te", i)
            p.setguard("sQb", "te", tc)
            p.wait(V, "V", "te", tc)
            p.wguard(V, "V", f"sQTg{qp}")
            i = V.tensor_copy(out=sQTg[:, qp, :, ti * 128:(ti + 1) * 128],
                              in_=pT.rearrange("p (c n) -> p c n", n=128))
            p.setguard("pT", "ve", p.inc("ve", i))
            p.wait(T, "T", "ve", st_["kbfc"])
            p.wguard(T, "T", "pT")
            i = T.transpose(pT[:, 0:128], sKbf[:, 0:128], sIdent[:])
            i = T.transpose(pT[:, 128:256], sKbf[:, 128:256], sIdent[:])
            tc = p.inc("te", i)
            p.setguard("sKbf", "te", tc)
            p.wait(V, "V", "te", tc)
            V.tensor_copy(out=sKTg0[:, t * 128:(t + 1) * 128], in_=pT[:, 0:128])
            i = V.tensor_copy(out=sKTg1[:, t * 128:(t + 1) * 128], in_=pT[:, 128:256])
            p.setguard("pT", "ve", p.inc("ve", i))
            if ti == 3:
                qready_ve[b] = p.cnt["ve"]

        return [s0, s1, s2, s3, s4, s5, s6]

    load_x(0, 2)
    load_x(1, 3)
    for t in range(min(4, NT)):
        for f in part1_stages(t):
            f()

    pending_tail = [None]
    pending_vq = [None]
    for b in range(NB):
        qp = b % 2
        nj = 4 * b + 4
        par = 0
        p.wait(T, "T", "ve", qready_ve[b])
        if pending_tail[0] is not None:
            for f in pending_tail[0]:
                f()
            pending_tail[0] = None
        vq = pending_vq[0] or []
        pending_vq[0] = None
        stage_q = []
        if b + 1 < NB:
            load_x(4 * (b + 1), 2)
            load_x(4 * (b + 1) + 1, 3)
            for t_ in range(4 * (b + 1), 4 * (b + 1) + 4):
                stage_q.extend(part1_stages(t_))

        def drain_stage():
            if stage_q:
                stage_q.pop(0)()

        def drain_pair_end():
            if vq:
                vq.pop(0)()
            elif stage_q:
                stage_q.pop(0)()

        pBk = [pSd, pP]
        for hp, h0 in enumerate((0, 1, 2, 3, 8, 9, 10, 11)):
            h1 = h0 + 4
            kv0_, kv1_ = h0 // 4, h1 // 4
            kt = sKTg0 if h0 < 8 else sKTg1
            p.wguard(T, "T", "pV0")
            p.wguard(T, "T", "pV1")

            def emit_pv_g2(pend, final=False):
                jc_, ioff_, sp_, ec_ = pend
                p.wait(T, "T", "se", ec_)
                if final:
                    for _ in range(3):
                        T.ldweights(sIdent[:])
                for hx, pv, kv_ in ((0, pV0, kv0_), (1, pV1, kv1_)):
                    for sub in range(ioff_, 4):
                        i = T.matmul(pv[:, sub * 66:sub * 66 + 65],
                                     sPT[:, sp_, hx * 512 + (sub - ioff_) * 128:
                                         hx * 512 + (sub - ioff_ + 1) * 128],
                                     sVg[:, jc_, kv_ * 65:kv_ * 65 + 65],
                                     start=(jc_ == 0 and sub == 0),
                                     stop=(jc_ == 4 * b + sub))
                pvc = p.inc("te", i)
                p.setguard(f"sPT{sp_}", "te", pvc)

            pend_q = []
            q0, q1 = QSLOT[h0], QSLOT[h1]
            for jc in range(nj):
                ioff = max(0, jc - 4 * b)
                N = 512 - ioff * 128
                bk = par % 2
                sp = scnt[0] % 3
                scnt[0] += 1
                if bk == 0:
                    p.wguard(T, "T", "pS0")
                    p.wguard(T, "T", "pS1")
                else:
                    p.wguard(T, "T", "pP")
                bank = pBk[bk]
                masked = jc >= 4 * b
                T.matmul(bank[:, 0:N],
                         kt[0:64, jc * 128:(jc + 1) * 128],
                         sQTg[0:64, qp, q0 // 2, ioff * 128:512],
                         start=True, stop=not masked, tile_position=(0, 0))
                i = T.matmul(bank[:, 512:512 + N],
                             kt[64:128, jc * 128:(jc + 1) * 128],
                             sQTg[64:128, qp, q1 // 2, ioff * 128:512],
                             start=True, stop=not masked, tile_position=(64, 0))
                if masked:
                    T.matmul(bank[:, 0:128], sMaskD[:], sIdent[:],
                             start=False, stop=False)
                    i = T.matmul(bank[:, 512:640], sMaskD[:], sIdent[:],
                                 start=False, stop=True)
                smm = p.inc("te", i)
                p.wait(Sc, "Sc", "te", smm)
                p.wguard(Sc, "Sc", f"sPT{sp}")
                i = Sc.activation(out=sPT[:, sp, 0:512 + N], in_=bank[:, 0:512 + N],
                                  func=AF.Exp, scale=0.125)
                ec = p.inc("se", i)
                if bk == 0:
                    p.setguard("pS0", "se", ec)
                    p.setguard("pS1", "se", ec)
                else:
                    p.setguard("pP", "se", ec)
                pend_q.append((jc, ioff, sp, ec))
                if len(pend_q) > 2:
                    emit_pv_g2(pend_q.pop(0))
                par += 1
                if nj >= 8 and jc in (nj // 8, nj // 4, nj // 2, 3 * nj // 4):
                    drain_stage()
            while pend_q:
                emit_pv_g2(pend_q.pop(0), final=(len(pend_q) == 0))
            p.wguard(V, "V", "sAttn")
            p.wguard(V, "V", "sAttn_dma")
            attn_evac(pV0, 0, sAttn[:, :, h0 * 64:(h0 + 1) * 64])
            attn_evac(pV1, 1, sAttn[:, :, h1 * 64:(h1 + 1) * 64])
            if hp < 7:
                drain_pair_end()
        while vq:
            vq.pop(0)()
        while stage_q:
            stage_q.pop(0)()
        p.setguard(f"sQTg{qp}", "te")
        attn_ve = p.cnt["ve"]

        if b == 0:
            load_x(0, 0)
            load_x(1, 1)
        vs = {}

        def v1(t):
            ti_ = t % 4
            st_ = vs.setdefault(t, {})
            if _DBG_FULL:
                p.wait(Sy, "Sy", "ve")
                p.inc("st", Sy.dma_start(out=dbgag_d[t * 128:(t + 1) * 128, :],
                                         in_=sAttn[:, ti_, :]), 16)
                p.setguard("sAttn_dma", "st")
            p.wait(T, "T", "ve", attn_ve)
            p.wguard(T, "T", "pT")
            for c in range(8):
                i = T.transpose(pT[:, c * 128:(c + 1) * 128],
                                sAttn[:, ti_, c * 128:(c + 1) * 128], sIdent[:])
            tc = p.inc("te", i)
            p.setguard("sAttn", "te", tc)
            p.wait(Sc, "Sc", "te", tc)
            p.wguard(Sc, "Sc", "sATT")
            i = Sc.copy(out=sATT[:, :, :],
                        in_=pT.rearrange("p (c n) -> p c n", n=128))
            attc = p.inc("se", i)
            p.setguard("pT", "se", attc)
            p.wait(T, "T", "se", attc)
            p.wguard(T, "T", "pP")
            for nb_ in range(2):
                for c in range(8):
                    i = T.matmul(pP[:, nb_ * 512:(nb_ + 1) * 512], sATT[:, c, :],
                                 wWoG[:, c, nb_ * 512:(nb_ + 1) * 512],
                                 start=(c == 0), stop=(c == 7))
            woc = p.inc("te", i)
            p.setguard("sATT", "te", woc)
            p.wait(V, "V", "te", woc)
            sem, lv, slot = xload_val[t]
            st_["slot"] = slot
            p.wait(V, "V", sem, lv)
            p.wguard(V, "V", f"sX1_{t % 2}")
            x1a = sX1[:, t % 2, :]
            i = V.tensor_add(out=x1a, in0=pP[:, :], in1=sXblk[:, slot, :])
            x1c = p.inc("ve", i)
            p.setguard("pP", "ve", x1c)
            p.wait(Sc, "Sc", "ve", x1c)
            rmsnorm_stats(x1a, rstd2w, sSq)
            p.wguard(V, "V", "sXb")
            p.wait(V, "V", "ve", x1c)
            i = V.tensor_copy(out=sXb[:], in_=x1a)
            xbc = p.inc("ve", i)
            p.wait(T, "T", "ve", xbc)
            transposes(sXb, 8, sXT[:, :, :], "sXT")
            p.setguard("sXb", "te")
            p.wait(T, "T", "ve", p.cnt["ve"])
            p.wguard(T, "T", "pP")
            for nb_ in range(2):
                for c in range(8):
                    i = T.matmul(pP[:, nb_ * 512:(nb_ + 1) * 512], sXT[:, c, :],
                                 wWqL[:, c, nb_ * 512:(nb_ + 1) * 512],
                                 start=(c == 0), stop=(c == 7))
            st_["qmm"] = p.inc("te", i)
            p.wguard(T, "T", "pX")
            for c in range(8):
                i = T.matmul(pX[:, 0:LAT], sXT[:, c, :], wWkL[:, c, :],
                             start=(c == 0), stop=(c == 7))
            kmm = p.inc("te", i)
            p.wait(Sc, "Sc", "te", kmm)
            p.wguard(Sc, "Sc", "sKtm")
            i = Sc.mul(out=sKtm[:], in_=pX[:, 0:LAT], mul=rstd2)
            kevc = p.inc("se", i)
            st_["kevc"] = kevc
            p.setguard("pX", "se", kevc)
            p.wait(Sc, "Sc", "te", st_["qmm"])
            p.wguard(Sc, "Sc", "sQtm")
            i = Sc.mul(out=sQtm[:], in_=pP[:, :], mul=rstd2)
            qtmc = p.inc("se", i)
            st_["qtmc"] = qtmc
            p.setguard("pP", "se", qtmc)

        def v2a(t):
            st_ = vs[t]
            p.wguard(T, "T", "pX")
            for c in range(8):
                i = T.matmul(pX[:, 0:LAT], sXT[:, c, :], wWvL[:, c, :],
                             start=(c == 0), stop=(c == 7))
            vmm = p.inc("te", i)
            p.setguard("sXT", "te", vmm)
            p.wait(Sc, "Sc", "te", vmm)
            p.wguard(Sc, "Sc", f"sVl{t % 3}")
            i = Sc.mul(
                out=sVl[:, t % 3, 0:264].rearrange("p (h k) -> p h k", k=66)[:, :, 0:64],
                in_=pX[:, 0:LAT].rearrange("p (h k) -> p h k", k=64),
                mul=rstd2)
            vlc = p.inc("se", i)
            st_["vlc"] = vlc
            p.setguard("pX", "se", vlc)
            p.wait(V, "V", "se", st_["qtmc"])
            p.wguard(V, "V", "sSq")
            p.wguard(V, "V", "sQrope")
            rope(sQtm, sQrope, NH, t, sCosL, sSinL, sSq)
            p.setguard("sQtm", "ve")

        def v2bV(t):
            st_ = vs[t]
            p.wguard(V, "V", "sQb")
            st_["qbc"] = qknorm(sQrope, sQb, NH, ssq16, rq16, sSq)
            p.wguard(V, "V", "sKbf")
            p.wait(V, "V", "se", st_["kevc"])
            rope(sKtm, sQrope[:, 0:LAT], KVH, t, sCosL, sSinL, sSq)
            p.setguard("sKtm", "ve")
            st_["kbfc"] = qknorm(sQrope[:, 0:LAT], sKbf, KVH, ssk4, rk4, sSq)
            p.setguard("sQrope", "ve")
            p.setguard("sSq", "ve")

        def v2bT(t):
            st_ = vs[t]
            qbc = st_["qbc"]
            kbfc = st_["kbfc"]
            p.wait(T, "T", "ve", qbc)
            p.wguard(T, "T", "pT")
            for c in range(8):
                i = T.transpose(pT[:, c * 128:(c + 1) * 128],
                                sQb[:, c * 128:(c + 1) * 128], sIdent[:])
            tc = p.inc("te", i)
            p.setguard("sQb", "te", tc)
            p.wait(Sc, "Sc", "te", tc)
            p.wguard(Sc, "Sc", "sQTl")
            i = Sc.copy(out=sQTl[:, :, :],
                        in_=pT.rearrange("p (c n) -> p c n", n=128))
            st_["qtl"] = p.inc("se", i)
            p.setguard("pT", "se", st_["qtl"])
            p.wait(T, "T", "ve", kbfc)
            p.wguard(T, "T", "pT")
            i = T.transpose(pT[:, 0:128], sKbf[:, 0:128], sIdent[:])
            i = T.transpose(pT[:, 128:256], sKbf[:, 128:256], sIdent[:])
            tc = p.inc("te", i)
            p.setguard("sKbf", "te", tc)
            p.wait(Sc, "Sc", "te", tc)
            p.wguard(Sc, "Sc", f"sKTl{t % 3}")
            i = Sc.copy(out=sKTl[:, t % 3, :], in_=pT[:, 0:256])
            st_["klc"] = p.inc("se", i)
            p.setguard("pT", "se", st_["klc"])

        def v3(t, grange):
            st_ = vs[t]
            slots = [s_ for s_ in (t - 2, t - 1, t) if s_ >= 0]
            p.wait(T, "T", "se", max(st_["klc"], st_["qtl"], st_["vlc"]))
            for g in grange:
                kv = g
                pvp = g % 2
                pv = pVb[pvp]
                lpar = g % 2
                p.wguard(T, "T", f"pV{pvp}")

                def emit_pv_l(pend, final=False):
                    si_, s_, sp_, ec_ = pend
                    p.wait(T, "T", "se", ec_)
                    if final:
                        for _ in range(3):
                            T.ldweights(sIdent[:])
                    for hh in range(4):
                        i = T.matmul(pv[:, hh * 66:hh * 66 + 65],
                                     sPT[:, sp_, hh * 128:(hh + 1) * 128],
                                     sVl[:, s_ % 3, kv * 66:kv * 66 + 65],
                                     start=(si_ == 0 and hh == 0),
                                     stop=(si_ == len(slots) - 1))
                    pvc = p.inc("te", i)
                    p.setguard(f"sPT{sp_}", "te", pvc)

                pend_q = []
                for si, s_ in enumerate(slots):
                    sp = scnt[0] % 3
                    scnt[0] += 1
                    p.wguard(T, "T", f"pS{lpar}")
                    mask = sMaskD if s_ == t else (sMaskL if s_ == t - 2 else None)
                    for hh in range(4):
                        h_ = 4 * g + hh
                        qs = QSLOT[h_]
                        i = T.matmul(pSb[lpar][:, hh * 128:(hh + 1) * 128],
                                     sKTl[(kv % 2) * 64:(kv % 2) * 64 + 64, s_ % 3,
                                          (kv // 2) * 128:(kv // 2) * 128 + 128],
                                     sQTl[(qs % 2) * 64:(qs % 2) * 64 + 64, qs // 2, :],
                                     start=(hh == 0),
                                     stop=(mask is None and hh == 3))
                    if mask is not None:
                        i = T.matmul(pSb[lpar][:, 0:512], mask[:], sId4[:],
                                     start=False, stop=True)
                    smm = p.inc("te", i)
                    p.wait(Sc, "Sc", "te", smm)
                    p.wguard(Sc, "Sc", f"sPT{sp}")
                    i = Sc.activation(out=sPT[:, sp, 0:512], in_=pSb[lpar][:, 0:512],
                                      func=AF.Exp, scale=0.125)
                    ec = p.inc("se", i)
                    p.setguard(f"pS{lpar}", "se", ec)
                    pend_q.append((si, s_, sp, ec))
                    if len(pend_q) > 2:
                        emit_pv_l(pend_q.pop(0))
                    lpar ^= 1
                while pend_q:
                    emit_pv_l(pend_q.pop(0), final=(len(pend_q) == 0))
                p.wguard(V, "V", f"sAttnL{t % 2}")
                p.wguard(V, "V", "sAttnL_dma")
                attn_evac(pv, pvp,
                          sAttnL[:, t % 2, g * 256:(g + 1) * 256].rearrange(
                              "p (s k) -> p s k", k=64))
            if grange[-1] == 3:
                for s_ in slots:
                    p.setguard(f"sVl{s_ % 3}", "te")
                    p.setguard(f"sKTl{s_ % 3}", "te")
                p.setguard("sQTl", "te")
                st_["alve"] = p.cnt["ve"]

        def v4(t):
            st_ = vs.pop(t)
            slot = st_["slot"]
            x1a = sX1[:, t % 2, :]
            if _DBG_FULL:
                p.wait(Sy, "Sy", "ve")
                p.inc("st", Sy.dma_start(out=dbgal_d[t * 128:(t + 1) * 128, :],
                                         in_=sAttnL[:, t % 2, :]), 16)
                p.setguard("sAttnL_dma", "st")
            p.wait(T, "T", "ve", st_["alve"])
            p.wguard(T, "T", "pT")
            for c in range(8):
                i = T.transpose(pT[:, c * 128:(c + 1) * 128],
                                sAttnL[:, t % 2, c * 128:(c + 1) * 128], sIdent[:])
            tc = p.inc("te", i)
            p.setguard(f"sAttnL{t % 2}", "te", tc)
            p.wait(Sc, "Sc", "te", tc)
            p.wguard(Sc, "Sc", "sATT")
            i = Sc.copy(out=sATT[:, :, :],
                        in_=pT.rearrange("p (c n) -> p c n", n=128))
            attc = p.inc("se", i)
            p.setguard("pT", "se", attc)
            p.wait(T, "T", "se", attc)
            p.wguard(T, "T", "pP")
            for nb_ in range(2):
                for c in range(8):
                    i = T.matmul(pP[:, nb_ * 512:(nb_ + 1) * 512], sATT[:, c, :],
                                 wWoL[:, c, nb_ * 512:(nb_ + 1) * 512],
                                 start=(c == 0), stop=(c == 7))
            woc = p.inc("te", i)
            p.setguard("sATT", "te", woc)
            p.wait(V, "V", "te", woc)
            x2a = sXblk[:, slot, :]
            i = V.tensor_add(out=x1a, in0=pP[:, :], in1=x1a)
            x2c = p.inc("ve", i)
            p.setguard("pP", "ve", x2c)
            i = V.tensor_add(out=x2a, in0=x1a, in1=x2a)
            x2c = p.inc("ve", i)
            p.setguard(f"sX1_{t % 2}", "ve", x2c)
            ln_apply(x2a, x2a)
            y1c = p.cnt["ve"]
            p.wait(G, "G", "ve", y1c)
            p.inc("gst", G.dma_start(out=y1_d[t * 128:(t + 1) * 128, :],
                                     in_=x2a), 16)
            p.setguard(f"xblkA{slot}", "gst")
            if t + 2 < NT:
                load_x(t + 2, (t + 2) % 2)

        v1(4 * b)
        for ti in range(3):
            t = 4 * b + ti
            if ti >= 1:
                v4(t - 1)
            v2a(t)
            v2bV(t)
            v1(t + 1)
            v2bT(t)
            v3(t, (0, 1))
            v3(t, (2, 3))
        t3 = 4 * b + 3
        pending_tail[0] = [
            (lambda f=v2a, a=t3: f(a)),
            (lambda f=v2bV, a=t3: f(a)),
            (lambda f=v2bT, a=t3: f(a)),
        ]
        pending_vq[0] = [
            (lambda f=v4, a=t3 - 1: f(a)),
            (lambda f=v3, a=t3: f(a, (0, 1))),
            (lambda f=v3, a=t3: f(a, (2, 3))),
            (lambda f=v4, a=t3: f(a)),
        ]

    for f in pending_tail[0]:
        f()
    pending_tail[0] = None
    for f in pending_vq[0]:
        f()
    pending_vq[0] = None
    ab_te, ab_ve, ab_se = p.cnt["te"], p.cnt["ve"], p.cnt["se"]
    stAB.close()

    wW1 = sb("wW1", [128, 8, 4 * D], BF16)
    wW2 = sb("wW2", [128, 32, D], BF16)
    sB1r = sb("sB1r", [1, 4 * D], BF16)
    sB2r = sb("sB2r", [1, D], BF16)
    sY1c = sb("sY1c", [128, 2, D], F32)
    sY1b = sb("sY1b", [128, 2, D], BF16)
    sY1T = sb("sY1T", [128, 2, 8, 128], BF16)
    sHtm = sb("sHtm", [128, 4 * D], BF16)
    sHT = sb("sHT", [128, 32, 128], BF16)
    sX3 = sb("sX3", [128, D], F32)
    sOut = sb("sOut", [128, 2, D], F32)

    p.wait(G, "G", "te", ab_te)
    p.wait(G, "G", "ve", ab_ve)
    p.wait(G, "G", "se", ab_se)
    p.inc("ldw", G.dma_start(out=sB1r[:], in_=b1_d[:, :]), 16)
    p.inc("ldw", G.dma_start(out=sB2r[:], in_=b2_d[:, :]), 16)
    w12_ldw = p.cnt["ldw"]
    w1p, w2p = [], []
    for w in range(4):
        sem = "lw%d" % w
        for hh in range(2):
            c0 = w * 1024 + hh * 512
            p.inc(sem, G.dma_start(
                out=wW1[:, :, c0:c0 + 512],
                in_=w1_d[:, c0:c0 + 512].rearrange("(c p) n -> p c n", p=128)), 16)
        w1p.append((sem, p.cnt[sem]))
    for w in range(4):
        sem = "lw%d" % (4 + w)
        for hh in range(2):
            r0 = w * 8 + hh * 4
            p.inc(sem, G.dma_start(
                out=wW2[:, r0:r0 + 4, :],
                in_=w2_d[r0 * 128:(r0 + 4) * 128, :].rearrange("(c p) n -> p c n", p=128)), 16)
        w2p.append((sem, p.cnt[sem]))

    y1load = {}
    p.wait(G, "G", "gst", 16 * NT)

    def load_y1(t):
        p.wguard(G, "G", f"sY1c{t % 2}")
        sem = f"ly{t % 3}"
        y1load[t] = (sem, p.inc(sem, G.dma_start(out=sY1c[:, t % 2, :],
                                                 in_=y1_d[t * 128:(t + 1) * 128, :]), 16))

    load_y1(0)
    if NT > 1:
        load_y1(1)
    prep_ve = {}

    def prep_y1(t):
        ya_ = sY1c[:, t % 2, :]
        p.wait(V, "V", y1load[t][0], y1load[t][1])
        p.wguard(V, "V", f"sY1b{t % 2}")
        i = V.tensor_copy(out=sY1b[:, t % 2, :], in_=ya_)
        ybc = p.inc("ve", i)
        p.wait(T, "T", "ve", ybc)
        p.wait(T, "T", "ldw", w12_ldw)
        p.wguard(T, "T", "pT")
        for c in range(8):
            i = T.transpose(pT[:, c * 128:(c + 1) * 128],
                            sY1b[:, t % 2, c * 128:(c + 1) * 128], sIdent[:])
        tc = p.inc("te", i)
        p.setguard(f"sY1b{t % 2}", "te", tc)
        p.wait(V, "V", "te", tc)
        p.wguard(V, "V", f"sY1T{t % 2}")
        i = V.tensor_copy(out=sY1T[:, t % 2, :, :],
                          in_=pT.rearrange("p (c n) -> p c n", n=128))
        p.setguard("pT", "ve", p.inc("ve", i))
        prep_ve[t] = p.cnt["ve"]

    prep_y1(0)
    for t in range(NT):
        ya = sY1c[:, t % 2, :]
        yT = sY1T[:, t % 2, :, :]
        p.wait(T, "T", "ve", prep_ve[t])
        hpair = [(pP[:, 0:512], pP[:, 512:1024], "pP"),
                 (pV0[:, 0:512], pV1[:, 0:512], "pVh")]
        hc_w = {}

        def emit_h(w):
            pa, pb, gname = hpair[w % 2]
            p.wait(T, "T", w1p[w][0], w1p[w][1])
            p.wguard(T, "T", gname)
            for nb_, dst in ((0, pa), (1, pb)):
                hcol = w * 1024 + nb_ * 512
                for c in range(8):
                    T.matmul(dst, yT[:, c, :], wW1[:, c, hcol:hcol + 512],
                             start=(c == 0), stop=False)
                i = T.matmul(dst, sOnes[0:1, :], sB1r[0:1, hcol:hcol + 512],
                             start=False, stop=True)
            hmm = p.inc("te", i)
            p.wait(Sc, "Sc", "te", hmm)
            p.wguard(Sc, "Sc", "sHtm")
            Sc.activation(out=sHtm[:, w * 1024:w * 1024 + 512], in_=pa, func=AF.Relu)
            i = Sc.activation(out=sHtm[:, w * 1024 + 512:(w + 1) * 1024], in_=pb,
                              func=AF.Relu)
            hc = p.inc("se", i)
            p.setguard(gname, "se", hc)
            hc_w[w] = hc

        ht_c = {}

        def emit_tailA(w):
            p.wait(T, "T", "se", hc_w[w])
            p.wguard(T, "T", "pT")
            for c in range(8):
                i = T.transpose(pT[:, c * 128:(c + 1) * 128],
                                sHtm[:, w * 1024 + c * 128:w * 1024 + (c + 1) * 128],
                                sIdent[:])
            tc = p.inc("te", i)
            p.wait(V, "V", "te", tc)
            p.wguard(V, "V", "sHT")
            i = V.tensor_copy(out=sHT[:, w * 8:(w + 1) * 8, :],
                              in_=pT.rearrange("p (c n) -> p c n", n=128))
            htc = p.inc("ve", i)
            p.setguard("pT", "ve", htc)
            ht_c[w] = htc

        def emit_tailB(w):
            p.wait(T, "T", "ve", ht_c[w])
            p.wait(T, "T", w2p[w][0], w2p[w][1])
            if w == 0:
                p.wguard(T, "T", "pSf")
            for c in range(8):
                cc = w * 8 + c
                T.matmul(pS0[:, :], sHT[:, cc, :], wW2[:, cc, 0:512],
                         start=(cc == 0), stop=False)
                i = T.matmul(pS1[:, :], sHT[:, cc, :], wW2[:, cc, 512:1024],
                             start=(cc == 0), stop=False)
            p.inc("te", i)

        for w in range(4):
            emit_h(w)
            if w == 1 and t + 1 < NT:
                prep_y1(t + 1)
            if w >= 1:
                emit_tailA(w - 1)
            if w >= 2:
                emit_tailB(w - 2)
        emit_tailA(3)
        emit_tailB(2)
        emit_tailB(3)
        p.setguard("sHtm", "te")
        p.setguard("sHT", "te")
        p.setguard(f"sY1T{t % 2}", "te")
        T.matmul(pS0[:, :], sOnes[0:1, :], sB2r[0:1, 0:512], start=False, stop=True)
        i = T.matmul(pS1[:, :], sOnes[0:1, :], sB2r[0:1, 512:1024], start=False, stop=True)
        fmm = p.inc("te", i)
        p.wait(V, "V", "te", fmm)
        p.wguard(V, "V", "sX3")
        p.wguard(V, "V", "sX3_dma")
        V.tensor_add(out=sX3[:, 0:512], in0=pS0[:, :], in1=ya[:, 0:512])
        i = V.tensor_add(out=sX3[:, 512:1024], in0=pS1[:, :], in1=ya[:, 512:1024])
        x3c = p.inc("ve", i)
        p.setguard("pSf", "ve", x3c)
        p.setguard(f"sY1c{t % 2}", "ve", x3c)
        if t + 2 < NT:
            load_y1(t + 2)
        if _DBG_FULL:
            p.wait(Sy, "Sy", "ve")
            p.inc("st", Sy.dma_start(out=dbgx3_d[t * 128:(t + 1) * 128, :],
                                     in_=sX3[:]), 16)
            p.setguard("sX3_dma", "st")
        p.wguard(V, "V", f"sOut{t % 2}")
        ln_apply(sX3, sOut[:, t % 2, :])
        oc = p.cnt["ve"]
        p.setguard("sX3", "ve", oc)
        p.wait(Sy, "Sy", "ve", oc)
        p.inc(f"st{t % 2}", Sy.dma_start(out=out_d[t * 128:(t + 1) * 128, :],
                                         in_=sOut[:, t % 2, :]), 16)
        p.setguard(f"sOut{t % 2}", f"st{t % 2}")

    if _DBG_FULL:
        p.wait(Sy, "Sy", "gst")
        p.inc("st", Sy.dma_start(out=dbgy1_d[:, :], in_=y1_d[:, :]), 16)
        Sy.wait_ge(p.sem["st"], p.cnt["st"])
    Sy.wait_ge(p.sem["st0"], p.cnt["st0"])
    Sy.wait_ge(p.sem["st1"], p.cnt["st1"])
    st.close()
    return nc


_EO = np.concatenate([np.arange(0, 64, 2), np.arange(1, 64, 2)])


def _perm_wq(w):
    w = np.asarray(w)
    cols = np.concatenate([h * 64 + _EO for h in QORDER])
    return np.ascontiguousarray(w[:, cols])


def _perm_eo(w, nheads):
    w = np.asarray(w)
    cols = np.concatenate([h * 64 + _EO for h in range(nheads)])
    return np.ascontiguousarray(w[:, cols])


def _prep_inputs(S, inputs):
    c = _host_consts(S)
    common = {
        "g_wq": _perm_wq(inputs["L0_attn_g_wq"]), "g_kv_down": inputs["L0_attn_g_kv_down"],
        "g_k_up": _perm_eo(inputs["L0_attn_g_k_up"], KVH),
        "g_v_up": inputs["L0_attn_g_v_up"],
        "g_wo": inputs["L0_attn_g_wo"], "l_wq": _perm_wq(inputs["L0_attn_l_wq"]),
        "l_wk": _perm_eo(inputs["L0_attn_l_wk"], KVH), "l_wv": inputs["L0_attn_l_wv"],
        "l_wo": inputs["L0_attn_l_wo"], "w1": inputs["L0_ffn_w1"],
        "w2": inputs["L0_ffn_w2"],
        "b1r": np.asarray(inputs["L0_ffn_b1"]).reshape(1, -1),
        "b2r": np.asarray(inputs["L0_ffn_b2"]).reshape(1, -1),
        "n1pc": np.asarray(inputs["L0_attn_norm1_w"]).reshape(8, 128).T,
        "n2pc": np.asarray(inputs["L0_attn_norm2_w"]).reshape(8, 128).T,
    }
    common.update(c)
    return {k: np.ascontiguousarray(v) for k, v in common.items()}


def kernel(**inputs):
    x = np.asarray(inputs["L0_attn_hidden"])
    B, S, _ = x.shape
    common = _prep_inputs(S, inputs)
    nc = build_program(S)
    in_maps = []
    for b in range(B):
        m = dict(common)
        m["x"] = np.ascontiguousarray(x[b])
        in_maps.append(m)
    res = run_bass_kernel_spmd(nc, in_maps, core_ids=list(range(B)))
    out = np.stack([np.asarray(res.results[b]["out"]) for b in range(B)], axis=0)
    return out.astype(np.float32)
